# version 08
# speedup vs baseline: 2.0339x; 1.0214x over previous
import math
from contextlib import ExitStack

import numpy as np
import ml_dtypes
import concourse.bass as bass
import concourse.mybir as mybir
from concourse.bass_utils import run_bass_kernel_spmd

F32 = mybir.dt.float32
BF16 = mybir.dt.bfloat16
I32 = mybir.dt.int32
AF = mybir.ActivationFunctionType
ALU = mybir.AluOpType
AX = mybir.AxisListType
BF16NP = ml_dtypes.bfloat16

S, D, H = 8192, 2048, 16
QLR, KVLR = 1536, 512
NOPE, ROPE, VD = 128, 64, 128
QKD = NOPE + ROPE
E, I_FF = 8, 1408
NCORE = 8
TPC = S // NCORE
NH = TPC // 512
SCALE = 1.0 / math.sqrt(QKD)

CP = {"ln_eps": 0, "rms_eps": 1, "neg_half": 2, "inv_d": 3, "inv_qlr": 4,
      "inv_kvlr": 5, "half": 6, "one": 7, "neg_big": 8, "zero": 9,
      "neg_one": 10, "scale": 11}


def _cpool_np():
    v = np.zeros((128, 16), np.float32)
    for k, val in [("ln_eps", 1e-5), ("rms_eps", 1e-6), ("neg_half", -0.5),
                   ("inv_d", 1.0 / D), ("inv_qlr", 1.0 / QLR),
                   ("inv_kvlr", 1.0 / KVLR), ("half", 0.5), ("one", 1.0),
                   ("neg_big", -1e30), ("zero", 0.0), ("neg_one", -1.0),
                   ("scale", SCALE)]:
        v[:, CP[k]] = val
    return v


class P:

    def __init__(self, nc, stack):
        self.nc = nc
        self.stack = stack
        self.en = stack.enter_context
        self.sem = {}
        self.cnt = {}
        self.bank = {}

    def sbuf(self, name, shape, dt):
        return self.en(self.nc.sbuf_tensor(name, shape, dt))

    def new_sem(self, *names):
        for n in names:
            if n not in self.sem:
                self.sem[n] = self.en(self.nc.semaphore(n))
                self.cnt[n] = 0

    def inc(self, instr, name, n=1):
        self.cnt[name] += n
        instr.then_inc(self.sem[name], n)
        return instr

    def wait(self, eng, name):
        if self.cnt[name] > 0:
            eng.wait_ge(self.sem[name], self.cnt[name])

    def wait_at(self, eng, name, v):
        if v > 0:
            eng.wait_ge(self.sem[name], v)

    def bank_wait(self, eng, keys):
        for k in keys:
            if k in self.bank:
                s, v = self.bank[k]
                self.wait_at(eng, s, v)

    def bank_set(self, keys, sem_name):
        for k in keys:
            self.bank[k] = (sem_name, self.cnt[sem_name])

    def edge(self, eng, instr, name):
        self.inc(instr, name)
        eng.wait_ge(self.sem[name], self.cnt[name])

    def cp(self, name):
        return self.cpool[:, CP[name]:CP[name] + 1]


def emit_layernorm(p, blk, tag, in_sb, out_sb, T, ready_sems, sem_tag=None,
                   pss_ext=None):
    nc = p.nc
    nh = T // 512
    st = sem_tag or tag
    with ExitStack() as ph:
        en = ph.enter_context
        sq = en(nc.sbuf_tensor(f"{tag}sq", [128, 16 * T], BF16))
        m_sb = en(nc.sbuf_tensor(f"{tag}m", [128, T], F32))
        v_sb = en(nc.sbuf_tensor(f"{tag}v", [128, T], F32))
        rstd = en(nc.sbuf_tensor(f"{tag}r", [128, T], F32))
        tsc = en(nc.sbuf_tensor(f"{tag}t", [128, T], F32))
        pss = pss_ext if pss_ext is not None else [
            en(nc.psum_tensor(f"{tag}ps{i}", [128, 512], F32))
            for i in range(2 * nh)]
        p.new_sem(f"{st}mm", f"{st}vv", f"{st}ss")
        mm, vv, ss = f"{st}mm", f"{st}vv", f"{st}ss"
        ent = {n: p.cnt[n] for n in (mm, vv, ss)}

        @blk.vector
        def _(v):
            for r in ready_sems:
                p.wait(v, r)
            for n, v_ in ent.items():
                p.wait_at(v, n, v_)
            p.inc(v.tensor_mul(sq[:, :], in_sb[:, :], in_sb[:, :]), ss)

        @blk.tensor
        def _(t):
            for r in ready_sems:
                p.wait(t, r)
            for n, v_ in ent.items():
                p.wait_at(t, n, v_)
            p.wait(t, ss)
            for k in range(16):
                for h in range(nh):
                    t.matmul(pss[h][:, :], p.ones_f32[:, :],
                             in_sb[:, k * T + h * 512: k * T + h * 512 + 512],
                             start=(k == 0), stop=(k == 15))
                    ins = t.matmul(pss[nh + h][:, :], p.ones_bf[:, :],
                                   sq[:, k * T + h * 512: k * T + h * 512 + 512],
                                   start=(k == 0), stop=(k == 15))
            p.inc(ins, mm)

        @blk.vector
        def _(v):
            p.wait(v, mm)
            for h in range(nh):
                ins = v.tensor_scalar(m_sb[:, h * 512:h * 512 + 512],
                                      pss[h][:, :], p.cp("inv_d"), None,
                                      ALU.mult)
                if h < nh - 1:
                    p.edge(v, ins, ss)
            p.inc(ins, ss)

        @blk.vector
        def _(v):
            p.wait(v, ss)
            for n, v_ in ent.items():
                p.wait_at(v, n, v_)
            p.edge(v, v.tensor_mul(tsc[:, :], m_sb[:, :], m_sb[:, :]), vv)
            for h in range(nh):
                ins = v.scalar_tensor_tensor(
                    v_sb[:, h * 512:h * 512 + 512], pss[nh + h][:, :],
                    p.cp("inv_d"), tsc[:, h * 512:h * 512 + 512],
                    ALU.mult, ALU.subtract)
            p.inc(ins, vv)

        @blk.scalar
        def _(s):
            p.wait(s, vv)
            p.edge(s, s.activation(rstd[:, :], v_sb[:, :], AF.Ln,
                                   bias=p.cp("ln_eps")), ss)
            p.inc(s.activation(rstd[:, :], rstd[:, :], AF.Exp,
                               scale=p.cp("neg_half")), ss)

        @blk.vector
        def _(v):
            p.wait(v, ss)
            for k in range(16):
                p.edge(v, v.tensor_sub(tsc[:, :], in_sb[:, k * T:(k + 1) * T],
                                       m_sb[:, :]), vv)
                ins = v.tensor_mul(out_sb[:, k * T:(k + 1) * T], tsc[:, :],
                                   rstd[:, :])
                if k == 15:
                    p.inc(ins, vv)
                else:
                    p.edge(v, ins, vv)
        done = p.cnt[vv]
    return vv, done


def emit_ffn(p, blk, tag, xh_sb, resid_sb, w1p_ap, w2t_ap, gate_sb, T,
             ready_sems, dbg=None):
    nc = p.nc
    nh = T // 512
    ntc = T // 128
    with ExitStack() as ph:
        en = ph.enter_context
        act = en(nc.sbuf_tensor(f"{tag}act", [128, 11 * T], BF16))
        sg2 = [en(nc.sbuf_tensor(f"{tag}sg{i}", [128, T], F32))
               for i in range(2)]
        ucs = en(nc.sbuf_tensor(f"{tag}uc", [128, T], F32))
        lg = en(nc.sbuf_tensor(f"{tag}lg", [8, T], F32))
        lt = en(nc.sbuf_tensor(f"{tag}lt", [128, 8], F32))
        ltd = en(nc.sbuf_tensor(f"{tag}ltd", [128, 8], F32))
        lsel = en(nc.sbuf_tensor(f"{tag}lsel", [128, 8], F32))
        lmsk = en(nc.sbuf_tensor(f"{tag}lmsk", [128, 8], F32))
        nm1 = en(nc.sbuf_tensor(f"{tag}nm1", [128, 1], F32))
        m2s = en(nc.sbuf_tensor(f"{tag}m2", [128, 1], F32))
        lz = en(nc.sbuf_tensor(f"{tag}lz", [128, 1], F32))
        cmbT = en(nc.sbuf_tensor(f"{tag}cmbT", [8, 128], F32))
        cT = en(nc.sbuf_tensor(f"{tag}cT", [1, E * T], BF16))
        c_bc = en(nc.sbuf_tensor(f"{tag}cbc", [128, T], F32))
        w1b = [en(nc.sbuf_tensor(f"{tag}w1{i}", [128, 16 * 256], BF16))
               for i in range(2)]
        w2b = [en(nc.sbuf_tensor(f"{tag}w2{i}", [128, 11 * 256], BF16))
               for i in range(2)]
        ps = [en(nc.psum_tensor(f"{tag}ps{i}", [128, 512], F32))
              for i in range(8)]
        cT_d = nc.dram_tensor(f"{tag}cTd", [E, T], F32, kind="Internal")

        p.new_sem(f"{tag}mm", f"{tag}vv", f"{tag}ss", f"{tag}w1", f"{tag}w2",
                  f"{tag}af", f"{tag}cd", f"{tag}ct")
        mm, vv, ss = f"{tag}mm", f"{tag}vv", f"{tag}ss"
        w1s, w2s, afs, cds = f"{tag}w1", f"{tag}w2", f"{tag}af", f"{tag}cd"
        cts = f"{tag}ct"

        @blk.tensor
        def _(t):
            for r in ready_sems:
                p.wait(t, r)
            p.bank_wait(t, ["ps6", "ps7"])
            for h in range(nh):
                for k in range(16):
                    ins = t.matmul(
                        ps[6 + h][0:8, :], gate_sb[:, k * 8:k * 8 + 8],
                        xh_sb[:, k * T + h * 512: k * T + h * 512 + 512],
                        start=(k == 0), stop=(k == 15))
            p.inc(ins, mm)

        @blk.vector
        def _(v):
            p.wait(v, mm)
            for h in range(nh):
                ins = v.tensor_copy(lg[0:8, h * 512:h * 512 + 512],
                                    ps[6 + h][0:8, :])
                if h < nh - 1:
                    p.edge(v, ins, ss)
            p.inc(ins, ss)
            p.bank_set(["ps6", "ps7"], ss)

        if dbg is not None:
            @blk.sync
            def _(sy):
                p.wait(sy, ss)
                p.inc(sy.dma_start(out=dbg["xh"], in_=xh_sb[:, :]), "std", 16)
                p.inc(sy.dma_start(out=dbg["lg"], in_=lg[0:8, :]), "std", 16)

        for tc in range(ntc):
            @blk.tensor
            def _(t):
                p.wait(t, ss)
                p.bank_wait(t, ["ps6"])
                p.inc(t.transpose(ps[6][:, 0:8],
                                  lg[0:8, tc * 128:(tc + 1) * 128],
                                  p.ident_f32[0:8, 0:8]), mm)

            @blk.vector
            def _(v):
                p.wait(v, mm)
                p.wait(v, ss)
                p.edge(v, v.tensor_copy(lt[:, :], ps[6][:, 0:8]), vv)
                p.edge(v, v.reduce_max(nm1[:, :], lt[:, :], axis=AX.X,
                                       negate=True), vv)
                p.bank_set(["ps6"], vv)
                p.edge(v, v.tensor_scalar(ltd[:, :], lt[:, :], nm1[:, :],
                                          None, ALU.add), vv)
                p.edge(v, v.tensor_scalar(lmsk[:, :], ltd[:, :], p.cp("zero"),
                                          None, ALU.is_equal), vv)
                p.edge(v, v.scalar_tensor_tensor(lmsk[:, :], lmsk[:, :],
                                                 p.cp("neg_big"), lt[:, :],
                                                 ALU.mult, ALU.add), vv)
                p.edge(v, v.reduce_max(m2s[:, :], lmsk[:, :], axis=AX.X), vv)
                p.inc(v.tensor_scalar(lsel[:, :], lt[:, :], m2s[:, :], None,
                                      ALU.is_ge), vv)

            @blk.scalar
            def _(s):
                p.wait(s, vv)
                p.inc(s.activation(ltd[:, :], lt[:, :], AF.Exp,
                                   bias=nm1[:, :]), ss)

            @blk.vector
            def _(v):
                p.wait(v, ss)
                p.edge(v, v.tensor_mul(ltd[:, :], ltd[:, :], lsel[:, :]), vv)
                p.edge(v, v.reduce_sum(lz[:, :], ltd[:, :], axis=AX.X), vv)
                p.edge(v, v.reciprocal(lz[:, :], lz[:, :]), vv)
                p.inc(v.tensor_scalar(ltd[:, :], ltd[:, :], lz[:, :],
                                      p.cp("half"), ALU.mult, ALU.mult), vv)

            @blk.tensor
            def _(t):
                p.wait(t, vv)
                p.bank_wait(t, ["ps6"])
                p.inc(t.transpose(ps[6][0:8, 0:128], ltd[:, :],
                                  p.ident_f32[:, :]), mm)

            @blk.vector
            def _(v):
                p.wait(v, mm)
                p.bank_wait(v, ["cmbT"])
                p.inc(v.tensor_copy(cmbT[:, :], ps[6][0:8, 0:128]), ss)
                p.bank_set(["ps6"], ss)

            @blk.sync
            def _(sy):
                p.wait(sy, ss)
                p.inc(sy.dma_start(out=cT_d.ap()[:, tc * 128:(tc + 1) * 128],
                                   in_=cmbT[:, :]), cds, 16)
                p.bank_set(["cmbT"], cds)

        @blk.gpsimd
        def _(g):
            p.wait(g, cds)
            for e in range(E):
                ins = g.dma_start(out=cT[0:1, e * T:(e + 1) * T],
                                  in_=cT_d.ap()[e:e + 1, :])
                p.inc(ins, cts, 16)

        if dbg is not None:
            dmisc = en(nc.sbuf_tensor(f"{tag}dmisc", [128, 256], F32))

            @blk.vector
            def _(v):
                p.wait(v, cts)
                v.memset(dmisc[:, :], 0.0)
                v.tensor_copy(dmisc[0:8, 0:128], cmbT[:, :])
                v.tensor_copy(dmisc[:, 128:136], ltd[:, :])
                v.tensor_copy(dmisc[:, 136:137], nm1[:, :])
                v.tensor_copy(dmisc[:, 137:138], lz[:, :])
                v.tensor_copy(dmisc[:, 138:146], lt[:, :])
                v.tensor_copy(dmisc[:, 146:154], lsel[:, :])
                p.inc(v.tensor_copy(dmisc[0:1, 160:160 + 64],
                                    cT[0:1, 7 * T:7 * T + 64]), vv)

            @blk.sync
            def _(sy):
                p.wait(sy, vv)
                p.inc(sy.dma_start(out=dbg["misc"], in_=dmisc[:, :]),
                      "std", 16)

        NP_SL = [1] * 11
        w1_free = []
        w2_free = []

        for e in range(E):
            @blk.tensor
            def _(t):
                p.wait(t, cts)
                p.bank_wait(t, ["ps6", "ps7"])
                for h in range(nh):
                    p.inc(t.matmul(
                        ps[6 + h][:, :], p.ones_col_bf[0:1, :],
                        cT[0:1, e * T + h * 512:e * T + h * 512 + 512],
                        start=True, stop=True), mm)

            @blk.vector
            def _(v):
                p.wait(v, mm)
                p.bank_wait(v, ["cbc"])
                for h in range(nh):
                    ins = v.tensor_copy(c_bc[:, h * 512:h * 512 + 512],
                                        ps[6 + h][:, :])
                    if h < nh - 1:
                        p.edge(v, ins, ss)
                p.inc(ins, ss)
                p.bank_set(["ps6", "ps7"], ss)

            cbc_ready = p.cnt[ss]
            silu_jobs = []
            ep_afs = {}
            ss_sil0 = p.cnt[ss]

            pair0 = 0
            for sli, npair in enumerate(NP_SL):
                gslice = e * len(NP_SL) + sli
                buf = gslice % 2

                @blk.gpsimd
                def _(g):
                    if gslice >= 2:
                        sname, v = w1_free[gslice - 2]
                        p.wait_at(g, sname, v)
                    else:
                        for r in ready_sems:
                            p.wait(g, r)
                    cols = npair * 256
                    p.inc(g.dma_start(
                        out=w1b[buf].ap().rearrange(
                            "pp (k m) -> pp k m", k=16)[:, :, 0:cols],
                        in_=w1p_ap[e, :, :,
                                   pair0 * 256:pair0 * 256 + cols].rearrange(
                            "k pp m -> pp k m")), w1s, 16)
                w1_need = p.cnt[w1s]

                for pl in range(npair):
                    gp = pair0 + pl
                    grp = gp % 2
                    bks = [f"ps{4 * grp + j}" for j in range(4)]
                    bg = [ps[4 * grp + j] for j in range(4)]

                    @blk.tensor
                    def _(t):
                        p.wait_at(t, w1s, w1_need)
                        p.wait_at(t, ss, cbc_ready)
                        p.bank_wait(t, bks)
                        W = 256
                        for k in range(16):
                            for hh in range(nh):
                                t.matmul(bg[hh][:, :],
                                         w1b[buf][:, k * W + pl * 256:
                                                  k * W + pl * 256 + 128],
                                         xh_sb[:, k * T + hh * 512:
                                               k * T + hh * 512 + 512],
                                         start=(k == 0), stop=(k == 15))
                                ins = t.matmul(
                                    bg[2 + hh][:, :],
                                    w1b[buf][:, k * W + pl * 256 + 128:
                                             k * W + pl * 256 + 256],
                                    xh_sb[:, k * T + hh * 512:
                                          k * T + hh * 512 + 512],
                                    start=(k == 0), stop=(k == 15))
                        p.inc(ins, mm)
                    mm_grp = p.cnt[mm]

                    ss_grp = ss_sil0 + gp + 1
                    silu_jobs.append((mm_grp, gp, bg, bks))

                    @blk.vector
                    def _(v, mm_grp=mm_grp, ss_grp=ss_grp, gp=gp, bg=bg):
                        p.wait_at(v, mm, mm_grp)
                        p.wait_at(v, ss, ss_grp)
                        sgb = sg2[gp % 2]
                        for hh in range(nh):
                            p.edge(v, v.tensor_mul(
                                ucs[:, hh * 512:hh * 512 + 512],
                                bg[2 + hh][:, :],
                                c_bc[:, hh * 512:hh * 512 + 512]), vv)
                            ins = v.tensor_mul(
                                act[:, gp * T + hh * 512:
                                    gp * T + hh * 512 + 512],
                                sgb[:, hh * 512:hh * 512 + 512],
                                ucs[:, hh * 512:hh * 512 + 512])
                            if hh < nh - 1:
                                p.edge(v, ins, vv)
                        p.inc(ins, afs)
                    p.bank_set(bks, afs)
                    ep_afs[gp] = p.cnt[afs]
                pair0 += npair
                w1_free.append((afs, p.cnt[afs]))

            @blk.scalar
            def _(s, silu_jobs=list(silu_jobs), e=e):
                for mm_grp, gp, bg, bks in silu_jobs:
                    p.wait_at(s, mm, mm_grp)
                    if gp - 2 in ep_afs:
                        p.wait_at(s, afs, ep_afs[gp - 2])
                    sgb = sg2[gp % 2]
                    for hh in range(nh):
                        ins = s.activation(sgb[:, hh * 512:hh * 512 + 512],
                                           bg[hh][:, :], AF.Silu)
                    p.inc(ins, ss)
            assert p.cnt[ss] == ss_sil0 + 11

            act_all = p.cnt[afs]

            for q in range(8):
                gq = e * 8 + q
                buf2 = gq % 2

                @blk.gpsimd
                def _(g):
                    if gq >= 2:
                        sname, v = w2_free[gq - 2]
                        p.wait_at(g, sname, v)
                    else:
                        for r in ready_sems:
                            p.wait(g, r)
                    p.inc(g.dma_start(
                        out=w2b[buf2].ap().rearrange(
                            "pp (k m) -> pp k m", k=11),
                        in_=w2t_ap[e, :, :, q * 256:q * 256 + 256].rearrange(
                            "k pp m -> pp k m")), w2s, 16)
                w2_need = p.cnt[w2s]

                for m2l in range(2):
                    m2 = q * 2 + m2l
                    slot = m2l % 2
                    bks = [f"ps{slot * 2}", f"ps{slot * 2 + 1}"]
                    bk = [ps[slot * 2], ps[slot * 2 + 1]]

                    @blk.tensor
                    def _(t):
                        p.wait_at(t, w2s, w2_need)
                        p.wait_at(t, afs, act_all)
                        p.bank_wait(t, bks)
                        for k2 in range(11):
                            for hh in range(nh):
                                ins = t.matmul(
                                    bk[hh][:, :],
                                    w2b[buf2][:, k2 * 256 + m2l * 128:
                                              k2 * 256 + m2l * 128 + 128],
                                    act[:, k2 * T + hh * 512:
                                        k2 * T + hh * 512 + 512],
                                    start=(k2 == 0), stop=(k2 == 10))
                        p.inc(ins, mm)
                    mm_w2 = p.cnt[mm]

                    @blk.vector
                    def _(v):
                        p.wait_at(v, mm, mm_w2)
                        p.bank_wait(v, [f"res{m2}"])
                        for hh in range(nh):
                            ins = v.tensor_add(
                                resid_sb[:, m2 * T + hh * 512:
                                         m2 * T + hh * 512 + 512],
                                bk[hh][:, :],
                                resid_sb[:, m2 * T + hh * 512:
                                         m2 * T + hh * 512 + 512])
                            if hh < nh - 1:
                                p.edge(v, ins, vv)
                        p.inc(ins, vv)
                    p.bank_set(bks, vv)
                    p.bank_set([f"res{m2}"], vv)
                w2_free.append((mm, p.cnt[mm]))

            p.bank_set(["cbc"], afs)
        if dbg is not None:
            @blk.sync
            def _(sy):
                p.wait(sy, afs)
                p.inc(sy.dma_start(out=dbg["cbc"], in_=c_bc[:, :]), "std", 16)
                p.inc(sy.dma_start(out=dbg["sg"], in_=sg[:, :]), "std", 16)
                p.inc(sy.dma_start(out=dbg["ucs"], in_=ucs[:, :]), "std", 16)

            @blk.vector
            def _(v):
                p.wait(v, afs)
                p.inc(v.tensor_copy(sg[:, :], act[:, 10 * T:11 * T]), vv)

            @blk.sync
            def _(sy):
                p.wait(sy, vv)
                p.wait(sy, "std")
                p.inc(sy.dma_start(out=dbg["act"], in_=sg[:, :]),
                      "std", 16)
        done = p.cnt[vv]
    return vv, done


def prep_ffn_weights(ins):
    f32 = np.float32
    w1 = np.asarray(ins["L0_ffn_w1"], f32)
    w1t = w1.transpose(0, 2, 1)
    idx = np.empty(2816, np.int64)
    for i in range(11):
        idx[i * 256:i * 256 + 128] = np.arange(i * 128, (i + 1) * 128)
        idx[i * 256 + 128:(i + 1) * 256] = np.arange(1408 + i * 128,
                                                     1408 + (i + 1) * 128)
    w1p = np.ascontiguousarray(w1t[:, :, idx].reshape(E, 16, 128, 2816))
    w2 = np.asarray(ins["L0_ffn_w2"], f32)
    w2t = np.ascontiguousarray(
        w2.transpose(0, 2, 1).reshape(E, 11, 128, 2048))
    gw = np.asarray(ins["L0_ffn_gate_w"], f32)
    gateT = np.ascontiguousarray(gw.T.reshape(16, 128, 8))
    return w1p, w2t, gateT


def _bias_tiles(b, nchunk):
    b = np.asarray(b, np.float32)
    padded = np.zeros(nchunk * 128, np.float32)
    padded[:b.shape[0]] = b
    return np.ascontiguousarray(padded.reshape(nchunk, 128).T)


def build_A(debug=False):
    nc = bass.Bass("TRN2", target_bir_lowering=False)
    xT = nc.dram_tensor("xT", [16, 128, TPC], F32, kind="ExternalInput").ap()
    cpool_h = nc.dram_tensor("cpool", [128, 16], F32,
                             kind="ExternalInput").ap()
    ident_h = nc.dram_tensor("ident", [128, 128], F32,
                             kind="ExternalInput").ap()
    w1p = nc.dram_tensor("w1p", [E, 16, 128, 2816], F32,
                         kind="ExternalInput").ap()
    w2t = nc.dram_tensor("w2t", [E, 11, 128, 2048], F32,
                         kind="ExternalInput").ap()
    gateT = nc.dram_tensor("gateT", [16, 128, 8], F32,
                           kind="ExternalInput").ap()
    uT = nc.dram_tensor("uT", [16, 128, TPC], F32, kind="ExternalOutput").ap()
    if debug:
        dbg_xh = nc.dram_tensor("dbg_xh", [128, 16 * TPC], BF16,
                            kind="ExternalOutput").ap()
        dbg_lg = nc.dram_tensor("dbg_lg", [8, TPC], F32,
                                kind="ExternalOutput").ap()
        dbg_cT = nc.dram_tensor("dbg_cT", [1, E * TPC], F32,
                                kind="ExternalOutput").ap()
        dbg_act = nc.dram_tensor("dbg_act", [128, TPC], F32,
                                 kind="ExternalOutput").ap()
        dbg_cbc = nc.dram_tensor("dbg_cbc", [128, TPC], F32,
                                 kind="ExternalOutput").ap()
        dbg_sg = nc.dram_tensor("dbg_sg", [128, TPC], F32,
                                kind="ExternalOutput").ap()
        dbg_ucs = nc.dram_tensor("dbg_ucs", [128, TPC], F32,
                                 kind="ExternalOutput").ap()
        dbg_misc = nc.dram_tensor("dbg_misc", [128, 256], F32,
                                  kind="ExternalOutput").ap()

    with ExitStack() as stack:
        p = P(nc, stack)
        p.cpool = p.sbuf("cpool_sb", [128, 16], F32)
        p.ident_f32 = p.sbuf("ident_sb", [128, 128], F32)
        p.ones_f32 = p.sbuf("ones_f32", [128, 128], F32)
        p.ones_bf = p.sbuf("ones_bf", [128, 128], BF16)
        p.ones_col = p.sbuf("ones_col", [1, 128], F32)
        p.ones_col_bf = p.sbuf("ones_col_bf", [1, 128], BF16)
        x_sb = p.sbuf("x_sb", [128, 16 * TPC], F32)
        xh_sb = p.sbuf("xh_sb", [128, 16 * TPC], BF16)
        gate_sb = p.sbuf("gate_sb", [128, 16 * 8], BF16)
        p.new_sem("ldx", "ldw", "std", "crdy")
        blk = p.en(nc.Block())

        @blk.sync
        def _(sy):
            p.inc(sy.dma_start(out=p.cpool[:, :], in_=cpool_h), "ldx", 16)
            p.inc(sy.dma_start(out=p.ident_f32[:, :], in_=ident_h), "ldx", 16)
            p.inc(sy.dma_start(
                out=x_sb.ap().rearrange("pp (k m) -> pp k m", k=16),
                in_=xT.rearrange("k pp m -> pp k m")), "ldx", 16)

        @blk.gpsimd
        def _(g):
            p.inc(g.dma_start(
                out=gate_sb.ap().rearrange("pp (k m) -> pp k m", k=16),
                in_=gateT.rearrange("k pp m -> pp k m")), "ldw", 16)

        @blk.vector
        def _(v):
            p.wait(v, "ldx")
            p.wait(v, "ldw")
            v.memset(p.ones_f32[:, :], 1.0)
            v.memset(p.ones_bf[:, :], 1.0)
            v.memset(p.ones_col[:, :], 1.0)
            p.inc(v.memset(p.ones_col_bf[:, :], 1.0), "crdy")

        lnsem, _c = emit_layernorm(p, blk, "ln1", x_sb, xh_sb, TPC, ["crdy"])
        dbg = None
        if debug:
            dbg = {"xh": dbg_xh, "lg": dbg_lg, "cT": dbg_cT, "act": dbg_act,
                   "cbc": dbg_cbc, "sg": dbg_sg, "ucs": dbg_ucs,
                   "misc": dbg_misc}
        fsem, _c2 = emit_ffn(p, blk, "f1", xh_sb, x_sb, w1p, w2t, gate_sb,
                             TPC, [lnsem], dbg=dbg)

        @blk.sync
        def _(sy):
            p.wait(sy, fsem)
            p.inc(sy.dma_start(
                out=uT.rearrange("k pp m -> pp k m"),
                in_=x_sb.ap().rearrange("pp (k m) -> pp k m", k=16)),
                "std", 16)
            p.wait(sy, "std")
    return nc


def host_prep_common(inputs):
    x = np.asarray(inputs["L0_attn_x"], np.float32).reshape(S, D)
    xT = np.ascontiguousarray(x.T)
    w1p, w2t, gateT = prep_ffn_weights(inputs)
    cpool = _cpool_np()
    ident = np.eye(128, dtype=np.float32)
    return x, xT, w1p, w2t, gateT, cpool, ident


def run_launch_A(inputs):
    x, xT, w1p, w2t, gateT, cpool, ident = host_prep_common(inputs)
    nc = build_A()
    ims = []
    for c in range(NCORE):
        sl = xT[:, c * TPC:(c + 1) * TPC]
        ims.append({
            "xT": np.ascontiguousarray(sl.reshape(16, 128, TPC)),
            "cpool": cpool, "ident": ident,
            "w1p": w1p, "w2t": w2t, "gateT": gateT,
        })
    res = run_bass_kernel_spmd(nc, ims, core_ids=list(range(NCORE)))
    u = np.empty((D, S), np.float32)
    for c in range(NCORE):
        u[:, c * TPC:(c + 1) * TPC] = \
            np.asarray(res.results[c]["uT"]).reshape(D, TPC)
    return u




NKT = 64
DIAG_T = {0: list(range(56, 60)), 1: list(range(60, 64))}
LAST_T = {0: 60, 1: 64}


def build_B():
    nc = bass.Bass("TRN2", target_bir_lowering=False)
    ap = lambda *a, **k: nc.dram_tensor(*a, **k).ap()
    uf = ap("uf", [16, 128, S], F32, kind="ExternalInput")
    uo = ap("uo", [16, 128, TPC], F32, kind="ExternalInput")
    cpool_h = ap("cpool", [128, 16], F32, kind="ExternalInput")
    ident_h = ap("ident", [128, 128], F32, kind="ExternalInput")
    cosg_h = ap("cosg", [64, S], F32, kind="ExternalInput")
    sing_h = ap("sing", [64, S], F32, kind="ExternalInput")
    coso_h = ap("coso", [64, TPC], F32, kind="ExternalInput")
    sino_h = ap("sino", [64, TPC], F32, kind="ExternalInput")
    dmask_h = ap("dmask", [128, 128], F32, kind="ExternalInput")
    kmask_h = ap("kmask", [4, 128, 512], F32, kind="ExternalInput")
    meta_h = ap("meta", [1, 8], I32, kind="ExternalInput")
    wqaT_h = ap("wqaT", [16, 128, QLR], F32, kind="ExternalInput")
    wqab_h = ap("wqab", [128, 12], F32, kind="ExternalInput")
    qnw_h = ap("qnw", [128, 12], F32, kind="ExternalInput")
    wqbT_h = ap("wqbT", [H, 12, 128, 192], F32, kind="ExternalInput")
    wqbb_h = ap("wqbb", [H, 128, 2], F32, kind="ExternalInput")
    wkvaT_h = ap("wkvaT", [16, 128, 576], F32, kind="ExternalInput")
    wkvab_h = ap("wkvab", [128, 5], F32, kind="ExternalInput")
    kvnw_h = ap("kvnw", [128, 4], F32, kind="ExternalInput")
    wknT_h = ap("wknT", [H, 4, 128, 128], F32, kind="ExternalInput")
    wkvvT_h = ap("wkvvT", [H, 4, 128, 128], F32, kind="ExternalInput")
    woT_h = ap("woT", [16, 128, D], F32, kind="ExternalInput")
    wob_h = ap("wob", [128, 16], F32, kind="ExternalInput")
    w1p_h = ap("w1p", [E, 16, 128, 2816], F32, kind="ExternalInput")
    w2t_h = ap("w2t", [E, 11, 128, 2048], F32, kind="ExternalInput")
    gateT_h = ap("gateT", [16, 128, 8], F32, kind="ExternalInput")
    outT = ap("outT", [16, 128, TPC], F32, kind="ExternalOutput")

    kvfm_d = nc.dram_tensor("kvfm_d", [128, 4 * 2 * S], BF16, kind="Internal")
    kvfm_g = nc.dram_tensor("kvfm_g", [128, 4 * S], BF16, kind="Internal")
    kpe_g = nc.dram_tensor("kpe_g", [66, S], BF16, kind="Internal")
    zrow_d = nc.dram_tensor("zrow_d", [1, S], BF16, kind="Internal")
    xh_d = nc.dram_tensor("xh_d", [128, 16 * S], BF16, kind="Internal")
    kpe_d = nc.dram_tensor("kpe_d", [66, 2 * S], BF16, kind="Internal")
    qn_d = nc.dram_tensor("qn_d", [H, 128, TPC], BF16, kind="Internal")
    qaug_d = nc.dram_tensor("qaug_d", [H, 66, TPC], BF16, kind="Internal")
    mrow_d = nc.dram_tensor("mrow_d", [8, 128], BF16, kind="Internal")
    o_d = nc.dram_tensor("o_d", [128, 16 * TPC], BF16, kind="Internal")

    with ExitStack() as stack:
        p = P(nc, stack)
        p.cpool = p.sbuf("cpool_sb", [128, 16], F32)
        p.ident_f32 = p.sbuf("ident_sb", [128, 128], F32)
        p.ident_bf = p.sbuf("identb_sb", [128, 128], BF16)
        p.ones_f32 = p.sbuf("ones_f32", [128, 128], F32)
        p.ones_bf = p.sbuf("ones_bf", [128, 128], BF16)
        p.ones_col = p.sbuf("ones_col", [1, 128], F32)
        p.ones_col_bf = p.sbuf("ones_col_bf", [1, 128], BF16)
        ones_row = p.sbuf("ones_row", [1, 512], BF16)
        zeros_row = p.sbuf("zeros_row", [1, 512], BF16)
        wob_sb = p.sbuf("wob_sb", [128, 16], F32)
        p.meta_sb = p.sbuf("meta_sb", [1, 8], I32)
        win_mark0 = p.sbuf("win_mark0", [128, 16], F32)
        p.new_sem("ldx", "ldw", "std", "crdy", "kvd", "win", "qpk", "odw")
        blk = p.en(nc.Block())

        @blk.sync
        def _(sy):
            p.inc(sy.dma_start(out=p.cpool[:, :], in_=cpool_h), "ldx", 16)
            p.inc(sy.dma_start(out=p.ident_f32[:, :], in_=ident_h), "ldx", 16)
            p.inc(sy.dma_start(out=wob_sb[:, :], in_=wob_h), "ldx", 16)
            p.inc(sy.dma_start(out=p.meta_sb[:, :], in_=meta_h), "ldx", 16)

        @blk.gpsimd
        def _(g):
            p.inc(g.dma_start(out=p.ident_bf[:, :], in_=ident_h), "ldw", 16)

        @blk.vector
        def _(v):
            p.wait(v, "ldx")
            p.wait(v, "ldw")
            v.memset(p.ones_f32[:, :], 1.0)
            v.memset(p.ones_bf[:, :], 1.0)
            v.memset(p.ones_col[:, :], 1.0)
            v.memset(p.ones_col_bf[:, :], 1.0)
            v.memset(zeros_row[:, :], 0.0)
            p.inc(v.memset(ones_row[:, :], 1.0), "crdy")

        negbig_row = p.sbuf("negbig_row", [1, 512], BF16)

        @blk.vector
        def _(v):
            p.wait(v, "crdy")
            p.inc(v.memset(negbig_row[:, :], -30000.0), "crdy")

        @blk.sync
        def _(sy):
            p.wait(sy, "crdy")
            for g_ in range(16):
                p.inc(sy.dma_start(
                    out=kpe_d.ap()[64:65, g_ * 512:(g_ + 1) * 512],
                    in_=ones_row[0:1, :]), "kvd", 16)
                p.inc(sy.dma_start(
                    out=kpe_d.ap()[65:66, g_ * 512:(g_ + 1) * 512],
                    in_=negbig_row[0:1, :]), "kvd", 16)

        with ExitStack() as b2:
            en2 = b2.enter_context
            ut = [en2(nc.sbuf_tensor("b2ut0", [128, 16 * 512], F32))]
            xh2 = en2(nc.sbuf_tensor("b2xh", [128, 16 * 512], BF16))
            cosg = en2(nc.sbuf_tensor("b2cos", [64, 512], F32))
            sing = en2(nc.sbuf_tensor("b2sin", [64, 512], F32))
            wkva = en2(nc.sbuf_tensor("b2wkva", [128, 16 * 576], BF16))
            wkvab = en2(nc.sbuf_tensor("b2wkvab", [128, 5], F32))
            kvnw = en2(nc.sbuf_tensor("b2kvnw", [128, 4], F32))
            kvt = en2(nc.sbuf_tensor("b2kvt", [128, 4 * 512], F32))
            sqkv = en2(nc.sbuf_tensor("b2sq", [128, 4 * 512], BF16))
            rstd2 = en2(nc.sbuf_tensor("b2rstd", [128, 512], F32))
            t1b = en2(nc.sbuf_tensor("b2t1", [128, 512], F32))
            kvbf = en2(nc.sbuf_tensor("b2kvbf", [128, 4 * 512], BF16))
            kperaw = en2(nc.sbuf_tensor("b2kpr", [64, 512], F32))
            kperot = en2(nc.sbuf_tensor("b2krt", [64, 512], F32))
            kpea = en2(nc.sbuf_tensor("b2kpa", [64, 512], F32))
            kpebf = en2(nc.sbuf_tensor("b2kpb", [64, 512], BF16))
            psb = [en2(nc.psum_tensor(f"b2ps{i}", [128, 512], F32))
                   for i in range(6)]
            zfill = en2(nc.sbuf_tensor("b2zf", [128, 2048], BF16))
            p.new_sem("b2mm", "b2vv", "b2ss", "b2ld")

            @blk.vector
            def _(v):
                p.wait(v, "crdy")
                p.inc(v.memset(zfill[:, :], 0.0), "crdy")

            @blk.sync
            def _(sy):
                p.wait(sy, "crdy")
                p.inc(sy.dma_start(out=wkvab[:, :], in_=wkvab_h), "ldx", 16)
                p.inc(sy.dma_start(out=kvnw[:, :], in_=kvnw_h), "ldx", 16)
                for m in range(4):
                    for cc_ in range(4):
                        p.inc(sy.dma_start(
                            out=kvfm_d.ap()[:, m * 2 * S + cc_ * 2048:
                                            m * 2 * S + (cc_ + 1) * 2048],
                            in_=zfill[:, :]), "kvd", 16)
                for cc_ in range(16):
                    p.inc(sy.dma_start(
                        out=kpe_d.ap()[0:64, cc_ * 512:(cc_ + 1) * 512],
                        in_=zfill[0:64, 0:512]), "kvd", 16)

            @blk.gpsimd
            def _(g):
                p.wait(g, "crdy")
                p.inc(g.dma_start(
                    out=wkva.ap().rearrange("pp (k m) -> pp k m", k=16),
                    in_=wkvaT_h.rearrange("k pp m -> pp k m")), "b2ld", 16)

            for g_ in range(16):
                gb = g_ % 2
                ent = {n: p.cnt[n] for n in
                       ("b2mm", "b2vv", "b2ss", "kvd")}

                @blk.sync
                def _(sy, g_=g_, gb=gb, ent=ent):
                    p.wait_at(sy, "b2vv", ent["b2vv"])
                    p.wait_at(sy, "b2mm", ent["b2mm"])
                    p.inc(sy.dma_start(
                        out=ut[0].ap().rearrange("pp (k m) -> pp k m", k=16),
                        in_=uf[:, :, g_ * 512:(g_ + 1) * 512].rearrange(
                            "k pp m -> pp k m")), "ldx", 16)
                    p.inc(sy.dma_start(
                        out=cosg[:, :],
                        in_=cosg_h[:, g_ * 512:(g_ + 1) * 512]), "ldx", 16)
                    p.inc(sy.dma_start(
                        out=sing[:, :],
                        in_=sing_h[:, g_ * 512:(g_ + 1) * 512]), "ldx", 16)

                lnr, _cc = emit_layernorm(p, blk, f"b2l{g_}", ut[0], xh2,
                                          512,
                                          ["ldx", "b2ld", "b2ss", "b2vv"],
                                          sem_tag="b2l",
                                          pss_ext=[psb[4], psb[5]])

                @blk.tensor
                def _(t, g_=g_, ent=ent):
                    p.wait(t, lnr)
                    p.wait_at(t, "b2ss", ent["b2ss"])
                    p.wait_at(t, "b2vv", ent["b2vv"])
                    p.wait_at(t, "kvd", ent["kvd"])
                    for m in range(5):
                        w = 128 if m < 4 else 64
                        for k in range(16):
                            ins = t.matmul(
                                psb[m][0:w, :],
                                wkva[:, k * 576 + m * 128:k * 576 + m * 128 + w],
                                xh2[:, k * 512:(k + 1) * 512],
                                start=(k == 0), stop=(k == 15))
                        mmv = p.cnt["b2mm"] + 1
                        p.inc(ins, "b2mm")

                        @blk.vector
                        def _(v, m=m, w=w, mmv=mmv):
                            p.wait_at(v, "b2mm", mmv)
                            if m < 4:
                                p.edge(v, v.tensor_scalar(
                                    kvt[:, m * 512:(m + 1) * 512],
                                    psb[m][:, :], wkvab[:, m:m + 1],
                                    None, ALU.add), "b2ss")
                                p.inc(v.tensor_mul(
                                    sqkv[:, m * 512:(m + 1) * 512],
                                    kvt[:, m * 512:(m + 1) * 512],
                                    kvt[:, m * 512:(m + 1) * 512]), "b2ss")
                            else:
                                p.inc(v.tensor_scalar(
                                    kperaw[:, :], psb[4][0:64, :],
                                    wkvab[0:64, 4:5], None, ALU.add), "b2ss")

                @blk.tensor
                def _(t, g_=g_):
                    p.wait(t, "b2ss")
                    for m in range(4):
                        ins = t.matmul(psb[5][:, :], p.ones_bf[:, :],
                                       sqkv[:, m * 512:(m + 1) * 512],
                                       start=(m == 0), stop=(m == 3))
                    p.inc(ins, "b2mm")

                @blk.scalar
                def _(s, g_=g_):
                    p.wait(s, "b2mm")
                    p.edge(s, s.activation(rstd2[:, :], psb[5][:, :], AF.Ln,
                                           bias=p.cp("rms_eps"),
                                           scale=p.cp("inv_kvlr")), "b2ss")
                    p.inc(s.activation(rstd2[:, :], rstd2[:, :], AF.Exp,
                                       scale=p.cp("neg_half")), "b2ss")

                @blk.vector
                def _(v, g_=g_):
                    p.wait(v, "b2ss")
                    for m in range(4):
                        p.edge(v, v.tensor_mul(t1b[:, :],
                                               kvt[:, m * 512:(m + 1) * 512],
                                               rstd2[:, :]), "b2vv")
                        ins = v.tensor_scalar(kvbf[:, m * 512:(m + 1) * 512],
                                              t1b[:, :], kvnw[:, m:m + 1],
                                              None, ALU.mult)
                        if m < 3:
                            p.edge(v, ins, "b2vv")
                    p.inc(ins, "b2vv")

                @blk.scalar
                def _(s, g_=g_):
                    p.wait(s, "b2ss")
                    p.edge(s, s.mul(kperot[0:32, :], kperaw[32:64, :],
                                    p.cp("neg_one")[0:32, :]), "b2ss")
                    p.inc(s.copy(kperot[32:64, :], kperaw[0:32, :]), "b2ss")

                @blk.vector
                def _(v, g_=g_):
                    p.wait(v, "b2ss")
                    p.edge(v, v.tensor_mul(
                        kpea[:, :], kperaw[:, :], cosg[:, :]), "b2vv")
                    p.edge(v, v.tensor_mul(
                        kperot[:, :], kperot[:, :], sing[:, :]), "b2vv")
                    p.inc(v.tensor_add(kpebf[:, :], kpea[:, :],
                                       kperot[:, :]), "b2vv")
                kv_done = p.cnt["b2vv"]

                @blk.sync
                def _(sy, g_=g_):
                    p.wait(sy, "b2vv")
                    p.wait(sy, "b2ss")
                    p.inc(sy.dma_start(
                        out=xh_d.ap().rearrange(
                            "pp (k m) -> pp k m", k=16)[:, :,
                                                        g_ * 512:(g_ + 1) * 512],
                        in_=xh2.ap().rearrange("pp (k m) -> pp k m", k=16)),
                        "kvd", 16)
                    p.inc(sy.dma_start(
                        out=kvfm_g.ap().rearrange(
                            "pp (m t) -> pp m t", m=4)[:, :,
                                                       g_ * 512:(g_ + 1) * 512],
                        in_=kvbf.ap().rearrange("pp (m t) -> pp m t", m=4)),
                        "kvd", 16)
                    p.inc(sy.dma_start(
                        out=kpe_g.ap()[0:64, g_ * 512:(g_ + 1) * 512],
                        in_=kpebf[:, :]), "kvd", 16)
                    if g_ == 0:
                        for cc_ in range(16):
                            p.inc(sy.dma_start(
                                out=kpe_g.ap()[65:66,
                                               cc_ * 512:(cc_ + 1) * 512],
                                in_=zeros_row[0:1, :]), "kvd", 16)

        with ExitStack() as b4:
            en4 = b4.enter_context
            coso = en4(nc.sbuf_tensor("b4cos", [64, TPC], F32))
            sino = en4(nc.sbuf_tensor("b4sin", [64, TPC], F32))
            wqab = en4(nc.sbuf_tensor("b4wqab", [128, 12], F32))
            qnw = en4(nc.sbuf_tensor("b4qnw", [128, 12], F32))
            rstd4 = en4(nc.sbuf_tensor("b4rstd", [128, TPC], F32))
            t14 = en4(nc.sbuf_tensor("b4t1", [128, TPC], F32))
            qn = en4(nc.sbuf_tensor("b4qn", [128, 12 * TPC], BF16))
            ps4 = [en4(nc.psum_tensor(f"b4ps{i}", [128, 512], F32))
                   for i in range(6)]
            p.new_sem("b4mm", "b4vv", "b4ss", "b4w")
            b4s1 = ExitStack()
            en41 = b4s1.enter_context
            qlo = en41(nc.sbuf_tensor("b4qlo", [128, 12 * TPC], BF16))
            sq4 = en41(nc.sbuf_tensor("b4sq", [128, 12 * TPC], BF16))
            xho = en41(nc.sbuf_tensor("b4xh", [128, 16 * TPC], BF16))
            wqa = [en41(nc.sbuf_tensor(f"b4wqa{i}", [128, 16 * 512], BF16))
                   for i in range(2)]

            @blk.sync
            def _(sy):
                p.wait(sy, "kvd")
                p.inc(sy.dma_start(out=coso[:, :], in_=coso_h), "ldx", 16)
                p.inc(sy.dma_start(out=sino[:, :], in_=sino_h), "ldx", 16)
                p.inc(sy.dma_start(out=wqab[:, :], in_=wqab_h), "ldx", 16)
                p.inc(sy.dma_start(out=qnw[:, :], in_=qnw_h), "ldx", 16)
                with sy.register("ost") as r2:
                    sy.reg_load(r2, p.meta_sb[0:1, 2:3])
                    ost = sy.snap(r2, min_val=0, max_val=S - TPC)
                    p.inc(sy.dma_start(
                        out=xho.ap().rearrange("pp (k m) -> pp k m", k=16),
                        in_=xh_d.ap().rearrange(
                            "pp (k m) -> pp k m", k=16)[
                            :, :, bass.ds(ost, TPC)]), "ldx", 16)

            lnr3 = "ldx"
            bank_free = {}

            for sl in range(3):
                @blk.gpsimd
                def _(g, sl=sl):
                    if sl >= 2:
                        p.wait_at(g, "b4mm", p.cnt["b4mm"])
                    else:
                        p.wait(g, "kvd")
                    p.inc(g.dma_start(
                        out=wqa[sl % 2].ap().rearrange(
                            "pp (k m) -> pp k m", k=16),
                        in_=wqaT_h[:, :, sl * 512:(sl + 1) * 512].rearrange(
                            "k pp m -> pp k m")), "b4w", 16)
                w_need = p.cnt["b4w"]

                @blk.tensor
                def _(t, sl=sl, w_need=w_need):
                    p.wait_at(t, "b4w", w_need)
                    p.wait(t, lnr3)
                    for ml in range(4):
                        m = sl * 4 + ml
                        for hh in range(NH):
                            key = f"ql{ml % 2}"
                            if key in bank_free:
                                t.wait_ge(p.sem["b4ss"], bank_free[key])
                            bk = ps4[ml % 2]
                            for k in range(16):
                                ins = t.matmul(
                                    bk[:, :],
                                    wqa[sl % 2][:, k * 512 + ml * 128:
                                                k * 512 + ml * 128 + 128],
                                    xho[:, k * TPC + hh * 512:
                                        k * TPC + hh * 512 + 512],
                                    start=(k == 0), stop=(k == 15))
                            mmv = p.cnt["b4mm"] + 1
                            p.inc(ins, "b4mm")

                            @blk.vector
                            def _(v, m=m, hh=hh, mmv=mmv, bk=bk):
                                p.wait_at(v, "b4mm", mmv)
                                p.edge(v, v.tensor_scalar(
                                    qlo[:, m * TPC + hh * 512:
                                        m * TPC + hh * 512 + 512],
                                    bk[:, :], wqab[:, m:m + 1],
                                    None, ALU.add), "b4ss")
                                p.inc(v.tensor_mul(
                                    sq4[:, m * TPC + hh * 512:
                                        m * TPC + hh * 512 + 512],
                                    qlo[:, m * TPC + hh * 512:
                                        m * TPC + hh * 512 + 512],
                                    qlo[:, m * TPC + hh * 512:
                                        m * TPC + hh * 512 + 512]), "b4ss")
                            bank_free[key] = p.cnt["b4ss"]

            @blk.tensor
            def _(t):
                p.wait(t, "b4ss")
                for hh in range(NH):
                    if "r5" in bank_free:
                        t.wait_ge(p.sem["b4ss"], bank_free["r5"])
                    for m in range(12):
                        ins = t.matmul(ps4[5][:, :], p.ones_bf[:, :],
                                       sq4[:, m * TPC + hh * 512:
                                           m * TPC + hh * 512 + 512],
                                       start=(m == 0), stop=(m == 11))
                    mmv = p.cnt["b4mm"] + 1
                    p.inc(ins, "b4mm")

                    @blk.scalar
                    def _(s, hh=hh, mmv=mmv):
                        p.wait_at(s, "b4mm", mmv)
                        p.edge(s, s.activation(
                            rstd4[:, hh * 512:hh * 512 + 512], ps4[5][:, :],
                            AF.Ln, bias=p.cp("rms_eps"),
                            scale=p.cp("inv_qlr")), "b4ss")
                        p.inc(s.activation(
                            rstd4[:, hh * 512:hh * 512 + 512],
                            rstd4[:, hh * 512:hh * 512 + 512],
                            AF.Exp, scale=p.cp("neg_half")), "b4ss")
                    bank_free["r5"] = p.cnt["b4ss"]

            @blk.vector
            def _(v):
                p.wait(v, "b4ss")
                for m in range(12):
                    p.edge(v, v.tensor_mul(t14[:, :],
                                           qlo[:, m * TPC:(m + 1) * TPC],
                                           rstd4[:, :]), "b4vv")
                    ins = v.tensor_scalar(
                        qn[:, m * TPC:(m + 1) * TPC], t14[:, :],
                        qnw[:, m:m + 1], None, ALU.mult)
                    if m < 11:
                        p.edge(v, ins, "b4vv")
                p.inc(ins, "b4vv")
            qn_done = p.cnt["b4vv"]

            @blk.sync
            def _(sy):
                p.wait(sy, "b4vv")
                p.wait(sy, "b4ss")
                p.inc(sy.dma_start(out=win_mark0[:, :], in_=cpool_h),
                      "qpk", 16)
            b4s1.close()
            wqb = [en4(nc.sbuf_tensor(f"b4wqb{i}", [128, 12 * 192], BF16))
                   for i in range(2)]
            wqbb = en4(nc.sbuf_tensor("b4wqbb", [128, 2], F32))
            qpe = en4(nc.sbuf_tensor("b4qpe", [64, TPC], F32))
            qper = en4(nc.sbuf_tensor("b4qper", [64, TPC], F32))
            qnope = en4(nc.sbuf_tensor("b4qnope", [128, TPC], BF16))
            qaug = en4(nc.sbuf_tensor("b4qaug", [66, TPC], BF16))

            @blk.sync
            def _(sy):
                p.wait(sy, "qpk")
                for cc_ in range(2):
                    p.inc(sy.dma_start(
                        out=qaug.ap()[65:66, cc_ * 512:(cc_ + 1) * 512],
                        in_=ones_row[0:1, :]), "qpk", 16)

            for h in range(H):
                @blk.gpsimd
                def _(g, h=h):
                    if h >= 2:
                        p.wait_at(g, "b4mm", p.cnt["b4mm"])
                    else:
                        p.wait(g, "qpk")
                    p.inc(g.dma_start(
                        out=wqb[h % 2].ap().rearrange(
                            "pp (k m) -> pp k m", k=12),
                        in_=wqbT_h[h].rearrange("k pp m -> pp k m")),
                        "b4w", 16)

                @blk.sync
                def _(sy, h=h):
                    p.wait(sy, "qpk")
                    p.inc(sy.dma_start(out=wqbb[:, :], in_=wqbb_h[h]),
                          "ldx", 16)
                wqbb_need = p.cnt["ldx"]
                w_need = p.cnt["b4w"]

                @blk.vector
                def _(v, h=h, wqbb_need=wqbb_need):
                    p.wait_at(v, "ldx", wqbb_need)
                    p.wait(v, "b4vv")
                    p.inc(v.tensor_scalar(wqbb[:, :], wqbb[:, :],
                                          p.cp("scale"), None, ALU.mult),
                          "b4vv")
                bias_done = p.cnt["b4vv"]

                @blk.tensor
                def _(t, h=h, w_need=w_need, bias_done=bias_done):
                    p.wait_at(t, "b4w", w_need)
                    p.wait_at(t, "b4vv", max(qn_done, bias_done))
                    p.wait(t, "qpk")
                    for mq in range(2):
                        w = 128 if mq == 0 else 64
                        for hh in range(NH):
                            key = f"qf{mq}"
                            if key in bank_free:
                                t.wait_ge(p.sem["b4ss"], bank_free[key])
                            for k in range(12):
                                ins = t.matmul(
                                    ps4[2 + mq][0:w, :],
                                    wqb[h % 2][:, k * 192 + mq * 128:
                                               k * 192 + mq * 128 + w],
                                    qn[:, k * TPC + hh * 512:
                                       k * TPC + hh * 512 + 512],
                                    start=(k == 0), stop=(k == 11))
                            mmv = p.cnt["b4mm"] + 1
                            p.inc(ins, "b4mm")

                            @blk.vector
                            def _(v, mq=mq, w=w, hh=hh, mmv=mmv):
                                p.wait_at(v, "b4mm", mmv)
                                if mq == 0:
                                    p.inc(v.tensor_scalar(
                                        qnope[:, hh * 512:hh * 512 + 512],
                                        ps4[2][:, :], p.cp("scale"),
                                        wqbb[:, 0:1], ALU.mult, ALU.add),
                                        "b4ss")
                                else:
                                    p.inc(v.tensor_scalar(
                                        qpe[:, hh * 512:hh * 512 + 512],
                                        ps4[3][0:64, :],
                                        p.cp("scale")[0:64, :],
                                        wqbb[0:64, 1:2], ALU.mult, ALU.add),
                                        "b4ss")
                            bank_free[key] = p.cnt["b4ss"]
                qf_done = p.cnt["b4ss"]

                @blk.scalar
                def _(s, h=h, qf_done=qf_done):
                    p.wait_at(s, "b4ss", qf_done)
                    p.wait(s, "qpk")
                    p.edge(s, s.mul(qper[0:32, :], qpe[32:64, :],
                                    p.cp("neg_one")[0:32, :]), "b4ss")
                    p.inc(s.copy(qper[32:64, :], qpe[0:32, :]), "b4ss")

                @blk.vector
                def _(v, h=h):
                    p.wait(v, "b4ss")
                    p.wait(v, "qpk")
                    p.edge(v, v.tensor_mul(qpe[:, :], qpe[:, :], coso[:, :]),
                           "b4vv")
                    p.edge(v, v.tensor_mul(qper[:, :], qper[:, :],
                                           sino[:, :]), "b4vv")
                    p.edge(v, v.tensor_add(qpe[:, :], qpe[:, :], qper[:, :]),
                           "b4vv")
                    p.inc(v.tensor_copy(qaug[0:64, :], qpe[:, :]), "b4vv")

                @blk.sync
                def _(sy, h=h):
                    p.wait(sy, "b4ss")
                    p.wait(sy, "b4vv")
                    p.inc(sy.dma_start(
                        out=qaug_d.ap()[h, 0:64, :], in_=qaug[0:64, :]),
                        "qpk", 16)
                    p.inc(sy.dma_start(
                        out=qaug_d.ap()[h, 65:66, :], in_=qaug[65:66, :]),
                        "qpk", 16)
                    p.inc(sy.dma_start(out=qn_d.ap()[h], in_=qnope[:, :]),
                          "qpk", 16)

        with ExitStack() as b5:
            en5 = b5.enter_context
            kvfm = en5(nc.sbuf_tensor("kvfm", [128, 4 * S], BF16))
            kpew = en5(nc.sbuf_tensor("kpew", [66, S], BF16))
            knop = en5(nc.sbuf_tensor("b5kn", [128, S], BF16))
            vsb = en5(nc.sbuf_tensor("b5vs", [128, S], BF16))
            wkvvT = en5(nc.sbuf_tensor("wkvvT_sb", [128, H * 4 * 128], BF16))
            wknT = en5(nc.sbuf_tensor("wknT_sb", [128, H * 4 * 128], BF16))
            kmask = en5(nc.sbuf_tensor("kmask_sb", [128, 4 * 512], BF16))
            dmask5 = en5(nc.sbuf_tensor("b5dm", [128, 128], F32))
            qns4 = [en5(nc.sbuf_tensor(f"b5qn{i}", [128, TPC], BF16))
                    for i in range(2)]
            qgs4 = [en5(nc.sbuf_tensor(f"b5qg{i}", [66, TPC], BF16))
                    for i in range(2)]
            probs = [en5(nc.sbuf_tensor(f"b5pr{i}", [128, 512], BF16))
                     for i in range(3)]
            denr = en5(nc.sbuf_tensor("b5den", [128, 512], F32))
            ovh = en5(nc.sbuf_tensor("b5ov", [128, 512], BF16))
            m8 = en5(nc.sbuf_tensor("b5m8", [128, 8], F32))
            m8b = en5(nc.sbuf_tensor("b5m8b", [128, 8], BF16))
            mr = en5(nc.sbuf_tensor("b5mr", [8, 128], BF16))
            tdg = en5(nc.sbuf_tensor("b5td", [128, 128], F32))
            win_mark = en5(nc.sbuf_tensor("win_mark", [128, 16], F32))
            psS = [en5(nc.psum_tensor(f"b5s{i}", [128, 512], F32))
                   for i in range(2)]
            psO = [en5(nc.psum_tensor(f"b5o{i}", [128, 512], F32))
                   for i in range(2)]
            psD = en5(nc.psum_tensor("b5d", [128, 512], F32))
            psE = [en5(nc.psum_tensor(f"b5e{i}", [128, 512], F32))
                   for i in range(2)]
            ptm = en5(nc.psum_tensor("b5pt", [128, 128], BF16))
            p.new_sem("b5mm", "b5vv", "b5ss", "b5q", "qgr")

            @blk.gpsimd
            def _(g):
                p.wait(g, "qpk")
                p.inc(g.dma_start(
                    out=wkvvT.ap().rearrange("pp (h c m) -> pp h c m",
                                             h=H, c=4),
                    in_=wkvvT_h.rearrange("h c pp m -> pp h c m")),
                    "ldw", 16)
                p.inc(g.dma_start(
                    out=wknT.ap().rearrange("pp (h c m) -> pp h c m",
                                            h=H, c=4),
                    in_=wknT_h.rearrange("h c pp m -> pp h c m")), "ldw", 16)
                p.inc(g.dma_start(
                    out=kmask.ap().rearrange("pp (r m) -> pp r m", r=4),
                    in_=kmask_h.rearrange("r pp m -> pp r m")), "ldw", 16)

            @blk.sync
            def _(sy):
                p.wait(sy, "qpk")
                p.inc(sy.dma_start(out=dmask5[:, :], in_=dmask_h), "ldx", 16)
            dm_need = p.cnt["ldx"]

            @blk.vector
            def _(v):
                p.wait(v, "qpk")
                p.inc(v.memset(win_mark[:, :], 0.0), "b5vv")

            @blk.sync
            def _(sy):
                p.wait(sy, "b5vv")
                p.wait(sy, "kvd")
                p.wait(sy, "ldx")
                winfm = kvfm_d.ap().rearrange("pp (b t) -> pp b t", t=512)
                glofm = kvfm_g.ap().rearrange("pp (b t) -> pp b t", t=512)
                winpe = kpe_d.ap().rearrange("pp (b t) -> pp b t", t=512)
                glope = kpe_g.ap().rearrange("pp (b t) -> pp b t", t=512)
                with sy.register("pB") as r0:
                    sy.reg_load(r0, p.meta_sb[0:1, 3:4])
                    padB = sy.snap(r0, min_val=0, max_val=(S - TPC) // 512)
                    for m in range(4):
                        p.inc(sy.dma_start(
                            out=winfm[:, bass.ds(padB + m * 32, 16), :],
                            in_=glofm[:, m * 16:(m + 1) * 16, :]),
                            "win", 16)
                    p.inc(sy.dma_start(
                        out=winpe[0:64, bass.ds(padB, 16), :],
                        in_=glope[0:64, :, :]), "win", 16)
                    p.inc(sy.dma_start(
                        out=winpe[65:66, bass.ds(padB, 16), :],
                        in_=glope[65:66, :, :]), "win", 16)
                p.wait(sy, "win")
                p.inc(sy.dma_start(
                    out=kvfm.ap().rearrange("pp (m t) -> pp m t", m=4),
                    in_=kvfm_d.ap().rearrange(
                        "pp (m t) -> pp m t", m=4)[:, :, 0:S]), "odw", 16)
                p.inc(sy.dma_start(out=kpew.ap()[:, :],
                                   in_=kpe_d.ap()[:, 0:S]), "odw", 16)

            eg = [0]

            def emit_kng(g_, h, first=False):
                par = eg[0] % 2
                eg[0] += 1

                @blk.tensor
                def _(t, g_=g_, h=h, par=par, first=first):
                    if first:
                        p.wait(t, "odw")
                        p.wait(t, "ldw")
                    p.bank_wait(t, [f"b5e{par}"])
                    for c in range(4):
                        ins = t.matmul(
                            psE[par][:, :],
                            wknT[:, h * 512 + c * 128:h * 512 + c * 128 + 128],
                            kvfm[:, c * S + g_ * 512:c * S + g_ * 512 + 512],
                            start=(c == 0), stop=(c == 3))
                    p.inc(ins, "b5mm")
                mmv = p.cnt["b5mm"]

                @blk.vector
                def _(v, g_=g_, par=par, mmv=mmv):
                    p.wait_at(v, "b5mm", mmv)
                    p.inc(v.tensor_copy(
                        knop[:, g_ * 512:(g_ + 1) * 512], psE[par][:, :]),
                        "b5vv")
                p.bank_set([f"b5e{par}"], "b5vv")

            def emit_vg(g_, h):
                par = eg[0] % 2
                eg[0] += 1

                @blk.tensor
                def _(t, g_=g_, h=h, par=par):
                    p.bank_wait(t, [f"b5e{par}"])
                    for j in range(4):
                        for c in range(4):
                            ins = t.matmul(
                                psE[par][:, j * 128:j * 128 + 128],
                                kvfm[:, c * S + g_ * 512 + j * 128:
                                     c * S + g_ * 512 + j * 128 + 128],
                                wkvvT[:, h * 512 + c * 128:
                                      h * 512 + c * 128 + 128],
                                start=(c == 0), stop=(c == 3))
                    p.inc(ins, "b5mm")
                mmv = p.cnt["b5mm"]

                @blk.vector
                def _(v, g_=g_, par=par, mmv=mmv):
                    p.wait_at(v, "b5mm", mmv)
                    p.inc(v.tensor_copy(
                        vsb[:, g_ * 512:(g_ + 1) * 512], psE[par][:, :]),
                        "b5vv")
                p.bank_set([f"b5e{par}"], "b5vv")

            for h in range(H):
                par_h = h % 2

                @blk.sync
                def _(sy, h=h, par_h=par_h):
                    if h < 2:
                        p.wait(sy, "qpk")
                    p.bank_wait(sy, [f"b5qb{par_h}"])
                    p.inc(sy.dma_start(out=qns4[par_h][:, :],
                                       in_=qn_d.ap()[h]), "b5q", 16)
                    p.inc(sy.dma_start(out=qgs4[par_h][:, :],
                                       in_=qaug_d.ap()[h]), "b5q", 16)
                q_need = p.cnt["b5q"]

                emit_kng(14, h, first=(h == 0))
                emit_kng(15, h)
                kn_tail = p.cnt["b5vv"]

                for qb in range(8):
                    sbq, j = qb // 4, qb % 4
                    td_ = DIAG_T[sbq][j]
                    par = eg[0] % 2
                    eg[0] += 1

                    @blk.tensor
                    def _(t, h=h, qb=qb, sbq=sbq, j=j, td_=td_, par=par,
                          q_need=q_need, kn_tail=kn_tail, par_h=par_h):
                        if qb == 0:
                            p.wait_at(t, "b5q", q_need)
                            p.wait_at(t, "b5vv", kn_tail)
                        p.bank_wait(t, [f"b5e{par}"])
                        t.matmul(psE[par][:, 0:128],
                                 qns4[par_h][:, sbq * 512 + j * 128:
                                             sbq * 512 + j * 128 + 128],
                                 knop[:, td_ * 128:td_ * 128 + 128],
                                 start=True, stop=False)
                        ins = t.matmul(
                            psE[par][:, 0:128],
                            qgs4[par_h][0:64, sbq * 512 + j * 128:
                                        sbq * 512 + j * 128 + 128],
                            kpew[0:64, td_ * 128:td_ * 128 + 128],
                            start=False, stop=True)
                        p.inc(ins, "b5mm")
                    mmv = p.cnt["b5mm"]

                    @blk.vector
                    def _(v, qb=qb, par=par, mmv=mmv):
                        p.wait_at(v, "b5mm", mmv)
                        p.wait_at(v, "ldx", dm_need)
                        p.edge(v, v.tensor_add(tdg[:, :],
                                               psE[par][:, 0:128],
                                               dmask5[:, :]), "b5vv")
                        p.inc(v.reduce_max(m8[:, qb:qb + 1], tdg[:, :],
                                           axis=AX.X), "b5vv")
                    p.bank_set([f"b5e{par}"], "b5vv")

                @blk.vector
                def _(v, h=h):
                    p.wait(v, "b5vv")
                    p.inc(v.tensor_scalar(m8b[:, :], m8[:, :],
                                          p.cp("neg_one"), None, ALU.mult),
                          "b5vv")
                neg_done = p.cnt["b5vv"]

                @blk.tensor
                def _(t, h=h, neg_done=neg_done):
                    p.wait_at(t, "b5vv", neg_done)
                    p.bank_wait(t, ["b5pt"])
                    p.inc(t.transpose(ptm[0:8, :], m8b[:, :],
                                      p.ident_bf[:, :]), "b5mm")
                mmv2 = p.cnt["b5mm"]

                @blk.vector
                def _(v, h=h, mmv2=mmv2):
                    p.wait_at(v, "b5mm", mmv2)
                    p.bank_wait(v, ["b5mr"])
                    p.inc(v.tensor_copy(mr[:, :], ptm[0:8, :]), "b5vv")
                p.bank_set(["b5pt"], "b5vv")
                mr_done = p.cnt["b5vv"]

                @blk.sync
                def _(sy, h=h, mr_done=mr_done):
                    p.wait_at(sy, "b5vv", mr_done)
                    p.wait(sy, "qgr")
                    p.inc(sy.dma_start(out=mrow_d.ap(), in_=mr[:, :]),
                          "b5q", 16)
                p.bank_set(["b5mr"], "b5q")
                mrd = p.cnt["b5q"]

                @blk.sync
                def _(sy, h=h, mrd=mrd, par_h=par_h):
                    p.wait_at(sy, "b5q", mrd)
                    p.inc(sy.dma_start(
                        out=qgs4[par_h].ap()[64:65, :],
                        in_=mrow_d.ap().rearrange("(a b) q -> a (b q)", a=1)),
                        "qgr", 16)
                qgr_need = p.cnt["qgr"]

                for g_ in range(14):
                    emit_kng(g_, h)
                kn_all = p.cnt["b5vv"]
                for g_ in range(16):
                    emit_vg(g_, h)
                v_all = p.cnt["b5vv"]

                for sb in range(2):
                    ih2 = h * 2 + sb
                    opar = ih2 % 2
                    LT = LAST_T[sb]
                    ss0 = p.cnt["b5ss"]
                    mm_sc = {}
                    pv_mm = {}
                    pv_ready = {}

                    def emit_scores_t(t_):
                        sbk = t_ % 2

                        @blk.tensor
                        def _(t, sb=sb, t_=t_, sbk=sbk, opar=opar,
                              par_h=par_h, ss0=ss0):
                            if t_ == 0:
                                p.wait_at(t, "qgr", qgr_need)
                                p.wait_at(t, "b5vv", kn_all)
                                p.bank_wait(t, [f"b5o{opar}", "b5d"])
                            if t_ >= 2:
                                p.wait_at(t, "b5ss", ss0 + t_ - 1)
                            else:
                                p.bank_wait(t, [f"b5sc{sbk}"])
                            t.matmul(psS[sbk][:, :],
                                     knop[:, t_ * 128:t_ * 128 + 128],
                                     qns4[par_h][:, sb * 512:sb * 512 + 512],
                                     start=True, stop=False)
                            ins = t.matmul(
                                psS[sbk][:, :],
                                kpew[:, t_ * 128:t_ * 128 + 128],
                                qgs4[par_h][:, sb * 512:sb * 512 + 512],
                                start=False, stop=True)
                            p.inc(ins, "b5mm")
                        mm_sc[t_] = p.cnt["b5mm"]
                        pbuf = probs[t_ % 3]

                        if t_ in DIAG_T[sb]:
                            ri = t_ - DIAG_T[sb][0]

                            @blk.vector
                            def _(v, ri=ri, pbuf=pbuf, t_=t_, ss0=ss0):
                                p.wait_at(v, "b5ss", ss0 + t_ + 1)
                                p.wait(v, "ldw")
                                p.inc(v.tensor_mul(
                                    pbuf[:, :], pbuf[:, :],
                                    kmask[:, ri * 512:ri * 512 + 512]),
                                    "b5vv")
                            pv_ready[t_] = ("b5vv", p.cnt["b5vv"])
                        else:
                            pv_ready[t_] = ("b5ss", ss0 + t_ + 1)

                    def emit_pv_t(t_):
                        pbuf = probs[t_ % 3]
                        pv_sem, pv_val = pv_ready[t_]

                        @blk.tensor
                        def _(t, t_=t_, pbuf=pbuf, pv_sem=pv_sem,
                              pv_val=pv_val, opar=opar):
                            p.wait_at(t, pv_sem, pv_val)
                            if t_ == 0:
                                p.wait_at(t, "b5vv", v_all)
                            last = (t_ == LT - 1)
                            t.matmul(psO[opar][:, :],
                                     vsb[:, t_ * 128:t_ * 128 + 128],
                                     pbuf[:, :],
                                     start=(t_ == 0), stop=last)
                            ins = t.matmul(psD[:, :], p.ones_bf[:, :],
                                           pbuf[:, :],
                                           start=(t_ == 0), stop=last)
                            p.inc(ins, "b5mm")
                        pv_mm[t_] = p.cnt["b5mm"]

                    emit_scores_t(0)
                    for t_ in range(1, LT):
                        emit_scores_t(t_)
                        emit_pv_t(t_ - 1)
                    emit_pv_t(LT - 1)
                    mm_pv = p.cnt["b5mm"]

                    @blk.scalar
                    def _(s, sb=sb, LT=LT, mm_sc=dict(mm_sc)):
                        for t_ in range(LT):
                            sbk = t_ % 2
                            p.wait_at(s, "b5mm", mm_sc[t_])
                            p.inc(s.activation(probs[t_ % 3][:, :],
                                               psS[sbk][:, :], AF.Exp),
                                  "b5ss")
                    assert p.cnt["b5ss"] == ss0 + LT
                    p.bank["b5sc0"] = ("b5ss", ss0 + LT - 1)
                    p.bank["b5sc1"] = ("b5ss", ss0 + LT)
                    for idx in range(3):
                        lastt = max(t for t in range(LT) if t % 3 == idx)
                        p.bank[f"b5pb{idx}"] = ("b5mm", pv_mm[lastt])

                    @blk.vector
                    def _(v, sb=sb, h=h, mm_pv=mm_pv, opar=opar):
                        p.wait_at(v, "b5mm", mm_pv)
                        p.bank_wait(v, ["ovh"])
                        p.edge(v, v.reciprocal(denr[:, :], psD[:, :]),
                               "b5vv")
                        p.inc(v.tensor_mul(ovh[:, :], psO[opar][:, :],
                                           denr[:, :]), "b5vv")
                    p.bank_set(["b5d", f"b5o{opar}"], "b5vv")
                    ov_done = p.cnt["b5vv"]

                    @blk.sync
                    def _(sy, sb=sb, h=h, ov_done=ov_done):
                        p.wait_at(sy, "b5vv", ov_done)
                        p.inc(sy.dma_start(
                            out=o_d.ap()[:, h * TPC + sb * 512:
                                         h * TPC + sb * 512 + 512],
                            in_=ovh[:, :]), "odw", 16)
                    p.bank_set(["ovh"], "odw")
                p.bank_set([f"b5qb{par_h}"], "b5mm")

            @blk.sync
            def _(sy):
                p.wait(sy, "odw")
                p.wait(sy, "b5vv")
                p.wait(sy, "b5ss")
                p.wait(sy, "b5mm")
                p.inc(sy.dma_start(out=win_mark0[:, :], in_=cpool_h),
                      "odw", 16)

        v_sb_l = p.sbuf("b6v", [128, 16 * TPC], F32)
        with ExitStack() as b6:
            en6 = b6.enter_context
            o_sb = en6(nc.sbuf_tensor("b6o", [128, 16 * TPC], BF16))
            wo_b_ = [en6(nc.sbuf_tensor(f"b6wo{i}", [128, 16 * 512], BF16))
                     for i in range(2)]
            t1_6 = en6(nc.sbuf_tensor("b6t1", [128, 512], F32))
            ps6_ = [en6(nc.psum_tensor(f"b6ps{i}", [128, 512], F32))
                    for i in range(4)]
            p.new_sem("b6mm", "b6vv", "b6ss", "b6w")

            @blk.sync
            def _(sy):
                p.wait(sy, "odw")
                p.inc(sy.dma_start(
                    out=v_sb_l.ap().rearrange("pp (k m) -> pp k m", k=16),
                    in_=uo.rearrange("k pp m -> pp k m")), "ldx", 16)

            @blk.gpsimd
            def _(g):
                p.wait(g, "odw")
                p.inc(g.dma_start(out=o_sb[:, :], in_=o_d.ap()), "b6w", 16)

            bank6 = {}
            for mg in range(4):
                @blk.gpsimd
                def _(g, mg=mg):
                    if mg >= 2:
                        p.wait_at(g, "b6mm", p.cnt["b6mm"])
                    p.inc(g.dma_start(
                        out=wo_b_[mg % 2].ap().rearrange(
                            "pp (k m) -> pp k m", k=16),
                        in_=woT_h[:, :, mg * 512:(mg + 1) * 512].rearrange(
                            "k pp m -> pp k m")), "b6w", 16)
                w_need = p.cnt["b6w"]

                @blk.tensor
                def _(t, mg=mg, w_need=w_need):
                    p.wait_at(t, "b6w", w_need)
                    p.wait(t, "ldx")
                    for ml in range(4):
                        m = mg * 4 + ml
                        for hh in range(NH):
                            key = f"wo{ml % 2}"
                            if key in bank6:
                                t.wait_ge(p.sem["b6vv"], bank6[key])
                            for k in range(16):
                                ins = t.matmul(
                                    ps6_[ml % 2][:, :],
                                    wo_b_[mg % 2][:, k * 512 + ml * 128:
                                                  k * 512 + ml * 128 + 128],
                                    o_sb[:, k * TPC + hh * 512:
                                         k * TPC + hh * 512 + 512],
                                    start=(k == 0), stop=(k == 15))
                            mmv = p.cnt["b6mm"] + 1
                            p.inc(ins, "b6mm")

                            @blk.vector
                            def _(v, m=m, ml=ml, hh=hh, mmv=mmv):
                                p.wait_at(v, "b6mm", mmv)
                                p.edge(v, v.tensor_scalar(
                                    t1_6[:, :], ps6_[ml % 2][:, :],
                                    wob_sb[:, m:m + 1], None, ALU.add),
                                    "b6vv")
                                p.inc(v.tensor_add(
                                    v_sb_l[:, m * TPC + hh * 512:
                                           m * TPC + hh * 512 + 512],
                                    t1_6[:, :],
                                    v_sb_l[:, m * TPC + hh * 512:
                                           m * TPC + hh * 512 + 512]),
                                    "b6vv")
                            bank6[f"wo{ml % 2}"] = p.cnt["b6vv"]

            @blk.sync
            def _(sy):
                p.wait(sy, "b6vv")
                p.wait(sy, "b6ss")
                p.wait(sy, "b6mm")
                p.wait(sy, "b6w")
                p.inc(sy.dma_start(out=win_mark0[:, :], in_=cpool_h),
                      "b6w", 16)

        xh3 = p.sbuf("b7xh", [128, 16 * TPC], BF16)
        gate_sb = p.sbuf("b7gate", [128, 16 * 8], BF16)

        @blk.gpsimd
        def _(g):
            p.wait(g, "b6vv")
            p.wait(g, "b6w")
            p.inc(g.dma_start(
                out=gate_sb.ap().rearrange("pp (k m) -> pp k m", k=16),
                in_=gateT_h.rearrange("k pp m -> pp k m")), "ldw", 16)

        ln3, _c3 = emit_layernorm(p, blk, "b7ln", v_sb_l, xh3, TPC,
                                  ["b6vv", "b6w", "ldw"])
        f2, _c4 = emit_ffn(p, blk, "f2", xh3, v_sb_l, w1p_h, w2t_h,
                           gate_sb, TPC, [ln3])

        @blk.sync
        def _(sy):
            p.wait(sy, f2)
            p.inc(sy.dma_start(
                out=outT.rearrange("k pp m -> pp k m"),
                in_=v_sb_l.ap().rearrange("pp (k m) -> pp k m", k=16)),
                "std", 16)
            p.wait(sy, "std")
    return nc


def prep_attn_weights(ins):
    f32 = np.float32
    g = lambda k: np.asarray(ins[k], f32)
    w = {}
    w["wqaT"] = np.ascontiguousarray(g("L0_attn_wq_a_w").T.reshape(16, 128, QLR))
    w["wqab"] = _bias_tiles(g("L0_attn_wq_a_b"), 12)
    w["qnw"] = _bias_tiles(g("L0_attn_q_norm_w"), 12)
    wqb = g("L0_attn_wq_b_w")
    wqbT = np.stack([
        np.ascontiguousarray(wqb[192 * h:192 * h + 192, :].T.reshape(
            12, 128, 192)) for h in range(H)])
    w["wqbT"] = wqbT
    bb = g("L0_attn_wq_b_b")
    t = np.zeros((H, 128, 2), f32)
    for h in range(H):
        t[h, :, 0] = bb[192 * h:192 * h + 128]
        t[h, 0:64, 1] = bb[192 * h + 128:192 * h + 192]
    w["wqbb"] = t
    w["wkvaT"] = np.ascontiguousarray(
        g("L0_attn_wkv_a_w").T.reshape(16, 128, 576))
    w["wkvab"] = _bias_tiles(g("L0_attn_wkv_a_b"), 5)
    w["kvnw"] = _bias_tiles(g("L0_attn_kv_norm_w"), 4)
    wkvb = g("L0_attn_wkv_b_w").reshape(H, 256, KVLR)
    w["wknT"] = np.ascontiguousarray(
        wkvb[:, 0:128, :].transpose(0, 2, 1).reshape(H, 4, 128, 128))
    w["wkvvT"] = np.ascontiguousarray(
        wkvb[:, 128:256, :].transpose(0, 2, 1).reshape(H, 4, 128, 128))
    w["woT"] = np.ascontiguousarray(g("L0_attn_wo_w").T.reshape(16, 128, D))
    w["wob"] = _bias_tiles(g("L0_attn_wo_b"), 16)
    return w


def _rope_tables_T():
    inv = 1.0 / 10000.0 ** (np.arange(0, ROPE, 2, dtype=np.float32) / ROPE)
    t = np.arange(S, dtype=np.float32)[:, None]
    fr = t * inv[None, :]
    fr = np.concatenate([fr, fr], axis=-1)
    return (np.ascontiguousarray(np.cos(fr).T),
            np.ascontiguousarray(np.sin(fr).T))


def _masks():
    q = np.arange(128)[:, None]
    k = np.arange(128)[None, :]
    dmask = np.where(k <= q, 0.0, -1e30).astype(np.float32)
    kk = np.arange(128)[:, None]
    qq = np.arange(512)[None, :]
    km = np.stack([(kk <= qq - 128 * ri).astype(np.float32)
                   for ri in range(4)])
    return dmask, np.ascontiguousarray(km)


def run_launch_B(inputs, u):
    aw = prep_attn_weights(inputs)
    w1p, w2t, gateT = prep_ffn_weights(inputs)
    cpool = _cpool_np()
    ident = np.eye(128, dtype=np.float32)
    cosT, sinT = _rope_tables_T()
    dmask, kmask = _masks()
    uf = np.ascontiguousarray(u.reshape(16, 128, S))
    nc = build_B()
    ims = []
    for c in range(NCORE):
        pad = 7168 - 1024 * c
        meta = np.array([[pad, 4 * pad, 1024 * c, pad // 512,
                          pad // 128, 0, 0, 0]], np.int32)
        ims.append({
            "uf": uf,
            "uo": np.ascontiguousarray(
                u[:, c * TPC:(c + 1) * TPC].reshape(16, 128, TPC)),
            "cpool": cpool, "ident": ident,
            "cosg": cosT, "sing": sinT,
            "coso": np.ascontiguousarray(cosT[:, c * TPC:(c + 1) * TPC]),
            "sino": np.ascontiguousarray(sinT[:, c * TPC:(c + 1) * TPC]),
            "dmask": dmask, "kmask": kmask, "meta": meta,
            "wqaT": aw["wqaT"], "wqab": aw["wqab"], "qnw": aw["qnw"],
            "wqbT": aw["wqbT"], "wqbb": aw["wqbb"],
            "wkvaT": aw["wkvaT"], "wkvab": aw["wkvab"], "kvnw": aw["kvnw"],
            "wknT": aw["wknT"], "wkvvT": aw["wkvvT"],
            "woT": aw["woT"], "wob": aw["wob"],
            "w1p": w1p, "w2t": w2t, "gateT": gateT,
        })
    res = run_bass_kernel_spmd(nc, ims, core_ids=list(range(NCORE)))
    out = np.empty((D, S), np.float32)
    for c in range(NCORE):
        out[:, c * TPC:(c + 1) * TPC] = \
            np.asarray(res.results[c]["outT"]).reshape(D, TPC)
    return out


def kernel(**inputs):
    u = run_launch_A(inputs)
    out = run_launch_B(inputs, u)
    return np.ascontiguousarray(out.T).reshape(1, S, D).astype(np.float32)



# version 09
# speedup vs baseline: 2.0436x; 1.0048x over previous
import math
from contextlib import ExitStack

import numpy as np
import ml_dtypes
import concourse.bass as bass
import concourse.mybir as mybir
from concourse.bass_utils import run_bass_kernel_spmd

F32 = mybir.dt.float32
BF16 = mybir.dt.bfloat16
I32 = mybir.dt.int32
AF = mybir.ActivationFunctionType
ALU = mybir.AluOpType
AX = mybir.AxisListType
BF16NP = ml_dtypes.bfloat16

S, D, H = 8192, 2048, 16
QLR, KVLR = 1536, 512
NOPE, ROPE, VD = 128, 64, 128
QKD = NOPE + ROPE
E, I_FF = 8, 1408
NCORE = 8
TPC = S // NCORE
NH = TPC // 512
SCALE = 1.0 / math.sqrt(QKD)

CP = {"ln_eps": 0, "rms_eps": 1, "neg_half": 2, "inv_d": 3, "inv_qlr": 4,
      "inv_kvlr": 5, "half": 6, "one": 7, "neg_big": 8, "zero": 9,
      "neg_one": 10, "scale": 11}


def _cpool_np():
    v = np.zeros((128, 16), np.float32)
    for k, val in [("ln_eps", 1e-5), ("rms_eps", 1e-6), ("neg_half", -0.5),
                   ("inv_d", 1.0 / D), ("inv_qlr", 1.0 / QLR),
                   ("inv_kvlr", 1.0 / KVLR), ("half", 0.5), ("one", 1.0),
                   ("neg_big", -1e30), ("zero", 0.0), ("neg_one", -1.0),
                   ("scale", SCALE)]:
        v[:, CP[k]] = val
    return v


class P:

    def __init__(self, nc, stack):
        self.nc = nc
        self.stack = stack
        self.en = stack.enter_context
        self.sem = {}
        self.cnt = {}
        self.bank = {}

    def sbuf(self, name, shape, dt):
        return self.en(self.nc.sbuf_tensor(name, shape, dt))

    def new_sem(self, *names):
        for n in names:
            if n not in self.sem:
                self.sem[n] = self.en(self.nc.semaphore(n))
                self.cnt[n] = 0

    def inc(self, instr, name, n=1):
        self.cnt[name] += n
        instr.then_inc(self.sem[name], n)
        return instr

    def wait(self, eng, name):
        if self.cnt[name] > 0:
            eng.wait_ge(self.sem[name], self.cnt[name])

    def wait_at(self, eng, name, v):
        if v > 0:
            eng.wait_ge(self.sem[name], v)

    def bank_wait(self, eng, keys):
        for k in keys:
            if k in self.bank:
                s, v = self.bank[k]
                self.wait_at(eng, s, v)

    def bank_set(self, keys, sem_name):
        for k in keys:
            self.bank[k] = (sem_name, self.cnt[sem_name])

    def edge(self, eng, instr, name):
        self.inc(instr, name)
        eng.wait_ge(self.sem[name], self.cnt[name])

    def cp(self, name):
        return self.cpool[:, CP[name]:CP[name] + 1]


def emit_layernorm(p, blk, tag, in_sb, out_sb, T, ready_sems, sem_tag=None,
                   pss_ext=None):
    nc = p.nc
    nh = T // 512
    st = sem_tag or tag
    with ExitStack() as ph:
        en = ph.enter_context
        sq = en(nc.sbuf_tensor(f"{tag}sq", [128, 16 * T], BF16))
        m_sb = en(nc.sbuf_tensor(f"{tag}m", [128, T], F32))
        v_sb = en(nc.sbuf_tensor(f"{tag}v", [128, T], F32))
        rstd = en(nc.sbuf_tensor(f"{tag}r", [128, T], F32))
        tsc = en(nc.sbuf_tensor(f"{tag}t", [128, T], F32))
        pss = pss_ext if pss_ext is not None else [
            en(nc.psum_tensor(f"{tag}ps{i}", [128, 512], F32))
            for i in range(2 * nh)]
        p.new_sem(f"{st}mm", f"{st}vv", f"{st}ss")
        mm, vv, ss = f"{st}mm", f"{st}vv", f"{st}ss"
        ent = {n: p.cnt[n] for n in (mm, vv, ss)}

        @blk.vector
        def _(v):
            for r in ready_sems:
                p.wait(v, r)
            for n, v_ in ent.items():
                p.wait_at(v, n, v_)
            p.inc(v.tensor_mul(sq[:, :], in_sb[:, :], in_sb[:, :]), ss)

        @blk.tensor
        def _(t):
            for r in ready_sems:
                p.wait(t, r)
            for n, v_ in ent.items():
                p.wait_at(t, n, v_)
            p.wait(t, ss)
            for k in range(16):
                for h in range(nh):
                    t.matmul(pss[h][:, :], p.ones_f32[:, :],
                             in_sb[:, k * T + h * 512: k * T + h * 512 + 512],
                             start=(k == 0), stop=(k == 15))
                    ins = t.matmul(pss[nh + h][:, :], p.ones_bf[:, :],
                                   sq[:, k * T + h * 512: k * T + h * 512 + 512],
                                   start=(k == 0), stop=(k == 15))
            p.inc(ins, mm)

        @blk.vector
        def _(v):
            p.wait(v, mm)
            for h in range(nh):
                ins = v.tensor_scalar(m_sb[:, h * 512:h * 512 + 512],
                                      pss[h][:, :], p.cp("inv_d"), None,
                                      ALU.mult)
                if h < nh - 1:
                    p.edge(v, ins, ss)
            p.inc(ins, ss)

        @blk.vector
        def _(v):
            p.wait(v, ss)
            for n, v_ in ent.items():
                p.wait_at(v, n, v_)
            p.edge(v, v.tensor_mul(tsc[:, :], m_sb[:, :], m_sb[:, :]), vv)
            for h in range(nh):
                ins = v.scalar_tensor_tensor(
                    v_sb[:, h * 512:h * 512 + 512], pss[nh + h][:, :],
                    p.cp("inv_d"), tsc[:, h * 512:h * 512 + 512],
                    ALU.mult, ALU.subtract)
            p.inc(ins, vv)

        @blk.scalar
        def _(s):
            p.wait(s, vv)
            p.edge(s, s.activation(rstd[:, :], v_sb[:, :], AF.Ln,
                                   bias=p.cp("ln_eps")), ss)
            p.inc(s.activation(rstd[:, :], rstd[:, :], AF.Exp,
                               scale=p.cp("neg_half")), ss)

        @blk.vector
        def _(v):
            p.wait(v, ss)
            for k in range(16):
                p.edge(v, v.tensor_sub(tsc[:, :], in_sb[:, k * T:(k + 1) * T],
                                       m_sb[:, :]), vv)
                ins = v.tensor_mul(out_sb[:, k * T:(k + 1) * T], tsc[:, :],
                                   rstd[:, :])
                if k == 15:
                    p.inc(ins, vv)
                else:
                    p.edge(v, ins, vv)
        done = p.cnt[vv]
    return vv, done


def emit_ffn(p, blk, tag, xh_sb, resid_sb, w1p_ap, w2t_ap, gate_sb, T,
             ready_sems, dbg=None):
    nc = p.nc
    nh = T // 512
    ntc = T // 128
    with ExitStack() as ph:
        en = ph.enter_context
        act = en(nc.sbuf_tensor(f"{tag}act", [128, 11 * T], BF16))
        sg2 = [en(nc.sbuf_tensor(f"{tag}sg0", [128, T], F32))] * 2
        ucs = en(nc.sbuf_tensor(f"{tag}uc", [128, T], F32))
        lg = en(nc.sbuf_tensor(f"{tag}lg", [8, T], F32))
        lt = en(nc.sbuf_tensor(f"{tag}lt", [128, 8], F32))
        ltd = en(nc.sbuf_tensor(f"{tag}ltd", [128, 8], F32))
        lsel = en(nc.sbuf_tensor(f"{tag}lsel", [128, 8], F32))
        lmsk = en(nc.sbuf_tensor(f"{tag}lmsk", [128, 8], F32))
        nm1 = en(nc.sbuf_tensor(f"{tag}nm1", [128, 1], F32))
        m2s = en(nc.sbuf_tensor(f"{tag}m2", [128, 1], F32))
        lz = en(nc.sbuf_tensor(f"{tag}lz", [128, 1], F32))
        cmbT = en(nc.sbuf_tensor(f"{tag}cmbT", [8, 128], F32))
        cTb = [en(nc.sbuf_tensor(f"{tag}cT{i}", [1, T], BF16))
               for i in range(2)]
        c_bc = en(nc.sbuf_tensor(f"{tag}cbc", [128, T], F32))
        w1b = [en(nc.sbuf_tensor(f"{tag}w1{i}", [128, 16 * 256], BF16))
               for i in range(2)]
        w2b = [en(nc.sbuf_tensor(f"{tag}w2{i}", [128, 11 * 256], BF16))
               for i in range(2)]
        ps = [en(nc.psum_tensor(f"{tag}ps{i}", [128, 512], F32))
              for i in range(8)]
        cT_d = nc.dram_tensor(f"{tag}cTd", [E, T], F32, kind="Internal")

        p.new_sem(f"{tag}mm", f"{tag}vv", f"{tag}ss", f"{tag}w1", f"{tag}w2",
                  f"{tag}af", f"{tag}cd", f"{tag}ct")
        mm, vv, ss = f"{tag}mm", f"{tag}vv", f"{tag}ss"
        w1s, w2s, afs, cds = f"{tag}w1", f"{tag}w2", f"{tag}af", f"{tag}cd"
        cts = f"{tag}ct"

        @blk.tensor
        def _(t):
            for r in ready_sems:
                p.wait(t, r)
            p.bank_wait(t, ["ps6", "ps7"])
            for h in range(nh):
                for k in range(16):
                    ins = t.matmul(
                        ps[6 + h][0:8, :], gate_sb[:, k * 8:k * 8 + 8],
                        xh_sb[:, k * T + h * 512: k * T + h * 512 + 512],
                        start=(k == 0), stop=(k == 15))
            p.inc(ins, mm)

        @blk.vector
        def _(v):
            p.wait(v, mm)
            for h in range(nh):
                ins = v.tensor_copy(lg[0:8, h * 512:h * 512 + 512],
                                    ps[6 + h][0:8, :])
                if h < nh - 1:
                    p.edge(v, ins, ss)
            p.inc(ins, ss)
            p.bank_set(["ps6", "ps7"], ss)

        if dbg is not None:
            @blk.sync
            def _(sy):
                p.wait(sy, ss)
                p.inc(sy.dma_start(out=dbg["xh"], in_=xh_sb[:, :]), "std", 16)
                p.inc(sy.dma_start(out=dbg["lg"], in_=lg[0:8, :]), "std", 16)

        for tc in range(ntc):
            @blk.tensor
            def _(t):
                p.wait(t, ss)
                p.bank_wait(t, ["ps6"])
                p.inc(t.transpose(ps[6][:, 0:8],
                                  lg[0:8, tc * 128:(tc + 1) * 128],
                                  p.ident_f32[0:8, 0:8]), mm)

            @blk.vector
            def _(v):
                p.wait(v, mm)
                p.wait(v, ss)
                p.edge(v, v.tensor_copy(lt[:, :], ps[6][:, 0:8]), vv)
                p.edge(v, v.reduce_max(nm1[:, :], lt[:, :], axis=AX.X,
                                       negate=True), vv)
                p.bank_set(["ps6"], vv)
                p.edge(v, v.tensor_scalar(ltd[:, :], lt[:, :], nm1[:, :],
                                          None, ALU.add), vv)
                p.edge(v, v.tensor_scalar(lmsk[:, :], ltd[:, :], p.cp("zero"),
                                          None, ALU.is_equal), vv)
                p.edge(v, v.scalar_tensor_tensor(lmsk[:, :], lmsk[:, :],
                                                 p.cp("neg_big"), lt[:, :],
                                                 ALU.mult, ALU.add), vv)
                p.edge(v, v.reduce_max(m2s[:, :], lmsk[:, :], axis=AX.X), vv)
                p.inc(v.tensor_scalar(lsel[:, :], lt[:, :], m2s[:, :], None,
                                      ALU.is_ge), vv)

            @blk.scalar
            def _(s):
                p.wait(s, vv)
                p.inc(s.activation(ltd[:, :], lt[:, :], AF.Exp,
                                   bias=nm1[:, :]), ss)

            @blk.vector
            def _(v):
                p.wait(v, ss)
                p.edge(v, v.tensor_mul(ltd[:, :], ltd[:, :], lsel[:, :]), vv)
                p.edge(v, v.reduce_sum(lz[:, :], ltd[:, :], axis=AX.X), vv)
                p.edge(v, v.reciprocal(lz[:, :], lz[:, :]), vv)
                p.inc(v.tensor_scalar(ltd[:, :], ltd[:, :], lz[:, :],
                                      p.cp("half"), ALU.mult, ALU.mult), vv)

            @blk.tensor
            def _(t):
                p.wait(t, vv)
                p.bank_wait(t, ["ps6"])
                p.inc(t.transpose(ps[6][0:8, 0:128], ltd[:, :],
                                  p.ident_f32[:, :]), mm)

            @blk.vector
            def _(v):
                p.wait(v, mm)
                p.bank_wait(v, ["cmbT"])
                p.inc(v.tensor_copy(cmbT[:, :], ps[6][0:8, 0:128]), ss)
                p.bank_set(["ps6"], ss)

            @blk.sync
            def _(sy):
                p.wait(sy, ss)
                p.inc(sy.dma_start(out=cT_d.ap()[:, tc * 128:(tc + 1) * 128],
                                   in_=cmbT[:, :]), cds, 16)
                p.bank_set(["cmbT"], cds)


        if dbg is not None:
            dmisc = en(nc.sbuf_tensor(f"{tag}dmisc", [128, 256], F32))

            @blk.vector
            def _(v):
                p.wait(v, cts)
                v.memset(dmisc[:, :], 0.0)
                v.tensor_copy(dmisc[0:8, 0:128], cmbT[:, :])
                v.tensor_copy(dmisc[:, 128:136], ltd[:, :])
                v.tensor_copy(dmisc[:, 136:137], nm1[:, :])
                v.tensor_copy(dmisc[:, 137:138], lz[:, :])
                v.tensor_copy(dmisc[:, 138:146], lt[:, :])
                v.tensor_copy(dmisc[:, 146:154], lsel[:, :])
                p.inc(v.tensor_copy(dmisc[0:1, 160:160 + 64],
                                    cTb[1][0:1, 0:64]), vv)

            @blk.sync
            def _(sy):
                p.wait(sy, vv)
                p.inc(sy.dma_start(out=dbg["misc"], in_=dmisc[:, :]),
                      "std", 16)

        NP_SL = [1] * 11
        w1_free = []
        w2_free = []

        bcast_mm = {}
        for e in range(E):
            @blk.gpsimd
            def _(g, e=e):
                p.wait(g, cds)
                if e - 2 in bcast_mm:
                    p.wait_at(g, mm, bcast_mm[e - 2])
                p.inc(g.dma_start(out=cTb[e % 2][0:1, :],
                                  in_=cT_d.ap()[e:e + 1, :]), cts, 16)
            ct_need = p.cnt[cts]

            @blk.tensor
            def _(t, e=e, ct_need=ct_need):
                p.wait_at(t, cts, ct_need)
                p.bank_wait(t, ["ps6", "ps7"])
                for h in range(nh):
                    p.inc(t.matmul(
                        ps[6 + h][:, :], p.ones_col_bf[0:1, :],
                        cTb[e % 2][0:1, h * 512:h * 512 + 512],
                        start=True, stop=True), mm)
            bcast_mm[e] = p.cnt[mm]

            @blk.vector
            def _(v):
                p.wait(v, mm)
                p.bank_wait(v, ["cbc"])
                for h in range(nh):
                    ins = v.tensor_copy(c_bc[:, h * 512:h * 512 + 512],
                                        ps[6 + h][:, :])
                    if h < nh - 1:
                        p.edge(v, ins, ss)
                p.inc(ins, ss)
                p.bank_set(["ps6", "ps7"], ss)

            cbc_ready = p.cnt[ss]
            silu_jobs = []
            ep_afs = {}
            ss_sil0 = p.cnt[ss]

            pair0 = 0
            for sli, npair in enumerate(NP_SL):
                gslice = e * len(NP_SL) + sli
                buf = gslice % 2

                @blk.gpsimd
                def _(g):
                    if gslice >= 2:
                        sname, v = w1_free[gslice - 2]
                        p.wait_at(g, sname, v)
                    else:
                        for r in ready_sems:
                            p.wait(g, r)
                    cols = npair * 256
                    p.inc(g.dma_start(
                        out=w1b[buf].ap().rearrange(
                            "pp (k m) -> pp k m", k=16)[:, :, 0:cols],
                        in_=w1p_ap[e, :, :,
                                   pair0 * 256:pair0 * 256 + cols].rearrange(
                            "k pp m -> pp k m")), w1s, 16)
                w1_need = p.cnt[w1s]

                for pl in range(npair):
                    gp = pair0 + pl
                    grp = gp % 2
                    bks = [f"ps{4 * grp + j}" for j in range(4)]
                    bg = [ps[4 * grp + j] for j in range(4)]

                    @blk.tensor
                    def _(t):
                        p.wait_at(t, w1s, w1_need)
                        p.wait_at(t, ss, cbc_ready)
                        p.bank_wait(t, bks)
                        W = 256
                        for k in range(16):
                            for hh in range(nh):
                                t.matmul(bg[hh][:, :],
                                         w1b[buf][:, k * W + pl * 256:
                                                  k * W + pl * 256 + 128],
                                         xh_sb[:, k * T + hh * 512:
                                               k * T + hh * 512 + 512],
                                         start=(k == 0), stop=(k == 15))
                                ins = t.matmul(
                                    bg[2 + hh][:, :],
                                    w1b[buf][:, k * W + pl * 256 + 128:
                                             k * W + pl * 256 + 256],
                                    xh_sb[:, k * T + hh * 512:
                                          k * T + hh * 512 + 512],
                                    start=(k == 0), stop=(k == 15))
                        p.inc(ins, mm)
                    mm_grp = p.cnt[mm]

                    ss_grp = ss_sil0 + gp + 1
                    silu_jobs.append((mm_grp, gp, bg, bks))

                    @blk.vector
                    def _(v, mm_grp=mm_grp, ss_grp=ss_grp, gp=gp, bg=bg):
                        p.wait_at(v, mm, mm_grp)
                        p.wait_at(v, ss, ss_grp)
                        sgb = sg2[gp % 2]
                        for hh in range(nh):
                            p.edge(v, v.tensor_mul(
                                ucs[:, hh * 512:hh * 512 + 512],
                                bg[2 + hh][:, :],
                                c_bc[:, hh * 512:hh * 512 + 512]), vv)
                            ins = v.tensor_mul(
                                act[:, gp * T + hh * 512:
                                    gp * T + hh * 512 + 512],
                                sgb[:, hh * 512:hh * 512 + 512],
                                ucs[:, hh * 512:hh * 512 + 512])
                            if hh < nh - 1:
                                p.edge(v, ins, vv)
                        p.inc(ins, afs)
                    p.bank_set(bks, afs)
                    ep_afs[gp] = p.cnt[afs]
                pair0 += npair
                w1_free.append((afs, p.cnt[afs]))

            @blk.scalar
            def _(s, silu_jobs=list(silu_jobs), e=e):
                for mm_grp, gp, bg, bks in silu_jobs:
                    p.wait_at(s, mm, mm_grp)
                    if gp - 1 in ep_afs:
                        p.wait_at(s, afs, ep_afs[gp - 1])
                    sgb = sg2[gp % 2]
                    for hh in range(nh):
                        ins = s.activation(sgb[:, hh * 512:hh * 512 + 512],
                                           bg[hh][:, :], AF.Silu)
                    p.inc(ins, ss)
            assert p.cnt[ss] == ss_sil0 + 11

            act_all = p.cnt[afs]

            for q in range(8):
                gq = e * 8 + q
                buf2 = gq % 2

                @blk.gpsimd
                def _(g):
                    if gq >= 2:
                        sname, v = w2_free[gq - 2]
                        p.wait_at(g, sname, v)
                    else:
                        for r in ready_sems:
                            p.wait(g, r)
                    p.inc(g.dma_start(
                        out=w2b[buf2].ap().rearrange(
                            "pp (k m) -> pp k m", k=11),
                        in_=w2t_ap[e, :, :, q * 256:q * 256 + 256].rearrange(
                            "k pp m -> pp k m")), w2s, 16)
                w2_need = p.cnt[w2s]

                for m2l in range(2):
                    m2 = q * 2 + m2l
                    slot = m2l % 2
                    bks = [f"ps{slot * 2}", f"ps{slot * 2 + 1}"]
                    bk = [ps[slot * 2], ps[slot * 2 + 1]]

                    @blk.tensor
                    def _(t):
                        p.wait_at(t, w2s, w2_need)
                        p.wait_at(t, afs, act_all)
                        p.bank_wait(t, bks)
                        for k2 in range(11):
                            for hh in range(nh):
                                ins = t.matmul(
                                    bk[hh][:, :],
                                    w2b[buf2][:, k2 * 256 + m2l * 128:
                                              k2 * 256 + m2l * 128 + 128],
                                    act[:, k2 * T + hh * 512:
                                        k2 * T + hh * 512 + 512],
                                    start=(k2 == 0), stop=(k2 == 10))
                        p.inc(ins, mm)
                    mm_w2 = p.cnt[mm]

                    @blk.vector
                    def _(v):
                        p.wait_at(v, mm, mm_w2)
                        p.bank_wait(v, [f"res{m2}"])
                        for hh in range(nh):
                            ins = v.tensor_add(
                                resid_sb[:, m2 * T + hh * 512:
                                         m2 * T + hh * 512 + 512],
                                bk[hh][:, :],
                                resid_sb[:, m2 * T + hh * 512:
                                         m2 * T + hh * 512 + 512])
                            if hh < nh - 1:
                                p.edge(v, ins, vv)
                        p.inc(ins, vv)
                    p.bank_set(bks, vv)
                    p.bank_set([f"res{m2}"], vv)
                w2_free.append((mm, p.cnt[mm]))

            p.bank_set(["cbc"], afs)
        if dbg is not None:
            @blk.sync
            def _(sy):
                p.wait(sy, afs)
                p.inc(sy.dma_start(out=dbg["cbc"], in_=c_bc[:, :]), "std", 16)
                p.inc(sy.dma_start(out=dbg["sg"], in_=sg[:, :]), "std", 16)
                p.inc(sy.dma_start(out=dbg["ucs"], in_=ucs[:, :]), "std", 16)

            @blk.vector
            def _(v):
                p.wait(v, afs)
                p.inc(v.tensor_copy(sg[:, :], act[:, 10 * T:11 * T]), vv)

            @blk.sync
            def _(sy):
                p.wait(sy, vv)
                p.wait(sy, "std")
                p.inc(sy.dma_start(out=dbg["act"], in_=sg[:, :]),
                      "std", 16)
        done = p.cnt[vv]
    return vv, done


def prep_ffn_weights(ins):
    f32 = np.float32
    w1 = np.asarray(ins["L0_ffn_w1"], f32)
    w1t = w1.transpose(0, 2, 1)
    idx = np.empty(2816, np.int64)
    for i in range(11):
        idx[i * 256:i * 256 + 128] = np.arange(i * 128, (i + 1) * 128)
        idx[i * 256 + 128:(i + 1) * 256] = np.arange(1408 + i * 128,
                                                     1408 + (i + 1) * 128)
    w1p = np.ascontiguousarray(w1t[:, :, idx].reshape(E, 16, 128, 2816))
    w2 = np.asarray(ins["L0_ffn_w2"], f32)
    w2t = np.ascontiguousarray(
        w2.transpose(0, 2, 1).reshape(E, 11, 128, 2048))
    gw = np.asarray(ins["L0_ffn_gate_w"], f32)
    gateT = np.ascontiguousarray(gw.T.reshape(16, 128, 8))
    return w1p, w2t, gateT


def _bias_tiles(b, nchunk):
    b = np.asarray(b, np.float32)
    padded = np.zeros(nchunk * 128, np.float32)
    padded[:b.shape[0]] = b
    return np.ascontiguousarray(padded.reshape(nchunk, 128).T)


def build_A(debug=False):
    nc = bass.Bass("TRN2", target_bir_lowering=False)
    xT = nc.dram_tensor("xT", [16, 128, TPC], F32, kind="ExternalInput").ap()
    cpool_h = nc.dram_tensor("cpool", [128, 16], F32,
                             kind="ExternalInput").ap()
    ident_h = nc.dram_tensor("ident", [128, 128], F32,
                             kind="ExternalInput").ap()
    w1p = nc.dram_tensor("w1p", [E, 16, 128, 2816], F32,
                         kind="ExternalInput").ap()
    w2t = nc.dram_tensor("w2t", [E, 11, 128, 2048], F32,
                         kind="ExternalInput").ap()
    gateT = nc.dram_tensor("gateT", [16, 128, 8], F32,
                           kind="ExternalInput").ap()
    uT = nc.dram_tensor("uT", [16, 128, TPC], F32, kind="ExternalOutput").ap()
    if debug:
        dbg_xh = nc.dram_tensor("dbg_xh", [128, 16 * TPC], BF16,
                            kind="ExternalOutput").ap()
        dbg_lg = nc.dram_tensor("dbg_lg", [8, TPC], F32,
                                kind="ExternalOutput").ap()
        dbg_cT = nc.dram_tensor("dbg_cT", [1, E * TPC], F32,
                                kind="ExternalOutput").ap()
        dbg_act = nc.dram_tensor("dbg_act", [128, TPC], F32,
                                 kind="ExternalOutput").ap()
        dbg_cbc = nc.dram_tensor("dbg_cbc", [128, TPC], F32,
                                 kind="ExternalOutput").ap()
        dbg_sg = nc.dram_tensor("dbg_sg", [128, TPC], F32,
                                kind="ExternalOutput").ap()
        dbg_ucs = nc.dram_tensor("dbg_ucs", [128, TPC], F32,
                                 kind="ExternalOutput").ap()
        dbg_misc = nc.dram_tensor("dbg_misc", [128, 256], F32,
                                  kind="ExternalOutput").ap()

    with ExitStack() as stack:
        p = P(nc, stack)
        p.cpool = p.sbuf("cpool_sb", [128, 16], F32)
        p.ident_f32 = p.sbuf("ident_sb", [128, 128], F32)
        p.ones_f32 = p.sbuf("ones_f32", [128, 128], F32)
        p.ones_bf = p.sbuf("ones_bf", [128, 128], BF16)
        p.ones_col = p.sbuf("ones_col", [1, 128], F32)
        p.ones_col_bf = p.sbuf("ones_col_bf", [1, 128], BF16)
        x_sb = p.sbuf("x_sb", [128, 16 * TPC], F32)
        xh_sb = p.sbuf("xh_sb", [128, 16 * TPC], BF16)
        gate_sb = p.sbuf("gate_sb", [128, 16 * 8], BF16)
        p.new_sem("ldx", "ldw", "std", "crdy")
        blk = p.en(nc.Block())

        @blk.sync
        def _(sy):
            p.inc(sy.dma_start(out=p.cpool[:, :], in_=cpool_h), "ldx", 16)
            p.inc(sy.dma_start(out=p.ident_f32[:, :], in_=ident_h), "ldx", 16)
            p.inc(sy.dma_start(
                out=x_sb.ap().rearrange("pp (k m) -> pp k m", k=16),
                in_=xT.rearrange("k pp m -> pp k m")), "ldx", 16)

        @blk.gpsimd
        def _(g):
            p.inc(g.dma_start(
                out=gate_sb.ap().rearrange("pp (k m) -> pp k m", k=16),
                in_=gateT.rearrange("k pp m -> pp k m")), "ldw", 16)

        @blk.vector
        def _(v):
            p.wait(v, "ldx")
            p.wait(v, "ldw")
            v.memset(p.ones_f32[:, :], 1.0)
            v.memset(p.ones_bf[:, :], 1.0)
            v.memset(p.ones_col[:, :], 1.0)
            p.inc(v.memset(p.ones_col_bf[:, :], 1.0), "crdy")

        lnsem, _c = emit_layernorm(p, blk, "ln1", x_sb, xh_sb, TPC, ["crdy"])
        dbg = None
        if debug:
            dbg = {"xh": dbg_xh, "lg": dbg_lg, "cT": dbg_cT, "act": dbg_act,
                   "cbc": dbg_cbc, "sg": dbg_sg, "ucs": dbg_ucs,
                   "misc": dbg_misc}
        fsem, _c2 = emit_ffn(p, blk, "f1", xh_sb, x_sb, w1p, w2t, gate_sb,
                             TPC, [lnsem], dbg=dbg)

        @blk.sync
        def _(sy):
            p.wait(sy, fsem)
            p.inc(sy.dma_start(
                out=uT.rearrange("k pp m -> pp k m"),
                in_=x_sb.ap().rearrange("pp (k m) -> pp k m", k=16)),
                "std", 16)
            p.wait(sy, "std")
    return nc


def host_prep_common(inputs):
    x = np.asarray(inputs["L0_attn_x"], np.float32).reshape(S, D)
    xT = np.ascontiguousarray(x.T)
    w1p, w2t, gateT = prep_ffn_weights(inputs)
    cpool = _cpool_np()
    ident = np.eye(128, dtype=np.float32)
    return x, xT, w1p, w2t, gateT, cpool, ident


def run_launch_A(inputs):
    x, xT, w1p, w2t, gateT, cpool, ident = host_prep_common(inputs)
    nc = build_A()
    ims = []
    for c in range(NCORE):
        sl = xT[:, c * TPC:(c + 1) * TPC]
        ims.append({
            "xT": np.ascontiguousarray(sl.reshape(16, 128, TPC)),
            "cpool": cpool, "ident": ident,
            "w1p": w1p, "w2t": w2t, "gateT": gateT,
        })
    res = run_bass_kernel_spmd(nc, ims, core_ids=list(range(NCORE)))
    u = np.empty((D, S), np.float32)
    for c in range(NCORE):
        u[:, c * TPC:(c + 1) * TPC] = \
            np.asarray(res.results[c]["uT"]).reshape(D, TPC)
    return u




NKT = 64
DIAG_T = {0: list(range(56, 60)), 1: list(range(60, 64))}
LAST_T = {0: 60, 1: 64}


def build_B():
    nc = bass.Bass("TRN2", target_bir_lowering=False)
    ap = lambda *a, **k: nc.dram_tensor(*a, **k).ap()
    uf = ap("uf", [16, 128, S], F32, kind="ExternalInput")
    uo = ap("uo", [16, 128, TPC], F32, kind="ExternalInput")
    cpool_h = ap("cpool", [128, 16], F32, kind="ExternalInput")
    ident_h = ap("ident", [128, 128], F32, kind="ExternalInput")
    cosg_h = ap("cosg", [64, S], F32, kind="ExternalInput")
    sing_h = ap("sing", [64, S], F32, kind="ExternalInput")
    coso_h = ap("coso", [64, TPC], F32, kind="ExternalInput")
    sino_h = ap("sino", [64, TPC], F32, kind="ExternalInput")
    dmask_h = ap("dmask", [128, 128], F32, kind="ExternalInput")
    kmask_h = ap("kmask", [4, 128, 512], F32, kind="ExternalInput")
    meta_h = ap("meta", [1, 8], I32, kind="ExternalInput")
    wqaT_h = ap("wqaT", [16, 128, QLR], F32, kind="ExternalInput")
    wqab_h = ap("wqab", [128, 12], F32, kind="ExternalInput")
    qnw_h = ap("qnw", [128, 12], F32, kind="ExternalInput")
    wqbT_h = ap("wqbT", [H, 12, 128, 192], F32, kind="ExternalInput")
    wqbb_h = ap("wqbb", [H, 128, 2], F32, kind="ExternalInput")
    wkvaT_h = ap("wkvaT", [16, 128, 576], F32, kind="ExternalInput")
    wkvab_h = ap("wkvab", [128, 5], F32, kind="ExternalInput")
    kvnw_h = ap("kvnw", [128, 4], F32, kind="ExternalInput")
    wknT_h = ap("wknT", [H, 4, 128, 128], F32, kind="ExternalInput")
    wkvvT_h = ap("wkvvT", [H, 4, 128, 128], F32, kind="ExternalInput")
    woT_h = ap("woT", [16, 128, D], F32, kind="ExternalInput")
    wob_h = ap("wob", [128, 16], F32, kind="ExternalInput")
    w1p_h = ap("w1p", [E, 16, 128, 2816], F32, kind="ExternalInput")
    w2t_h = ap("w2t", [E, 11, 128, 2048], F32, kind="ExternalInput")
    gateT_h = ap("gateT", [16, 128, 8], F32, kind="ExternalInput")
    outT = ap("outT", [16, 128, TPC], F32, kind="ExternalOutput")

    kvfm_d = nc.dram_tensor("kvfm_d", [128, 4 * 2 * S], BF16, kind="Internal")
    kvfm_g = nc.dram_tensor("kvfm_g", [128, 4 * S], BF16, kind="Internal")
    kpe_g = nc.dram_tensor("kpe_g", [66, S], BF16, kind="Internal")
    zrow_d = nc.dram_tensor("zrow_d", [1, S], BF16, kind="Internal")
    xh_d = nc.dram_tensor("xh_d", [128, 16 * S], BF16, kind="Internal")
    kpe_d = nc.dram_tensor("kpe_d", [66, 2 * S], BF16, kind="Internal")
    qn_d = nc.dram_tensor("qn_d", [H, 128, TPC], BF16, kind="Internal")
    qaug_d = nc.dram_tensor("qaug_d", [H, 66, TPC], BF16, kind="Internal")
    mrow_d = nc.dram_tensor("mrow_d", [8, 128], BF16, kind="Internal")

    with ExitStack() as stack:
        p = P(nc, stack)
        p.cpool = p.sbuf("cpool_sb", [128, 16], F32)
        p.ident_f32 = p.sbuf("ident_sb", [128, 128], F32)
        p.ident_bf = p.sbuf("identb_sb", [128, 128], BF16)
        p.ones_f32 = p.sbuf("ones_f32", [128, 128], F32)
        p.ones_bf = p.sbuf("ones_bf", [128, 128], BF16)
        p.ones_col = p.sbuf("ones_col", [1, 128], F32)
        p.ones_col_bf = p.sbuf("ones_col_bf", [1, 128], BF16)
        ones_row = p.sbuf("ones_row", [1, 512], BF16)
        zeros_row = p.sbuf("zeros_row", [1, 512], BF16)
        wob_sb = p.sbuf("wob_sb", [128, 16], F32)
        p.meta_sb = p.sbuf("meta_sb", [1, 8], I32)
        win_mark0 = p.sbuf("win_mark0", [128, 16], F32)
        p.new_sem("ldx", "ldw", "std", "crdy", "kvd", "win", "qpk", "odw")
        blk = p.en(nc.Block())

        @blk.sync
        def _(sy):
            p.inc(sy.dma_start(out=p.cpool[:, :], in_=cpool_h), "ldx", 16)
            p.inc(sy.dma_start(out=p.ident_f32[:, :], in_=ident_h), "ldx", 16)
            p.inc(sy.dma_start(out=wob_sb[:, :], in_=wob_h), "ldx", 16)
            p.inc(sy.dma_start(out=p.meta_sb[:, :], in_=meta_h), "ldx", 16)

        @blk.gpsimd
        def _(g):
            p.inc(g.dma_start(out=p.ident_bf[:, :], in_=ident_h), "ldw", 16)

        @blk.vector
        def _(v):
            p.wait(v, "ldx")
            p.wait(v, "ldw")
            v.memset(p.ones_f32[:, :], 1.0)
            v.memset(p.ones_bf[:, :], 1.0)
            v.memset(p.ones_col[:, :], 1.0)
            v.memset(p.ones_col_bf[:, :], 1.0)
            v.memset(zeros_row[:, :], 0.0)
            p.inc(v.memset(ones_row[:, :], 1.0), "crdy")

        negbig_row = p.sbuf("negbig_row", [1, 512], BF16)

        @blk.vector
        def _(v):
            p.wait(v, "crdy")
            p.inc(v.memset(negbig_row[:, :], -30000.0), "crdy")

        @blk.sync
        def _(sy):
            p.wait(sy, "crdy")
            for g_ in range(16):
                p.inc(sy.dma_start(
                    out=kpe_d.ap()[64:65, g_ * 512:(g_ + 1) * 512],
                    in_=ones_row[0:1, :]), "kvd", 16)
                p.inc(sy.dma_start(
                    out=kpe_d.ap()[65:66, g_ * 512:(g_ + 1) * 512],
                    in_=negbig_row[0:1, :]), "kvd", 16)

        with ExitStack() as b2:
            en2 = b2.enter_context
            ut = [en2(nc.sbuf_tensor("b2ut0", [128, 16 * 512], F32))]
            xh2 = en2(nc.sbuf_tensor("b2xh", [128, 16 * 512], BF16))
            cosg = en2(nc.sbuf_tensor("b2cos", [64, 512], F32))
            sing = en2(nc.sbuf_tensor("b2sin", [64, 512], F32))
            wkva = en2(nc.sbuf_tensor("b2wkva", [128, 16 * 576], BF16))
            wkvab = en2(nc.sbuf_tensor("b2wkvab", [128, 5], F32))
            kvnw = en2(nc.sbuf_tensor("b2kvnw", [128, 4], F32))
            kvt = en2(nc.sbuf_tensor("b2kvt", [128, 4 * 512], F32))
            sqkv = en2(nc.sbuf_tensor("b2sq", [128, 4 * 512], BF16))
            rstd2 = en2(nc.sbuf_tensor("b2rstd", [128, 512], F32))
            t1b = en2(nc.sbuf_tensor("b2t1", [128, 512], F32))
            kvbf = en2(nc.sbuf_tensor("b2kvbf", [128, 4 * 512], BF16))
            kperaw = en2(nc.sbuf_tensor("b2kpr", [64, 512], F32))
            kperot = en2(nc.sbuf_tensor("b2krt", [64, 512], F32))
            kpea = en2(nc.sbuf_tensor("b2kpa", [64, 512], F32))
            kpebf = en2(nc.sbuf_tensor("b2kpb", [64, 512], BF16))
            psb = [en2(nc.psum_tensor(f"b2ps{i}", [128, 512], F32))
                   for i in range(6)]
            zfill = en2(nc.sbuf_tensor("b2zf", [128, 2048], BF16))
            p.new_sem("b2mm", "b2vv", "b2ss", "b2ld")

            @blk.vector
            def _(v):
                p.wait(v, "crdy")
                p.inc(v.memset(zfill[:, :], 0.0), "crdy")

            @blk.sync
            def _(sy):
                p.wait(sy, "crdy")
                p.inc(sy.dma_start(out=wkvab[:, :], in_=wkvab_h), "ldx", 16)
                p.inc(sy.dma_start(out=kvnw[:, :], in_=kvnw_h), "ldx", 16)
                for m in range(4):
                    for cc_ in range(4):
                        p.inc(sy.dma_start(
                            out=kvfm_d.ap()[:, m * 2 * S + cc_ * 2048:
                                            m * 2 * S + (cc_ + 1) * 2048],
                            in_=zfill[:, :]), "kvd", 16)
                for cc_ in range(16):
                    p.inc(sy.dma_start(
                        out=kpe_d.ap()[0:64, cc_ * 512:(cc_ + 1) * 512],
                        in_=zfill[0:64, 0:512]), "kvd", 16)

            @blk.gpsimd
            def _(g):
                p.wait(g, "crdy")
                p.inc(g.dma_start(
                    out=wkva.ap().rearrange("pp (k m) -> pp k m", k=16),
                    in_=wkvaT_h.rearrange("k pp m -> pp k m")), "b2ld", 16)

            for g_ in range(16):
                gb = g_ % 2
                ent = {n: p.cnt[n] for n in
                       ("b2mm", "b2vv", "b2ss", "kvd")}

                @blk.sync
                def _(sy, g_=g_, gb=gb, ent=ent):
                    p.wait_at(sy, "b2vv", ent["b2vv"])
                    p.wait_at(sy, "b2mm", ent["b2mm"])
                    p.inc(sy.dma_start(
                        out=ut[0].ap().rearrange("pp (k m) -> pp k m", k=16),
                        in_=uf[:, :, g_ * 512:(g_ + 1) * 512].rearrange(
                            "k pp m -> pp k m")), "ldx", 16)
                    p.inc(sy.dma_start(
                        out=cosg[:, :],
                        in_=cosg_h[:, g_ * 512:(g_ + 1) * 512]), "ldx", 16)
                    p.inc(sy.dma_start(
                        out=sing[:, :],
                        in_=sing_h[:, g_ * 512:(g_ + 1) * 512]), "ldx", 16)

                lnr, _cc = emit_layernorm(p, blk, f"b2l{g_}", ut[0], xh2,
                                          512,
                                          ["ldx", "b2ld", "b2ss", "b2vv"],
                                          sem_tag="b2l",
                                          pss_ext=[psb[4], psb[5]])

                @blk.tensor
                def _(t, g_=g_, ent=ent):
                    p.wait(t, lnr)
                    p.wait_at(t, "b2ss", ent["b2ss"])
                    p.wait_at(t, "b2vv", ent["b2vv"])
                    p.wait_at(t, "kvd", ent["kvd"])
                    for m in range(5):
                        w = 128 if m < 4 else 64
                        for k in range(16):
                            ins = t.matmul(
                                psb[m][0:w, :],
                                wkva[:, k * 576 + m * 128:k * 576 + m * 128 + w],
                                xh2[:, k * 512:(k + 1) * 512],
                                start=(k == 0), stop=(k == 15))
                        mmv = p.cnt["b2mm"] + 1
                        p.inc(ins, "b2mm")

                        @blk.vector
                        def _(v, m=m, w=w, mmv=mmv):
                            p.wait_at(v, "b2mm", mmv)
                            if m < 4:
                                p.edge(v, v.tensor_scalar(
                                    kvt[:, m * 512:(m + 1) * 512],
                                    psb[m][:, :], wkvab[:, m:m + 1],
                                    None, ALU.add), "b2ss")
                                p.inc(v.tensor_mul(
                                    sqkv[:, m * 512:(m + 1) * 512],
                                    kvt[:, m * 512:(m + 1) * 512],
                                    kvt[:, m * 512:(m + 1) * 512]), "b2ss")
                            else:
                                p.inc(v.tensor_scalar(
                                    kperaw[:, :], psb[4][0:64, :],
                                    wkvab[0:64, 4:5], None, ALU.add), "b2ss")

                @blk.tensor
                def _(t, g_=g_):
                    p.wait(t, "b2ss")
                    for m in range(4):
                        ins = t.matmul(psb[5][:, :], p.ones_bf[:, :],
                                       sqkv[:, m * 512:(m + 1) * 512],
                                       start=(m == 0), stop=(m == 3))
                    p.inc(ins, "b2mm")

                @blk.scalar
                def _(s, g_=g_):
                    p.wait(s, "b2mm")
                    p.edge(s, s.activation(rstd2[:, :], psb[5][:, :], AF.Ln,
                                           bias=p.cp("rms_eps"),
                                           scale=p.cp("inv_kvlr")), "b2ss")
                    p.inc(s.activation(rstd2[:, :], rstd2[:, :], AF.Exp,
                                       scale=p.cp("neg_half")), "b2ss")

                @blk.vector
                def _(v, g_=g_):
                    p.wait(v, "b2ss")
                    for m in range(4):
                        p.edge(v, v.tensor_mul(t1b[:, :],
                                               kvt[:, m * 512:(m + 1) * 512],
                                               rstd2[:, :]), "b2vv")
                        ins = v.tensor_scalar(kvbf[:, m * 512:(m + 1) * 512],
                                              t1b[:, :], kvnw[:, m:m + 1],
                                              None, ALU.mult)
                        if m < 3:
                            p.edge(v, ins, "b2vv")
                    p.inc(ins, "b2vv")

                @blk.scalar
                def _(s, g_=g_):
                    p.wait(s, "b2ss")
                    p.edge(s, s.mul(kperot[0:32, :], kperaw[32:64, :],
                                    p.cp("neg_one")[0:32, :]), "b2ss")
                    p.inc(s.copy(kperot[32:64, :], kperaw[0:32, :]), "b2ss")

                @blk.vector
                def _(v, g_=g_):
                    p.wait(v, "b2ss")
                    p.edge(v, v.tensor_mul(
                        kpea[:, :], kperaw[:, :], cosg[:, :]), "b2vv")
                    p.edge(v, v.tensor_mul(
                        kperot[:, :], kperot[:, :], sing[:, :]), "b2vv")
                    p.inc(v.tensor_add(kpebf[:, :], kpea[:, :],
                                       kperot[:, :]), "b2vv")
                kv_done = p.cnt["b2vv"]

                @blk.sync
                def _(sy, g_=g_):
                    p.wait(sy, "b2vv")
                    p.wait(sy, "b2ss")
                    p.inc(sy.dma_start(
                        out=xh_d.ap().rearrange(
                            "pp (k m) -> pp k m", k=16)[:, :,
                                                        g_ * 512:(g_ + 1) * 512],
                        in_=xh2.ap().rearrange("pp (k m) -> pp k m", k=16)),
                        "kvd", 16)
                    p.inc(sy.dma_start(
                        out=kvfm_g.ap().rearrange(
                            "pp (m t) -> pp m t", m=4)[:, :,
                                                       g_ * 512:(g_ + 1) * 512],
                        in_=kvbf.ap().rearrange("pp (m t) -> pp m t", m=4)),
                        "kvd", 16)
                    p.inc(sy.dma_start(
                        out=kpe_g.ap()[0:64, g_ * 512:(g_ + 1) * 512],
                        in_=kpebf[:, :]), "kvd", 16)
                    if g_ == 0:
                        for cc_ in range(16):
                            p.inc(sy.dma_start(
                                out=kpe_g.ap()[65:66,
                                               cc_ * 512:(cc_ + 1) * 512],
                                in_=zeros_row[0:1, :]), "kvd", 16)

        with ExitStack() as b4:
            en4 = b4.enter_context
            coso = en4(nc.sbuf_tensor("b4cos", [64, TPC], F32))
            sino = en4(nc.sbuf_tensor("b4sin", [64, TPC], F32))
            wqab = en4(nc.sbuf_tensor("b4wqab", [128, 12], F32))
            qnw = en4(nc.sbuf_tensor("b4qnw", [128, 12], F32))
            rstd4 = en4(nc.sbuf_tensor("b4rstd", [128, TPC], F32))
            t14 = en4(nc.sbuf_tensor("b4t1", [128, TPC], F32))
            qn = en4(nc.sbuf_tensor("b4qn", [128, 12 * TPC], BF16))
            ps4 = [en4(nc.psum_tensor(f"b4ps{i}", [128, 512], F32))
                   for i in range(6)]
            p.new_sem("b4mm", "b4vv", "b4ss", "b4w")
            b4s1 = ExitStack()
            en41 = b4s1.enter_context
            qlo = en41(nc.sbuf_tensor("b4qlo", [128, 12 * TPC], BF16))
            sq4 = en41(nc.sbuf_tensor("b4sq", [128, 12 * TPC], BF16))
            xho = en41(nc.sbuf_tensor("b4xh", [128, 16 * TPC], BF16))
            wqa = [en41(nc.sbuf_tensor(f"b4wqa{i}", [128, 16 * 512], BF16))
                   for i in range(2)]

            @blk.sync
            def _(sy):
                p.wait(sy, "kvd")
                p.inc(sy.dma_start(out=coso[:, :], in_=coso_h), "ldx", 16)
                p.inc(sy.dma_start(out=sino[:, :], in_=sino_h), "ldx", 16)
                p.inc(sy.dma_start(out=wqab[:, :], in_=wqab_h), "ldx", 16)
                p.inc(sy.dma_start(out=qnw[:, :], in_=qnw_h), "ldx", 16)
                with sy.register("ost") as r2:
                    sy.reg_load(r2, p.meta_sb[0:1, 2:3])
                    ost = sy.snap(r2, min_val=0, max_val=S - TPC)
                    p.inc(sy.dma_start(
                        out=xho.ap().rearrange("pp (k m) -> pp k m", k=16),
                        in_=xh_d.ap().rearrange(
                            "pp (k m) -> pp k m", k=16)[
                            :, :, bass.ds(ost, TPC)]), "ldx", 16)

            lnr3 = "ldx"
            bank_free = {}

            for sl in range(3):
                @blk.gpsimd
                def _(g, sl=sl):
                    if sl >= 2:
                        p.wait_at(g, "b4mm", p.cnt["b4mm"])
                    else:
                        p.wait(g, "kvd")
                    p.inc(g.dma_start(
                        out=wqa[sl % 2].ap().rearrange(
                            "pp (k m) -> pp k m", k=16),
                        in_=wqaT_h[:, :, sl * 512:(sl + 1) * 512].rearrange(
                            "k pp m -> pp k m")), "b4w", 16)
                w_need = p.cnt["b4w"]

                @blk.tensor
                def _(t, sl=sl, w_need=w_need):
                    p.wait_at(t, "b4w", w_need)
                    p.wait(t, lnr3)
                    for ml in range(4):
                        m = sl * 4 + ml
                        for hh in range(NH):
                            key = f"ql{ml % 2}"
                            if key in bank_free:
                                t.wait_ge(p.sem["b4ss"], bank_free[key])
                            bk = ps4[ml % 2]
                            for k in range(16):
                                ins = t.matmul(
                                    bk[:, :],
                                    wqa[sl % 2][:, k * 512 + ml * 128:
                                                k * 512 + ml * 128 + 128],
                                    xho[:, k * TPC + hh * 512:
                                        k * TPC + hh * 512 + 512],
                                    start=(k == 0), stop=(k == 15))
                            mmv = p.cnt["b4mm"] + 1
                            p.inc(ins, "b4mm")

                            @blk.vector
                            def _(v, m=m, hh=hh, mmv=mmv, bk=bk):
                                p.wait_at(v, "b4mm", mmv)
                                p.edge(v, v.tensor_scalar(
                                    qlo[:, m * TPC + hh * 512:
                                        m * TPC + hh * 512 + 512],
                                    bk[:, :], wqab[:, m:m + 1],
                                    None, ALU.add), "b4ss")
                                p.inc(v.tensor_mul(
                                    sq4[:, m * TPC + hh * 512:
                                        m * TPC + hh * 512 + 512],
                                    qlo[:, m * TPC + hh * 512:
                                        m * TPC + hh * 512 + 512],
                                    qlo[:, m * TPC + hh * 512:
                                        m * TPC + hh * 512 + 512]), "b4ss")
                            bank_free[key] = p.cnt["b4ss"]

            @blk.tensor
            def _(t):
                p.wait(t, "b4ss")
                for hh in range(NH):
                    if "r5" in bank_free:
                        t.wait_ge(p.sem["b4ss"], bank_free["r5"])
                    for m in range(12):
                        ins = t.matmul(ps4[5][:, :], p.ones_bf[:, :],
                                       sq4[:, m * TPC + hh * 512:
                                           m * TPC + hh * 512 + 512],
                                       start=(m == 0), stop=(m == 11))
                    mmv = p.cnt["b4mm"] + 1
                    p.inc(ins, "b4mm")

                    @blk.scalar
                    def _(s, hh=hh, mmv=mmv):
                        p.wait_at(s, "b4mm", mmv)
                        p.edge(s, s.activation(
                            rstd4[:, hh * 512:hh * 512 + 512], ps4[5][:, :],
                            AF.Ln, bias=p.cp("rms_eps"),
                            scale=p.cp("inv_qlr")), "b4ss")
                        p.inc(s.activation(
                            rstd4[:, hh * 512:hh * 512 + 512],
                            rstd4[:, hh * 512:hh * 512 + 512],
                            AF.Exp, scale=p.cp("neg_half")), "b4ss")
                    bank_free["r5"] = p.cnt["b4ss"]

            @blk.vector
            def _(v):
                p.wait(v, "b4ss")
                for m in range(12):
                    p.edge(v, v.tensor_mul(t14[:, :],
                                           qlo[:, m * TPC:(m + 1) * TPC],
                                           rstd4[:, :]), "b4vv")
                    ins = v.tensor_scalar(
                        qn[:, m * TPC:(m + 1) * TPC], t14[:, :],
                        qnw[:, m:m + 1], None, ALU.mult)
                    if m < 11:
                        p.edge(v, ins, "b4vv")
                p.inc(ins, "b4vv")
            qn_done = p.cnt["b4vv"]

            @blk.sync
            def _(sy):
                p.wait(sy, "b4vv")
                p.wait(sy, "b4ss")
                p.inc(sy.dma_start(out=win_mark0[:, :], in_=cpool_h),
                      "qpk", 16)
            b4s1.close()
            wqb = [en4(nc.sbuf_tensor(f"b4wqb{i}", [128, 12 * 192], BF16))
                   for i in range(2)]
            wqbb = en4(nc.sbuf_tensor("b4wqbb", [128, 2], F32))
            qpe = en4(nc.sbuf_tensor("b4qpe", [64, TPC], F32))
            qper = en4(nc.sbuf_tensor("b4qper", [64, TPC], F32))
            qnope = en4(nc.sbuf_tensor("b4qnope", [128, TPC], BF16))
            qaug = en4(nc.sbuf_tensor("b4qaug", [66, TPC], BF16))

            @blk.sync
            def _(sy):
                p.wait(sy, "qpk")
                for cc_ in range(2):
                    p.inc(sy.dma_start(
                        out=qaug.ap()[65:66, cc_ * 512:(cc_ + 1) * 512],
                        in_=ones_row[0:1, :]), "qpk", 16)

            for h in range(H):
                @blk.gpsimd
                def _(g, h=h):
                    if h >= 2:
                        p.wait_at(g, "b4mm", p.cnt["b4mm"])
                    else:
                        p.wait(g, "qpk")
                    p.inc(g.dma_start(
                        out=wqb[h % 2].ap().rearrange(
                            "pp (k m) -> pp k m", k=12),
                        in_=wqbT_h[h].rearrange("k pp m -> pp k m")),
                        "b4w", 16)

                @blk.sync
                def _(sy, h=h):
                    p.wait(sy, "qpk")
                    p.inc(sy.dma_start(out=wqbb[:, :], in_=wqbb_h[h]),
                          "ldx", 16)
                wqbb_need = p.cnt["ldx"]
                w_need = p.cnt["b4w"]

                @blk.vector
                def _(v, h=h, wqbb_need=wqbb_need):
                    p.wait_at(v, "ldx", wqbb_need)
                    p.wait(v, "b4vv")
                    p.inc(v.tensor_scalar(wqbb[:, :], wqbb[:, :],
                                          p.cp("scale"), None, ALU.mult),
                          "b4vv")
                bias_done = p.cnt["b4vv"]

                @blk.tensor
                def _(t, h=h, w_need=w_need, bias_done=bias_done):
                    p.wait_at(t, "b4w", w_need)
                    p.wait_at(t, "b4vv", max(qn_done, bias_done))
                    p.wait(t, "qpk")
                    for mq in range(2):
                        w = 128 if mq == 0 else 64
                        for hh in range(NH):
                            key = f"qf{mq}"
                            if key in bank_free:
                                t.wait_ge(p.sem["b4ss"], bank_free[key])
                            for k in range(12):
                                ins = t.matmul(
                                    ps4[2 + mq][0:w, :],
                                    wqb[h % 2][:, k * 192 + mq * 128:
                                               k * 192 + mq * 128 + w],
                                    qn[:, k * TPC + hh * 512:
                                       k * TPC + hh * 512 + 512],
                                    start=(k == 0), stop=(k == 11))
                            mmv = p.cnt["b4mm"] + 1
                            p.inc(ins, "b4mm")

                            @blk.vector
                            def _(v, mq=mq, w=w, hh=hh, mmv=mmv):
                                p.wait_at(v, "b4mm", mmv)
                                if mq == 0:
                                    p.inc(v.tensor_scalar(
                                        qnope[:, hh * 512:hh * 512 + 512],
                                        ps4[2][:, :], p.cp("scale"),
                                        wqbb[:, 0:1], ALU.mult, ALU.add),
                                        "b4ss")
                                else:
                                    p.inc(v.tensor_scalar(
                                        qpe[:, hh * 512:hh * 512 + 512],
                                        ps4[3][0:64, :],
                                        p.cp("scale")[0:64, :],
                                        wqbb[0:64, 1:2], ALU.mult, ALU.add),
                                        "b4ss")
                            bank_free[key] = p.cnt["b4ss"]
                qf_done = p.cnt["b4ss"]

                @blk.scalar
                def _(s, h=h, qf_done=qf_done):
                    p.wait_at(s, "b4ss", qf_done)
                    p.wait(s, "qpk")
                    p.edge(s, s.mul(qper[0:32, :], qpe[32:64, :],
                                    p.cp("neg_one")[0:32, :]), "b4ss")
                    p.inc(s.copy(qper[32:64, :], qpe[0:32, :]), "b4ss")

                @blk.vector
                def _(v, h=h):
                    p.wait(v, "b4ss")
                    p.wait(v, "qpk")
                    p.edge(v, v.tensor_mul(qpe[:, :], qpe[:, :], coso[:, :]),
                           "b4vv")
                    p.edge(v, v.tensor_mul(qper[:, :], qper[:, :],
                                           sino[:, :]), "b4vv")
                    p.edge(v, v.tensor_add(qpe[:, :], qpe[:, :], qper[:, :]),
                           "b4vv")
                    p.inc(v.tensor_copy(qaug[0:64, :], qpe[:, :]), "b4vv")

                @blk.sync
                def _(sy, h=h):
                    p.wait(sy, "b4ss")
                    p.wait(sy, "b4vv")
                    p.inc(sy.dma_start(
                        out=qaug_d.ap()[h, 0:64, :], in_=qaug[0:64, :]),
                        "qpk", 16)
                    p.inc(sy.dma_start(
                        out=qaug_d.ap()[h, 65:66, :], in_=qaug[65:66, :]),
                        "qpk", 16)
                    p.inc(sy.dma_start(out=qn_d.ap()[h], in_=qnope[:, :]),
                          "qpk", 16)

        o_sb = p.sbuf("o_sbr", [128, 16 * TPC], BF16)
        with ExitStack() as b5:
            en5 = b5.enter_context
            kvfm = en5(nc.sbuf_tensor("kvfm", [128, 4 * S], BF16))
            kpew = en5(nc.sbuf_tensor("kpew", [66, S], BF16))
            knop = en5(nc.sbuf_tensor("b5kn", [128, S], BF16))
            vsb = en5(nc.sbuf_tensor("b5vs", [128, S], BF16))
            wkvvT = en5(nc.sbuf_tensor("wkvvT_sb", [128, H * 4 * 128], BF16))
            wknT = en5(nc.sbuf_tensor("wknT_sb", [128, H * 4 * 128], BF16))
            kmask = en5(nc.sbuf_tensor("kmask_sb", [128, 4 * 512], BF16))
            dmask5 = en5(nc.sbuf_tensor("b5dm", [128, 128], F32))
            qns4 = [en5(nc.sbuf_tensor(f"b5qn{i}", [128, TPC], BF16))
                    for i in range(2)]
            qgs4 = [en5(nc.sbuf_tensor(f"b5qg{i}", [66, TPC], BF16))
                    for i in range(2)]
            probs = [en5(nc.sbuf_tensor(f"b5pr{i}", [128, 512], BF16))
                     for i in range(3)]
            denr = en5(nc.sbuf_tensor("b5den", [128, 512], F32))
            m8 = en5(nc.sbuf_tensor("b5m8", [128, 8], F32))
            m8b = en5(nc.sbuf_tensor("b5m8b", [128, 8], BF16))
            mr = en5(nc.sbuf_tensor("b5mr", [8, 128], BF16))
            tdg = en5(nc.sbuf_tensor("b5td", [128, 128], F32))
            win_mark = en5(nc.sbuf_tensor("win_mark", [128, 16], F32))
            psS = [en5(nc.psum_tensor(f"b5s{i}", [128, 512], F32))
                   for i in range(2)]
            psO = [en5(nc.psum_tensor(f"b5o{i}", [128, 512], F32))
                   for i in range(2)]
            psD = en5(nc.psum_tensor("b5d", [128, 512], F32))
            psE = [en5(nc.psum_tensor(f"b5e{i}", [128, 512], F32))
                   for i in range(2)]
            ptm = en5(nc.psum_tensor("b5pt", [128, 128], BF16))
            p.new_sem("b5mm", "b5vv", "b5ss", "b5q", "qgr")

            @blk.gpsimd
            def _(g):
                p.wait(g, "qpk")
                p.inc(g.dma_start(
                    out=wkvvT.ap().rearrange("pp (h c m) -> pp h c m",
                                             h=H, c=4),
                    in_=wkvvT_h.rearrange("h c pp m -> pp h c m")),
                    "ldw", 16)
                p.inc(g.dma_start(
                    out=wknT.ap().rearrange("pp (h c m) -> pp h c m",
                                            h=H, c=4),
                    in_=wknT_h.rearrange("h c pp m -> pp h c m")), "ldw", 16)
                p.inc(g.dma_start(
                    out=kmask.ap().rearrange("pp (r m) -> pp r m", r=4),
                    in_=kmask_h.rearrange("r pp m -> pp r m")), "ldw", 16)

            @blk.sync
            def _(sy):
                p.wait(sy, "qpk")
                p.inc(sy.dma_start(out=dmask5[:, :], in_=dmask_h), "ldx", 16)
            dm_need = p.cnt["ldx"]

            @blk.vector
            def _(v):
                p.wait(v, "qpk")
                p.inc(v.memset(win_mark[:, :], 0.0), "b5vv")

            @blk.sync
            def _(sy):
                p.wait(sy, "b5vv")
                p.wait(sy, "kvd")
                p.wait(sy, "ldx")
                winfm = kvfm_d.ap().rearrange("pp (b t) -> pp b t", t=512)
                glofm = kvfm_g.ap().rearrange("pp (b t) -> pp b t", t=512)
                winpe = kpe_d.ap().rearrange("pp (b t) -> pp b t", t=512)
                glope = kpe_g.ap().rearrange("pp (b t) -> pp b t", t=512)
                with sy.register("pB") as r0:
                    sy.reg_load(r0, p.meta_sb[0:1, 3:4])
                    padB = sy.snap(r0, min_val=0, max_val=(S - TPC) // 512)
                    for m in range(4):
                        p.inc(sy.dma_start(
                            out=winfm[:, bass.ds(padB + m * 32, 16), :],
                            in_=glofm[:, m * 16:(m + 1) * 16, :]),
                            "win", 16)
                    p.inc(sy.dma_start(
                        out=winpe[0:64, bass.ds(padB, 16), :],
                        in_=glope[0:64, :, :]), "win", 16)
                    p.inc(sy.dma_start(
                        out=winpe[65:66, bass.ds(padB, 16), :],
                        in_=glope[65:66, :, :]), "win", 16)
                p.wait(sy, "win")
                p.inc(sy.dma_start(
                    out=kvfm.ap().rearrange("pp (m t) -> pp m t", m=4),
                    in_=kvfm_d.ap().rearrange(
                        "pp (m t) -> pp m t", m=4)[:, :, 0:S]), "odw", 16)
                p.inc(sy.dma_start(out=kpew.ap()[:, :],
                                   in_=kpe_d.ap()[:, 0:S]), "odw", 16)

            eg = [0]

            def emit_kng(g_, h, first=False):
                par = eg[0] % 2
                eg[0] += 1

                @blk.tensor
                def _(t, g_=g_, h=h, par=par, first=first):
                    if first:
                        p.wait(t, "odw")
                        p.wait(t, "ldw")
                    p.bank_wait(t, [f"b5e{par}"])
                    for c in range(4):
                        ins = t.matmul(
                            psE[par][:, :],
                            wknT[:, h * 512 + c * 128:h * 512 + c * 128 + 128],
                            kvfm[:, c * S + g_ * 512:c * S + g_ * 512 + 512],
                            start=(c == 0), stop=(c == 3))
                    p.inc(ins, "b5mm")
                mmv = p.cnt["b5mm"]

                @blk.vector
                def _(v, g_=g_, par=par, mmv=mmv):
                    p.wait_at(v, "b5mm", mmv)
                    p.inc(v.tensor_copy(
                        knop[:, g_ * 512:(g_ + 1) * 512], psE[par][:, :]),
                        "b5vv")
                p.bank_set([f"b5e{par}"], "b5vv")

            def emit_vg(g_, h):
                par = eg[0] % 2
                eg[0] += 1

                @blk.tensor
                def _(t, g_=g_, h=h, par=par):
                    p.bank_wait(t, [f"b5e{par}"])
                    for j in range(4):
                        for c in range(4):
                            ins = t.matmul(
                                psE[par][:, j * 128:j * 128 + 128],
                                kvfm[:, c * S + g_ * 512 + j * 128:
                                     c * S + g_ * 512 + j * 128 + 128],
                                wkvvT[:, h * 512 + c * 128:
                                      h * 512 + c * 128 + 128],
                                start=(c == 0), stop=(c == 3))
                    p.inc(ins, "b5mm")
                mmv = p.cnt["b5mm"]

                @blk.vector
                def _(v, g_=g_, par=par, mmv=mmv):
                    p.wait_at(v, "b5mm", mmv)
                    p.inc(v.tensor_copy(
                        vsb[:, g_ * 512:(g_ + 1) * 512], psE[par][:, :]),
                        "b5vv")
                p.bank_set([f"b5e{par}"], "b5vv")

            for h in range(H):
                par_h = h % 2

                @blk.sync
                def _(sy, h=h, par_h=par_h):
                    if h < 2:
                        p.wait(sy, "qpk")
                    p.bank_wait(sy, [f"b5qb{par_h}"])
                    p.inc(sy.dma_start(out=qns4[par_h][:, :],
                                       in_=qn_d.ap()[h]), "b5q", 16)
                    p.inc(sy.dma_start(out=qgs4[par_h][:, :],
                                       in_=qaug_d.ap()[h]), "b5q", 16)
                q_need = p.cnt["b5q"]

                emit_kng(14, h, first=(h == 0))
                emit_kng(15, h)
                kn_tail = p.cnt["b5vv"]

                for qb in range(8):
                    sbq, j = qb // 4, qb % 4
                    td_ = DIAG_T[sbq][j]
                    par = eg[0] % 2
                    eg[0] += 1

                    @blk.tensor
                    def _(t, h=h, qb=qb, sbq=sbq, j=j, td_=td_, par=par,
                          q_need=q_need, kn_tail=kn_tail, par_h=par_h):
                        if qb == 0:
                            p.wait_at(t, "b5q", q_need)
                            p.wait_at(t, "b5vv", kn_tail)
                        p.bank_wait(t, [f"b5e{par}"])
                        t.matmul(psE[par][:, 0:128],
                                 qns4[par_h][:, sbq * 512 + j * 128:
                                             sbq * 512 + j * 128 + 128],
                                 knop[:, td_ * 128:td_ * 128 + 128],
                                 start=True, stop=False)
                        ins = t.matmul(
                            psE[par][:, 0:128],
                            qgs4[par_h][0:64, sbq * 512 + j * 128:
                                        sbq * 512 + j * 128 + 128],
                            kpew[0:64, td_ * 128:td_ * 128 + 128],
                            start=False, stop=True)
                        p.inc(ins, "b5mm")
                    mmv = p.cnt["b5mm"]

                    @blk.vector
                    def _(v, qb=qb, par=par, mmv=mmv):
                        p.wait_at(v, "b5mm", mmv)
                        p.wait_at(v, "ldx", dm_need)
                        p.edge(v, v.tensor_add(tdg[:, :],
                                               psE[par][:, 0:128],
                                               dmask5[:, :]), "b5vv")
                        p.inc(v.reduce_max(m8[:, qb:qb + 1], tdg[:, :],
                                           axis=AX.X), "b5vv")
                    p.bank_set([f"b5e{par}"], "b5vv")

                @blk.vector
                def _(v, h=h):
                    p.wait(v, "b5vv")
                    p.inc(v.tensor_scalar(m8b[:, :], m8[:, :],
                                          p.cp("neg_one"), None, ALU.mult),
                          "b5vv")
                neg_done = p.cnt["b5vv"]

                @blk.tensor
                def _(t, h=h, neg_done=neg_done):
                    p.wait_at(t, "b5vv", neg_done)
                    p.bank_wait(t, ["b5pt"])
                    p.inc(t.transpose(ptm[0:8, :], m8b[:, :],
                                      p.ident_bf[:, :]), "b5mm")
                mmv2 = p.cnt["b5mm"]

                @blk.vector
                def _(v, h=h, mmv2=mmv2):
                    p.wait_at(v, "b5mm", mmv2)
                    p.bank_wait(v, ["b5mr"])
                    p.inc(v.tensor_copy(mr[:, :], ptm[0:8, :]), "b5vv")
                p.bank_set(["b5pt"], "b5vv")
                mr_done = p.cnt["b5vv"]

                @blk.sync
                def _(sy, h=h, mr_done=mr_done):
                    p.wait_at(sy, "b5vv", mr_done)
                    p.wait(sy, "qgr")
                    p.inc(sy.dma_start(out=mrow_d.ap(), in_=mr[:, :]),
                          "b5q", 16)
                p.bank_set(["b5mr"], "b5q")
                mrd = p.cnt["b5q"]

                @blk.sync
                def _(sy, h=h, mrd=mrd, par_h=par_h):
                    p.wait_at(sy, "b5q", mrd)
                    p.inc(sy.dma_start(
                        out=qgs4[par_h].ap()[64:65, :],
                        in_=mrow_d.ap().rearrange("(a b) q -> a (b q)", a=1)),
                        "qgr", 16)
                qgr_need = p.cnt["qgr"]

                for g_ in range(14):
                    emit_kng(g_, h)
                kn_all = p.cnt["b5vv"]
                for g_ in range(16):
                    emit_vg(g_, h)
                v_all = p.cnt["b5vv"]

                for sb in range(2):
                    ih2 = h * 2 + sb
                    opar = ih2 % 2
                    LT = LAST_T[sb]
                    ss0 = p.cnt["b5ss"]
                    mm_sc = {}
                    pv_mm = {}
                    pv_ready = {}

                    def emit_scores_t(t_):
                        sbk = t_ % 2

                        @blk.tensor
                        def _(t, sb=sb, t_=t_, sbk=sbk, opar=opar,
                              par_h=par_h, ss0=ss0):
                            if t_ == 0:
                                p.wait_at(t, "qgr", qgr_need)
                                p.wait_at(t, "b5vv", kn_all)
                                p.bank_wait(t, [f"b5o{opar}", "b5d"])
                            if t_ >= 2:
                                p.wait_at(t, "b5ss", ss0 + t_ - 1)
                            else:
                                p.bank_wait(t, [f"b5sc{sbk}"])
                            t.matmul(psS[sbk][:, :],
                                     knop[:, t_ * 128:t_ * 128 + 128],
                                     qns4[par_h][:, sb * 512:sb * 512 + 512],
                                     start=True, stop=False)
                            ins = t.matmul(
                                psS[sbk][:, :],
                                kpew[:, t_ * 128:t_ * 128 + 128],
                                qgs4[par_h][:, sb * 512:sb * 512 + 512],
                                start=False, stop=True)
                            p.inc(ins, "b5mm")
                        mm_sc[t_] = p.cnt["b5mm"]
                        pbuf = probs[t_ % 3]

                        if t_ in DIAG_T[sb]:
                            ri = t_ - DIAG_T[sb][0]

                            @blk.vector
                            def _(v, ri=ri, pbuf=pbuf, t_=t_, ss0=ss0):
                                p.wait_at(v, "b5ss", ss0 + t_ + 1)
                                p.wait(v, "ldw")
                                p.inc(v.tensor_mul(
                                    pbuf[:, :], pbuf[:, :],
                                    kmask[:, ri * 512:ri * 512 + 512]),
                                    "b5vv")
                            pv_ready[t_] = ("b5vv", p.cnt["b5vv"])
                        else:
                            pv_ready[t_] = ("b5ss", ss0 + t_ + 1)

                    def emit_pv_t(t_):
                        pbuf = probs[t_ % 3]
                        pv_sem, pv_val = pv_ready[t_]

                        @blk.tensor
                        def _(t, t_=t_, pbuf=pbuf, pv_sem=pv_sem,
                              pv_val=pv_val, opar=opar):
                            p.wait_at(t, pv_sem, pv_val)
                            if t_ == 0:
                                p.wait_at(t, "b5vv", v_all)
                            last = (t_ == LT - 1)
                            t.matmul(psO[opar][:, :],
                                     vsb[:, t_ * 128:t_ * 128 + 128],
                                     pbuf[:, :],
                                     start=(t_ == 0), stop=last)
                            ins = t.matmul(psD[:, :], p.ones_bf[:, :],
                                           pbuf[:, :],
                                           start=(t_ == 0), stop=last)
                            p.inc(ins, "b5mm")
                        pv_mm[t_] = p.cnt["b5mm"]

                    emit_scores_t(0)
                    for t_ in range(1, LT):
                        emit_scores_t(t_)
                        emit_pv_t(t_ - 1)
                    emit_pv_t(LT - 1)
                    mm_pv = p.cnt["b5mm"]

                    @blk.scalar
                    def _(s, sb=sb, LT=LT, mm_sc=dict(mm_sc)):
                        for t_ in range(LT):
                            sbk = t_ % 2
                            p.wait_at(s, "b5mm", mm_sc[t_])
                            p.inc(s.activation(probs[t_ % 3][:, :],
                                               psS[sbk][:, :], AF.Exp),
                                  "b5ss")
                    assert p.cnt["b5ss"] == ss0 + LT
                    p.bank["b5sc0"] = ("b5ss", ss0 + LT - 1)
                    p.bank["b5sc1"] = ("b5ss", ss0 + LT)
                    for idx in range(3):
                        lastt = max(t for t in range(LT) if t % 3 == idx)
                        p.bank[f"b5pb{idx}"] = ("b5mm", pv_mm[lastt])

                    @blk.vector
                    def _(v, sb=sb, h=h, mm_pv=mm_pv, opar=opar):
                        p.wait_at(v, "b5mm", mm_pv)
                        p.edge(v, v.reciprocal(denr[:, :], psD[:, :]),
                               "b5vv")
                        p.inc(v.tensor_mul(
                            o_sb[:, h * TPC + sb * 512:
                                 h * TPC + sb * 512 + 512],
                            psO[opar][:, :], denr[:, :]), "b5vv")
                    p.bank_set(["b5d", f"b5o{opar}"], "b5vv")
                p.bank_set([f"b5qb{par_h}"], "b5mm")

            @blk.sync
            def _(sy):
                p.wait(sy, "odw")
                p.wait(sy, "b5vv")
                p.wait(sy, "b5ss")
                p.wait(sy, "b5mm")
                p.inc(sy.dma_start(out=win_mark0[:, :], in_=cpool_h),
                      "odw", 16)

        v_sb_l = p.sbuf("b6v", [128, 16 * TPC], F32)
        with ExitStack() as b6:
            en6 = b6.enter_context
            wo_b_ = [en6(nc.sbuf_tensor(f"b6wo{i}", [128, 16 * 512], BF16))
                     for i in range(2)]
            t1_6 = en6(nc.sbuf_tensor("b6t1", [128, 512], F32))
            ps6_ = [en6(nc.psum_tensor(f"b6ps{i}", [128, 512], F32))
                    for i in range(4)]
            p.new_sem("b6mm", "b6vv", "b6ss", "b6w")

            @blk.sync
            def _(sy):
                p.wait(sy, "odw")
                p.inc(sy.dma_start(
                    out=v_sb_l.ap().rearrange("pp (k m) -> pp k m", k=16),
                    in_=uo.rearrange("k pp m -> pp k m")), "ldx", 16)

            bank6 = {}
            for mg in range(4):
                @blk.gpsimd
                def _(g, mg=mg):
                    if mg >= 2:
                        p.wait_at(g, "b6mm", p.cnt["b6mm"])
                    else:
                        p.wait(g, "odw")
                    p.inc(g.dma_start(
                        out=wo_b_[mg % 2].ap().rearrange(
                            "pp (k m) -> pp k m", k=16),
                        in_=woT_h[:, :, mg * 512:(mg + 1) * 512].rearrange(
                            "k pp m -> pp k m")), "b6w", 16)
                w_need = p.cnt["b6w"]

                @blk.tensor
                def _(t, mg=mg, w_need=w_need):
                    p.wait_at(t, "b6w", w_need)
                    p.wait(t, "ldx")
                    if mg == 0:
                        p.wait(t, "b5vv")
                    for ml in range(4):
                        m = mg * 4 + ml
                        for hh in range(NH):
                            key = f"wo{ml % 2}"
                            if key in bank6:
                                t.wait_ge(p.sem["b6vv"], bank6[key])
                            for k in range(16):
                                ins = t.matmul(
                                    ps6_[ml % 2][:, :],
                                    wo_b_[mg % 2][:, k * 512 + ml * 128:
                                                  k * 512 + ml * 128 + 128],
                                    o_sb[:, k * TPC + hh * 512:
                                         k * TPC + hh * 512 + 512],
                                    start=(k == 0), stop=(k == 15))
                            mmv = p.cnt["b6mm"] + 1
                            p.inc(ins, "b6mm")

                            @blk.vector
                            def _(v, m=m, ml=ml, hh=hh, mmv=mmv):
                                p.wait_at(v, "b6mm", mmv)
                                p.edge(v, v.tensor_scalar(
                                    t1_6[:, :], ps6_[ml % 2][:, :],
                                    wob_sb[:, m:m + 1], None, ALU.add),
                                    "b6vv")
                                p.inc(v.tensor_add(
                                    v_sb_l[:, m * TPC + hh * 512:
                                           m * TPC + hh * 512 + 512],
                                    t1_6[:, :],
                                    v_sb_l[:, m * TPC + hh * 512:
                                           m * TPC + hh * 512 + 512]),
                                    "b6vv")
                            bank6[f"wo{ml % 2}"] = p.cnt["b6vv"]

            @blk.sync
            def _(sy):
                p.wait(sy, "b6vv")
                p.wait(sy, "b6ss")
                p.wait(sy, "b6mm")
                p.wait(sy, "b6w")
                p.inc(sy.dma_start(out=win_mark0[:, :], in_=cpool_h),
                      "b6w", 16)

        xh3 = p.sbuf("b7xh", [128, 16 * TPC], BF16)
        gate_sb = p.sbuf("b7gate", [128, 16 * 8], BF16)

        @blk.gpsimd
        def _(g):
            p.wait(g, "b6vv")
            p.wait(g, "b6w")
            p.inc(g.dma_start(
                out=gate_sb.ap().rearrange("pp (k m) -> pp k m", k=16),
                in_=gateT_h.rearrange("k pp m -> pp k m")), "ldw", 16)

        ln3, _c3 = emit_layernorm(p, blk, "b7ln", v_sb_l, xh3, TPC,
                                  ["b6vv", "b6w", "ldw"])
        f2, _c4 = emit_ffn(p, blk, "f2", xh3, v_sb_l, w1p_h, w2t_h,
                           gate_sb, TPC, [ln3])

        @blk.sync
        def _(sy):
            p.wait(sy, f2)
            p.inc(sy.dma_start(
                out=outT.rearrange("k pp m -> pp k m"),
                in_=v_sb_l.ap().rearrange("pp (k m) -> pp k m", k=16)),
                "std", 16)
            p.wait(sy, "std")
    return nc


def prep_attn_weights(ins):
    f32 = np.float32
    g = lambda k: np.asarray(ins[k], f32)
    w = {}
    w["wqaT"] = np.ascontiguousarray(g("L0_attn_wq_a_w").T.reshape(16, 128, QLR))
    w["wqab"] = _bias_tiles(g("L0_attn_wq_a_b"), 12)
    w["qnw"] = _bias_tiles(g("L0_attn_q_norm_w"), 12)
    wqb = g("L0_attn_wq_b_w")
    wqbT = np.stack([
        np.ascontiguousarray(wqb[192 * h:192 * h + 192, :].T.reshape(
            12, 128, 192)) for h in range(H)])
    w["wqbT"] = wqbT
    bb = g("L0_attn_wq_b_b")
    t = np.zeros((H, 128, 2), f32)
    for h in range(H):
        t[h, :, 0] = bb[192 * h:192 * h + 128]
        t[h, 0:64, 1] = bb[192 * h + 128:192 * h + 192]
    w["wqbb"] = t
    w["wkvaT"] = np.ascontiguousarray(
        g("L0_attn_wkv_a_w").T.reshape(16, 128, 576))
    w["wkvab"] = _bias_tiles(g("L0_attn_wkv_a_b"), 5)
    w["kvnw"] = _bias_tiles(g("L0_attn_kv_norm_w"), 4)
    wkvb = g("L0_attn_wkv_b_w").reshape(H, 256, KVLR)
    w["wknT"] = np.ascontiguousarray(
        wkvb[:, 0:128, :].transpose(0, 2, 1).reshape(H, 4, 128, 128))
    w["wkvvT"] = np.ascontiguousarray(
        wkvb[:, 128:256, :].transpose(0, 2, 1).reshape(H, 4, 128, 128))
    w["woT"] = np.ascontiguousarray(g("L0_attn_wo_w").T.reshape(16, 128, D))
    w["wob"] = _bias_tiles(g("L0_attn_wo_b"), 16)
    return w


def _rope_tables_T():
    inv = 1.0 / 10000.0 ** (np.arange(0, ROPE, 2, dtype=np.float32) / ROPE)
    t = np.arange(S, dtype=np.float32)[:, None]
    fr = t * inv[None, :]
    fr = np.concatenate([fr, fr], axis=-1)
    return (np.ascontiguousarray(np.cos(fr).T),
            np.ascontiguousarray(np.sin(fr).T))


def _masks():
    q = np.arange(128)[:, None]
    k = np.arange(128)[None, :]
    dmask = np.where(k <= q, 0.0, -1e30).astype(np.float32)
    kk = np.arange(128)[:, None]
    qq = np.arange(512)[None, :]
    km = np.stack([(kk <= qq - 128 * ri).astype(np.float32)
                   for ri in range(4)])
    return dmask, np.ascontiguousarray(km)


def run_launch_B(inputs, u):
    aw = prep_attn_weights(inputs)
    w1p, w2t, gateT = prep_ffn_weights(inputs)
    cpool = _cpool_np()
    ident = np.eye(128, dtype=np.float32)
    cosT, sinT = _rope_tables_T()
    dmask, kmask = _masks()
    uf = np.ascontiguousarray(u.reshape(16, 128, S))
    nc = build_B()
    ims = []
    for c in range(NCORE):
        pad = 7168 - 1024 * c
        meta = np.array([[pad, 4 * pad, 1024 * c, pad // 512,
                          pad // 128, 0, 0, 0]], np.int32)
        ims.append({
            "uf": uf,
            "uo": np.ascontiguousarray(
                u[:, c * TPC:(c + 1) * TPC].reshape(16, 128, TPC)),
            "cpool": cpool, "ident": ident,
            "cosg": cosT, "sing": sinT,
            "coso": np.ascontiguousarray(cosT[:, c * TPC:(c + 1) * TPC]),
            "sino": np.ascontiguousarray(sinT[:, c * TPC:(c + 1) * TPC]),
            "dmask": dmask, "kmask": kmask, "meta": meta,
            "wqaT": aw["wqaT"], "wqab": aw["wqab"], "qnw": aw["qnw"],
            "wqbT": aw["wqbT"], "wqbb": aw["wqbb"],
            "wkvaT": aw["wkvaT"], "wkvab": aw["wkvab"], "kvnw": aw["kvnw"],
            "wknT": aw["wknT"], "wkvvT": aw["wkvvT"],
            "woT": aw["woT"], "wob": aw["wob"],
            "w1p": w1p, "w2t": w2t, "gateT": gateT,
        })
    res = run_bass_kernel_spmd(nc, ims, core_ids=list(range(NCORE)))
    out = np.empty((D, S), np.float32)
    for c in range(NCORE):
        out[:, c * TPC:(c + 1) * TPC] = \
            np.asarray(res.results[c]["outT"]).reshape(D, TPC)
    return out


def kernel(**inputs):
    u = run_launch_A(inputs)
    out = run_launch_B(inputs, u)
    return np.ascontiguousarray(out.T).reshape(1, S, D).astype(np.float32)



# version 10
# speedup vs baseline: 2.0923x; 1.0238x over previous
import math
from contextlib import ExitStack

import numpy as np
import ml_dtypes
import concourse.bass as bass
import concourse.mybir as mybir
from concourse.bass_utils import run_bass_kernel_spmd

F32 = mybir.dt.float32
BF16 = mybir.dt.bfloat16
I32 = mybir.dt.int32
AF = mybir.ActivationFunctionType
ALU = mybir.AluOpType
AX = mybir.AxisListType
BF16NP = ml_dtypes.bfloat16

S, D, H = 8192, 2048, 16
QLR, KVLR = 1536, 512
NOPE, ROPE, VD = 128, 64, 128
QKD = NOPE + ROPE
E, I_FF = 8, 1408
NCORE = 8
TPC = S // NCORE
NH = TPC // 512
SCALE = 1.0 / math.sqrt(QKD)

CP = {"ln_eps": 0, "rms_eps": 1, "neg_half": 2, "inv_d": 3, "inv_qlr": 4,
      "inv_kvlr": 5, "half": 6, "one": 7, "neg_big": 8, "zero": 9,
      "neg_one": 10, "scale": 11}


def _cpool_np():
    v = np.zeros((128, 16), np.float32)
    for k, val in [("ln_eps", 1e-5), ("rms_eps", 1e-6), ("neg_half", -0.5),
                   ("inv_d", 1.0 / D), ("inv_qlr", 1.0 / QLR),
                   ("inv_kvlr", 1.0 / KVLR), ("half", 0.5), ("one", 1.0),
                   ("neg_big", -1e30), ("zero", 0.0), ("neg_one", -1.0),
                   ("scale", SCALE)]:
        v[:, CP[k]] = val
    return v


class P:

    def __init__(self, nc, stack):
        self.nc = nc
        self.stack = stack
        self.en = stack.enter_context
        self.sem = {}
        self.cnt = {}
        self.bank = {}

    def sbuf(self, name, shape, dt):
        return self.en(self.nc.sbuf_tensor(name, shape, dt))

    def new_sem(self, *names):
        for n in names:
            if n not in self.sem:
                self.sem[n] = self.en(self.nc.semaphore(n))
                self.cnt[n] = 0

    def inc(self, instr, name, n=1):
        self.cnt[name] += n
        instr.then_inc(self.sem[name], n)
        return instr

    def wait(self, eng, name):
        if self.cnt[name] > 0:
            eng.wait_ge(self.sem[name], self.cnt[name])

    def wait_at(self, eng, name, v):
        if v > 0:
            eng.wait_ge(self.sem[name], v)

    def bank_wait(self, eng, keys):
        for k in keys:
            if k in self.bank:
                s, v = self.bank[k]
                self.wait_at(eng, s, v)

    def bank_set(self, keys, sem_name):
        for k in keys:
            self.bank[k] = (sem_name, self.cnt[sem_name])

    def edge(self, eng, instr, name):
        self.inc(instr, name)
        eng.wait_ge(self.sem[name], self.cnt[name])

    def cp(self, name):
        return self.cpool[:, CP[name]:CP[name] + 1]


def emit_layernorm(p, blk, tag, in_sb, out_sb, T, ready_sems, sem_tag=None,
                   pss_ext=None):
    nc = p.nc
    nh = T // 512
    st = sem_tag or tag
    with ExitStack() as ph:
        en = ph.enter_context
        sq = en(nc.sbuf_tensor(f"{tag}sq", [128, 16 * T], BF16))
        m_sb = en(nc.sbuf_tensor(f"{tag}m", [128, T], F32))
        v_sb = en(nc.sbuf_tensor(f"{tag}v", [128, T], F32))
        rstd = en(nc.sbuf_tensor(f"{tag}r", [128, T], F32))
        tsc = en(nc.sbuf_tensor(f"{tag}t", [128, T], F32))
        pss = pss_ext if pss_ext is not None else [
            en(nc.psum_tensor(f"{tag}ps{i}", [128, 512], F32))
            for i in range(2 * nh)]
        p.new_sem(f"{st}mm", f"{st}vv", f"{st}ss")
        mm, vv, ss = f"{st}mm", f"{st}vv", f"{st}ss"
        ent = {n: p.cnt[n] for n in (mm, vv, ss)}

        @blk.vector
        def _(v):
            for r in ready_sems:
                p.wait(v, r)
            for n, v_ in ent.items():
                p.wait_at(v, n, v_)
            p.inc(v.tensor_mul(sq[:, :], in_sb[:, :], in_sb[:, :]), ss)

        @blk.tensor
        def _(t):
            for r in ready_sems:
                p.wait(t, r)
            for n, v_ in ent.items():
                p.wait_at(t, n, v_)
            p.wait(t, ss)
            for k in range(16):
                for h in range(nh):
                    t.matmul(pss[h][:, :], p.ones_f32[:, :],
                             in_sb[:, k * T + h * 512: k * T + h * 512 + 512],
                             start=(k == 0), stop=(k == 15))
                    ins = t.matmul(pss[nh + h][:, :], p.ones_bf[:, :],
                                   sq[:, k * T + h * 512: k * T + h * 512 + 512],
                                   start=(k == 0), stop=(k == 15))
            p.inc(ins, mm)

        @blk.vector
        def _(v):
            p.wait(v, mm)
            for h in range(nh):
                ins = v.tensor_scalar(m_sb[:, h * 512:h * 512 + 512],
                                      pss[h][:, :], p.cp("inv_d"), None,
                                      ALU.mult)
                if h < nh - 1:
                    p.edge(v, ins, ss)
            p.inc(ins, ss)

        @blk.vector
        def _(v):
            p.wait(v, ss)
            for n, v_ in ent.items():
                p.wait_at(v, n, v_)
            p.edge(v, v.tensor_mul(tsc[:, :], m_sb[:, :], m_sb[:, :]), vv)
            for h in range(nh):
                ins = v.scalar_tensor_tensor(
                    v_sb[:, h * 512:h * 512 + 512], pss[nh + h][:, :],
                    p.cp("inv_d"), tsc[:, h * 512:h * 512 + 512],
                    ALU.mult, ALU.subtract)
            p.inc(ins, vv)

        @blk.scalar
        def _(s):
            p.wait(s, vv)
            p.edge(s, s.activation(rstd[:, :], v_sb[:, :], AF.Ln,
                                   bias=p.cp("ln_eps")), ss)
            p.inc(s.activation(rstd[:, :], rstd[:, :], AF.Exp,
                               scale=p.cp("neg_half")), ss)

        @blk.vector
        def _(v):
            p.wait(v, ss)
            for k in range(16):
                p.edge(v, v.tensor_sub(tsc[:, :], in_sb[:, k * T:(k + 1) * T],
                                       m_sb[:, :]), vv)
                ins = v.tensor_mul(out_sb[:, k * T:(k + 1) * T], tsc[:, :],
                                   rstd[:, :])
                if k == 15:
                    p.inc(ins, vv)
                else:
                    p.edge(v, ins, vv)
        done = p.cnt[vv]
    return vv, done


def emit_ffn(p, blk, tag, xh_sb, resid_sb, w1p_ap, w2t_ap, gate_sb, T,
             ready_sems, dbg=None):
    nc = p.nc
    nh = T // 512
    ntc = T // 128
    with ExitStack() as ph:
        en = ph.enter_context
        act = en(nc.sbuf_tensor(f"{tag}act", [128, 11 * T], BF16))
        sg2 = [en(nc.sbuf_tensor(f"{tag}sg0", [128, T], F32))] * 2
        ucs = en(nc.sbuf_tensor(f"{tag}uc", [128, T], F32))
        lg = en(nc.sbuf_tensor(f"{tag}lg", [8, T], F32))
        lt = en(nc.sbuf_tensor(f"{tag}lt", [128, 8], F32))
        ltd = en(nc.sbuf_tensor(f"{tag}ltd", [128, 8], F32))
        lsel = en(nc.sbuf_tensor(f"{tag}lsel", [128, 8], F32))
        lmsk = en(nc.sbuf_tensor(f"{tag}lmsk", [128, 8], F32))
        nm1 = en(nc.sbuf_tensor(f"{tag}nm1", [128, 1], F32))
        m2s = en(nc.sbuf_tensor(f"{tag}m2", [128, 1], F32))
        lz = en(nc.sbuf_tensor(f"{tag}lz", [128, 1], F32))
        cmbT = en(nc.sbuf_tensor(f"{tag}cmbT", [8, 128], F32))
        cTb = [en(nc.sbuf_tensor(f"{tag}cT{i}", [1, T], BF16))
               for i in range(2)]
        c_bc = en(nc.sbuf_tensor(f"{tag}cbc", [128, T], F32))
        w1b = [en(nc.sbuf_tensor(f"{tag}w1{i}", [128, 16 * 256], BF16))
               for i in range(2)]
        w2b = [en(nc.sbuf_tensor(f"{tag}w2{i}", [128, 11 * 256], BF16))
               for i in range(2)]
        ps = [en(nc.psum_tensor(f"{tag}ps{i}", [128, 512], F32))
              for i in range(8)]
        cT_d = nc.dram_tensor(f"{tag}cTd", [E, T], F32, kind="Internal")

        p.new_sem(f"{tag}mm", f"{tag}vv", f"{tag}ss", f"{tag}w1", f"{tag}w2",
                  f"{tag}af", f"{tag}cd", f"{tag}ct")
        mm, vv, ss = f"{tag}mm", f"{tag}vv", f"{tag}ss"
        w1s, w2s, afs, cds = f"{tag}w1", f"{tag}w2", f"{tag}af", f"{tag}cd"
        cts = f"{tag}ct"

        @blk.tensor
        def _(t):
            for r in ready_sems:
                p.wait(t, r)
            p.bank_wait(t, ["ps6", "ps7"])
            for h in range(nh):
                for k in range(16):
                    ins = t.matmul(
                        ps[6 + h][0:8, :], gate_sb[:, k * 8:k * 8 + 8],
                        xh_sb[:, k * T + h * 512: k * T + h * 512 + 512],
                        start=(k == 0), stop=(k == 15))
            p.inc(ins, mm)

        @blk.vector
        def _(v):
            p.wait(v, mm)
            for h in range(nh):
                ins = v.tensor_copy(lg[0:8, h * 512:h * 512 + 512],
                                    ps[6 + h][0:8, :])
                if h < nh - 1:
                    p.edge(v, ins, ss)
            p.inc(ins, ss)
            p.bank_set(["ps6", "ps7"], ss)

        if dbg is not None:
            @blk.sync
            def _(sy):
                p.wait(sy, ss)
                p.inc(sy.dma_start(out=dbg["xh"], in_=xh_sb[:, :]), "std", 16)
                p.inc(sy.dma_start(out=dbg["lg"], in_=lg[0:8, :]), "std", 16)

        for tc in range(ntc):
            @blk.tensor
            def _(t):
                p.wait(t, ss)
                p.bank_wait(t, ["ps6"])
                p.inc(t.transpose(ps[6][:, 0:8],
                                  lg[0:8, tc * 128:(tc + 1) * 128],
                                  p.ident_f32[0:8, 0:8]), mm)

            @blk.vector
            def _(v):
                p.wait(v, mm)
                p.wait(v, ss)
                p.edge(v, v.tensor_copy(lt[:, :], ps[6][:, 0:8]), vv)
                p.edge(v, v.reduce_max(nm1[:, :], lt[:, :], axis=AX.X,
                                       negate=True), vv)
                p.bank_set(["ps6"], vv)
                p.edge(v, v.tensor_scalar(ltd[:, :], lt[:, :], nm1[:, :],
                                          None, ALU.add), vv)
                p.edge(v, v.tensor_scalar(lmsk[:, :], ltd[:, :], p.cp("zero"),
                                          None, ALU.is_equal), vv)
                p.edge(v, v.scalar_tensor_tensor(lmsk[:, :], lmsk[:, :],
                                                 p.cp("neg_big"), lt[:, :],
                                                 ALU.mult, ALU.add), vv)
                p.edge(v, v.reduce_max(m2s[:, :], lmsk[:, :], axis=AX.X), vv)
                p.inc(v.tensor_scalar(lsel[:, :], lt[:, :], m2s[:, :], None,
                                      ALU.is_ge), vv)

            @blk.scalar
            def _(s):
                p.wait(s, vv)
                p.inc(s.activation(ltd[:, :], lt[:, :], AF.Exp,
                                   bias=nm1[:, :]), ss)

            @blk.vector
            def _(v):
                p.wait(v, ss)
                p.edge(v, v.tensor_mul(ltd[:, :], ltd[:, :], lsel[:, :]), vv)
                p.edge(v, v.reduce_sum(lz[:, :], ltd[:, :], axis=AX.X), vv)
                p.edge(v, v.reciprocal(lz[:, :], lz[:, :]), vv)
                p.inc(v.tensor_scalar(ltd[:, :], ltd[:, :], lz[:, :],
                                      p.cp("half"), ALU.mult, ALU.mult), vv)

            @blk.tensor
            def _(t):
                p.wait(t, vv)
                p.bank_wait(t, ["ps6"])
                p.inc(t.transpose(ps[6][0:8, 0:128], ltd[:, :],
                                  p.ident_f32[:, :]), mm)

            @blk.vector
            def _(v):
                p.wait(v, mm)
                p.bank_wait(v, ["cmbT"])
                p.inc(v.tensor_copy(cmbT[:, :], ps[6][0:8, 0:128]), ss)
                p.bank_set(["ps6"], ss)

            @blk.sync
            def _(sy):
                p.wait(sy, ss)
                p.inc(sy.dma_start(out=cT_d.ap()[:, tc * 128:(tc + 1) * 128],
                                   in_=cmbT[:, :]), cds, 16)
                p.bank_set(["cmbT"], cds)


        if dbg is not None:
            dmisc = en(nc.sbuf_tensor(f"{tag}dmisc", [128, 256], F32))

            @blk.vector
            def _(v):
                p.wait(v, cts)
                v.memset(dmisc[:, :], 0.0)
                v.tensor_copy(dmisc[0:8, 0:128], cmbT[:, :])
                v.tensor_copy(dmisc[:, 128:136], ltd[:, :])
                v.tensor_copy(dmisc[:, 136:137], nm1[:, :])
                v.tensor_copy(dmisc[:, 137:138], lz[:, :])
                v.tensor_copy(dmisc[:, 138:146], lt[:, :])
                v.tensor_copy(dmisc[:, 146:154], lsel[:, :])
                p.inc(v.tensor_copy(dmisc[0:1, 160:160 + 64],
                                    cTb[1][0:1, 0:64]), vv)

            @blk.sync
            def _(sy):
                p.wait(sy, vv)
                p.inc(sy.dma_start(out=dbg["misc"], in_=dmisc[:, :]),
                      "std", 16)

        NP_SL = [1] * 11
        w1_free = []
        w2_free = []

        bcast_mm = {}
        for e in range(E):
            @blk.gpsimd
            def _(g, e=e):
                p.wait(g, cds)
                if e - 2 in bcast_mm:
                    p.wait_at(g, mm, bcast_mm[e - 2])
                p.inc(g.dma_start(out=cTb[e % 2][0:1, :],
                                  in_=cT_d.ap()[e:e + 1, :]), cts, 16)
            ct_need = p.cnt[cts]

            @blk.tensor
            def _(t, e=e, ct_need=ct_need):
                p.wait_at(t, cts, ct_need)
                p.bank_wait(t, ["ps6", "ps7"])
                for h in range(nh):
                    p.inc(t.matmul(
                        ps[6 + h][:, :], p.ones_col_bf[0:1, :],
                        cTb[e % 2][0:1, h * 512:h * 512 + 512],
                        start=True, stop=True), mm)
            bcast_mm[e] = p.cnt[mm]

            @blk.vector
            def _(v):
                p.wait(v, mm)
                p.bank_wait(v, ["cbc"])
                for h in range(nh):
                    ins = v.tensor_copy(c_bc[:, h * 512:h * 512 + 512],
                                        ps[6 + h][:, :])
                    if h < nh - 1:
                        p.edge(v, ins, ss)
                p.inc(ins, ss)
                p.bank_set(["ps6", "ps7"], ss)

            cbc_ready = p.cnt[ss]
            silu_jobs = []
            ep_afs = {}
            ss_sil0 = p.cnt[ss]

            pair0 = 0
            for sli, npair in enumerate(NP_SL):
                gslice = e * len(NP_SL) + sli
                buf = gslice % 2

                @blk.gpsimd
                def _(g):
                    if gslice >= 2:
                        sname, v = w1_free[gslice - 2]
                        p.wait_at(g, sname, v)
                    else:
                        for r in ready_sems:
                            p.wait(g, r)
                    cols = npair * 256
                    p.inc(g.dma_start(
                        out=w1b[buf].ap().rearrange(
                            "pp (k m) -> pp k m", k=16)[:, :, 0:cols],
                        in_=w1p_ap[e, :, :,
                                   pair0 * 256:pair0 * 256 + cols].rearrange(
                            "k pp m -> pp k m")), w1s, 16)
                w1_need = p.cnt[w1s]

                for pl in range(npair):
                    gp = pair0 + pl
                    grp = gp % 2
                    bks = [f"ps{4 * grp + j}" for j in range(4)]
                    bg = [ps[4 * grp + j] for j in range(4)]

                    @blk.tensor
                    def _(t):
                        p.wait_at(t, w1s, w1_need)
                        p.wait_at(t, ss, cbc_ready)
                        p.bank_wait(t, bks)
                        W = 256
                        for k in range(16):
                            for hh in range(nh):
                                t.matmul(bg[hh][:, :],
                                         w1b[buf][:, k * W + pl * 256:
                                                  k * W + pl * 256 + 128],
                                         xh_sb[:, k * T + hh * 512:
                                               k * T + hh * 512 + 512],
                                         start=(k == 0), stop=(k == 15))
                                ins = t.matmul(
                                    bg[2 + hh][:, :],
                                    w1b[buf][:, k * W + pl * 256 + 128:
                                             k * W + pl * 256 + 256],
                                    xh_sb[:, k * T + hh * 512:
                                          k * T + hh * 512 + 512],
                                    start=(k == 0), stop=(k == 15))
                        p.inc(ins, mm)
                    mm_grp = p.cnt[mm]

                    ss_grp = ss_sil0 + gp + 1
                    silu_jobs.append((mm_grp, gp, bg, bks))

                    @blk.vector
                    def _(v, mm_grp=mm_grp, ss_grp=ss_grp, gp=gp, bg=bg):
                        p.wait_at(v, mm, mm_grp)
                        p.wait_at(v, ss, ss_grp)
                        sgb = sg2[gp % 2]
                        for hh in range(nh):
                            p.edge(v, v.tensor_mul(
                                ucs[:, hh * 512:hh * 512 + 512],
                                bg[2 + hh][:, :],
                                c_bc[:, hh * 512:hh * 512 + 512]), vv)
                            ins = v.tensor_mul(
                                act[:, gp * T + hh * 512:
                                    gp * T + hh * 512 + 512],
                                sgb[:, hh * 512:hh * 512 + 512],
                                ucs[:, hh * 512:hh * 512 + 512])
                            if hh < nh - 1:
                                p.edge(v, ins, vv)
                        p.inc(ins, afs)
                    p.bank_set(bks, afs)
                    ep_afs[gp] = p.cnt[afs]
                pair0 += npair
                w1_free.append((afs, p.cnt[afs]))

            @blk.scalar
            def _(s, silu_jobs=list(silu_jobs), e=e):
                for mm_grp, gp, bg, bks in silu_jobs:
                    p.wait_at(s, mm, mm_grp)
                    if gp - 1 in ep_afs:
                        p.wait_at(s, afs, ep_afs[gp - 1])
                    sgb = sg2[gp % 2]
                    for hh in range(nh):
                        ins = s.activation(sgb[:, hh * 512:hh * 512 + 512],
                                           bg[hh][:, :], AF.Silu)
                    p.inc(ins, ss)
            assert p.cnt[ss] == ss_sil0 + 11

            act_all = p.cnt[afs]

            for q in range(8):
                gq = e * 8 + q
                buf2 = gq % 2

                @blk.gpsimd
                def _(g):
                    if gq >= 2:
                        sname, v = w2_free[gq - 2]
                        p.wait_at(g, sname, v)
                    else:
                        for r in ready_sems:
                            p.wait(g, r)
                    p.inc(g.dma_start(
                        out=w2b[buf2].ap().rearrange(
                            "pp (k m) -> pp k m", k=11),
                        in_=w2t_ap[e, :, :, q * 256:q * 256 + 256].rearrange(
                            "k pp m -> pp k m")), w2s, 16)
                w2_need = p.cnt[w2s]

                for m2l in range(2):
                    m2 = q * 2 + m2l
                    slot = m2l % 2
                    bks = [f"ps{slot * 2}", f"ps{slot * 2 + 1}"]
                    bk = [ps[slot * 2], ps[slot * 2 + 1]]

                    @blk.tensor
                    def _(t):
                        p.wait_at(t, w2s, w2_need)
                        p.wait_at(t, afs, act_all)
                        p.bank_wait(t, bks)
                        for k2 in range(11):
                            for hh in range(nh):
                                ins = t.matmul(
                                    bk[hh][:, :],
                                    w2b[buf2][:, k2 * 256 + m2l * 128:
                                              k2 * 256 + m2l * 128 + 128],
                                    act[:, k2 * T + hh * 512:
                                        k2 * T + hh * 512 + 512],
                                    start=(k2 == 0), stop=(k2 == 10))
                        p.inc(ins, mm)
                    mm_w2 = p.cnt[mm]

                    @blk.vector
                    def _(v):
                        p.wait_at(v, mm, mm_w2)
                        p.bank_wait(v, [f"res{m2}"])
                        for hh in range(nh):
                            ins = v.tensor_add(
                                resid_sb[:, m2 * T + hh * 512:
                                         m2 * T + hh * 512 + 512],
                                bk[hh][:, :],
                                resid_sb[:, m2 * T + hh * 512:
                                         m2 * T + hh * 512 + 512])
                            if hh < nh - 1:
                                p.edge(v, ins, vv)
                        p.inc(ins, vv)
                    p.bank_set(bks, vv)
                    p.bank_set([f"res{m2}"], vv)
                w2_free.append((mm, p.cnt[mm]))

            p.bank_set(["cbc"], afs)
        if dbg is not None:
            @blk.sync
            def _(sy):
                p.wait(sy, afs)
                p.inc(sy.dma_start(out=dbg["cbc"], in_=c_bc[:, :]), "std", 16)
                p.inc(sy.dma_start(out=dbg["sg"], in_=sg[:, :]), "std", 16)
                p.inc(sy.dma_start(out=dbg["ucs"], in_=ucs[:, :]), "std", 16)

            @blk.vector
            def _(v):
                p.wait(v, afs)
                p.inc(v.tensor_copy(sg[:, :], act[:, 10 * T:11 * T]), vv)

            @blk.sync
            def _(sy):
                p.wait(sy, vv)
                p.wait(sy, "std")
                p.inc(sy.dma_start(out=dbg["act"], in_=sg[:, :]),
                      "std", 16)
        done = p.cnt[vv]
    return vv, done


def prep_ffn_weights(ins):
    f32 = np.float32
    w1 = np.asarray(ins["L0_ffn_w1"], f32)
    w1t = w1.transpose(0, 2, 1)
    idx = np.empty(2816, np.int64)
    for i in range(11):
        idx[i * 256:i * 256 + 128] = np.arange(i * 128, (i + 1) * 128)
        idx[i * 256 + 128:(i + 1) * 256] = np.arange(1408 + i * 128,
                                                     1408 + (i + 1) * 128)
    w1p = np.ascontiguousarray(w1t[:, :, idx].reshape(E, 16, 128, 2816))
    w2 = np.asarray(ins["L0_ffn_w2"], f32)
    w2t = np.ascontiguousarray(
        w2.transpose(0, 2, 1).reshape(E, 11, 128, 2048))
    gw = np.asarray(ins["L0_ffn_gate_w"], f32)
    gateT = np.ascontiguousarray(gw.T.reshape(16, 128, 8))
    return w1p, w2t, gateT


def _bias_tiles(b, nchunk):
    b = np.asarray(b, np.float32)
    padded = np.zeros(nchunk * 128, np.float32)
    padded[:b.shape[0]] = b
    return np.ascontiguousarray(padded.reshape(nchunk, 128).T)


def build_A(debug=False):
    nc = bass.Bass("TRN2", target_bir_lowering=False)
    xT = nc.dram_tensor("xT", [16, 128, TPC], F32, kind="ExternalInput").ap()
    cpool_h = nc.dram_tensor("cpool", [128, 16], F32,
                             kind="ExternalInput").ap()
    ident_h = nc.dram_tensor("ident", [128, 128], F32,
                             kind="ExternalInput").ap()
    w1p = nc.dram_tensor("w1p", [E, 16, 128, 2816], F32,
                         kind="ExternalInput").ap()
    w2t = nc.dram_tensor("w2t", [E, 11, 128, 2048], F32,
                         kind="ExternalInput").ap()
    gateT = nc.dram_tensor("gateT", [16, 128, 8], F32,
                           kind="ExternalInput").ap()
    uT = nc.dram_tensor("uT", [16, 128, TPC], F32, kind="ExternalOutput").ap()
    if debug:
        dbg_xh = nc.dram_tensor("dbg_xh", [128, 16 * TPC], BF16,
                            kind="ExternalOutput").ap()
        dbg_lg = nc.dram_tensor("dbg_lg", [8, TPC], F32,
                                kind="ExternalOutput").ap()
        dbg_cT = nc.dram_tensor("dbg_cT", [1, E * TPC], F32,
                                kind="ExternalOutput").ap()
        dbg_act = nc.dram_tensor("dbg_act", [128, TPC], F32,
                                 kind="ExternalOutput").ap()
        dbg_cbc = nc.dram_tensor("dbg_cbc", [128, TPC], F32,
                                 kind="ExternalOutput").ap()
        dbg_sg = nc.dram_tensor("dbg_sg", [128, TPC], F32,
                                kind="ExternalOutput").ap()
        dbg_ucs = nc.dram_tensor("dbg_ucs", [128, TPC], F32,
                                 kind="ExternalOutput").ap()
        dbg_misc = nc.dram_tensor("dbg_misc", [128, 256], F32,
                                  kind="ExternalOutput").ap()

    with ExitStack() as stack:
        p = P(nc, stack)
        p.cpool = p.sbuf("cpool_sb", [128, 16], F32)
        p.ident_f32 = p.sbuf("ident_sb", [128, 128], F32)
        p.ones_f32 = p.sbuf("ones_f32", [128, 128], F32)
        p.ones_bf = p.sbuf("ones_bf", [128, 128], BF16)
        p.ones_col = p.sbuf("ones_col", [1, 128], F32)
        p.ones_col_bf = p.sbuf("ones_col_bf", [1, 128], BF16)
        x_sb = p.sbuf("x_sb", [128, 16 * TPC], F32)
        xh_sb = p.sbuf("xh_sb", [128, 16 * TPC], BF16)
        gate_sb = p.sbuf("gate_sb", [128, 16 * 8], BF16)
        p.new_sem("ldx", "ldw", "std", "crdy")
        blk = p.en(nc.Block())

        @blk.sync
        def _(sy):
            p.inc(sy.dma_start(out=p.cpool[:, :], in_=cpool_h), "ldx", 16)
            p.inc(sy.dma_start(out=p.ident_f32[:, :], in_=ident_h), "ldx", 16)
            p.inc(sy.dma_start(
                out=x_sb.ap().rearrange("pp (k m) -> pp k m", k=16),
                in_=xT.rearrange("k pp m -> pp k m")), "ldx", 16)

        @blk.gpsimd
        def _(g):
            p.inc(g.dma_start(
                out=gate_sb.ap().rearrange("pp (k m) -> pp k m", k=16),
                in_=gateT.rearrange("k pp m -> pp k m")), "ldw", 16)

        @blk.vector
        def _(v):
            p.wait(v, "ldx")
            p.wait(v, "ldw")
            v.memset(p.ones_f32[:, :], 1.0)
            v.memset(p.ones_bf[:, :], 1.0)
            v.memset(p.ones_col[:, :], 1.0)
            p.inc(v.memset(p.ones_col_bf[:, :], 1.0), "crdy")

        lnsem, _c = emit_layernorm(p, blk, "ln1", x_sb, xh_sb, TPC, ["crdy"])
        dbg = None
        if debug:
            dbg = {"xh": dbg_xh, "lg": dbg_lg, "cT": dbg_cT, "act": dbg_act,
                   "cbc": dbg_cbc, "sg": dbg_sg, "ucs": dbg_ucs,
                   "misc": dbg_misc}
        fsem, _c2 = emit_ffn(p, blk, "f1", xh_sb, x_sb, w1p, w2t, gate_sb,
                             TPC, [lnsem], dbg=dbg)

        @blk.sync
        def _(sy):
            p.wait(sy, fsem)
            p.inc(sy.dma_start(
                out=uT.rearrange("k pp m -> pp k m"),
                in_=x_sb.ap().rearrange("pp (k m) -> pp k m", k=16)),
                "std", 16)
            p.wait(sy, "std")
    return nc


def host_prep_common(inputs):
    x = np.asarray(inputs["L0_attn_x"], np.float32).reshape(S, D)
    xT = np.ascontiguousarray(x.T)
    w1p, w2t, gateT = prep_ffn_weights(inputs)
    cpool = _cpool_np()
    ident = np.eye(128, dtype=np.float32)
    return x, xT, w1p, w2t, gateT, cpool, ident


def run_launch_A(inputs):
    x, xT, w1p, w2t, gateT, cpool, ident = host_prep_common(inputs)
    nc = build_A()
    ims = []
    for c in range(NCORE):
        sl = xT[:, c * TPC:(c + 1) * TPC]
        ims.append({
            "xT": np.ascontiguousarray(sl.reshape(16, 128, TPC)),
            "cpool": cpool, "ident": ident,
            "w1p": w1p, "w2t": w2t, "gateT": gateT,
        })
    res = run_bass_kernel_spmd(nc, ims, core_ids=list(range(NCORE)))
    u = np.empty((D, S), np.float32)
    for c in range(NCORE):
        u[:, c * TPC:(c + 1) * TPC] = \
            np.asarray(res.results[c]["uT"]).reshape(D, TPC)
    return u




NKT = 64
DIAG_T = {0: list(range(56, 60)), 1: list(range(60, 64))}
LAST_T = {0: 60, 1: 64}


def build_B():
    nc = bass.Bass("TRN2", target_bir_lowering=False)
    ap = lambda *a, **k: nc.dram_tensor(*a, **k).ap()
    uf = ap("uf", [16, 128, S], F32, kind="ExternalInput")
    uo = ap("uo", [16, 128, TPC], F32, kind="ExternalInput")
    cpool_h = ap("cpool", [128, 16], F32, kind="ExternalInput")
    ident_h = ap("ident", [128, 128], F32, kind="ExternalInput")
    cosg_h = ap("cosg", [64, S], F32, kind="ExternalInput")
    sing_h = ap("sing", [64, S], F32, kind="ExternalInput")
    coso_h = ap("coso", [64, TPC], F32, kind="ExternalInput")
    sino_h = ap("sino", [64, TPC], F32, kind="ExternalInput")
    dmask_h = ap("dmask", [128, 128], F32, kind="ExternalInput")
    kmask_h = ap("kmask", [4, 128, 512], F32, kind="ExternalInput")
    meta_h = ap("meta", [1, 8], I32, kind="ExternalInput")
    wqaT_h = ap("wqaT", [16, 128, QLR], F32, kind="ExternalInput")
    wqab_h = ap("wqab", [128, 12], F32, kind="ExternalInput")
    qnw_h = ap("qnw", [128, 12], F32, kind="ExternalInput")
    wqbT_h = ap("wqbT", [H, 12, 128, 192], F32, kind="ExternalInput")
    wqbb_h = ap("wqbb", [H, 128, 2], F32, kind="ExternalInput")
    wkvaT_h = ap("wkvaT", [16, 128, 576], F32, kind="ExternalInput")
    wkvab_h = ap("wkvab", [128, 5], F32, kind="ExternalInput")
    kvnw_h = ap("kvnw", [128, 4], F32, kind="ExternalInput")
    wknT_h = ap("wknT", [H, 4, 128, 128], F32, kind="ExternalInput")
    wkvvT_h = ap("wkvvT", [H, 4, 128, 128], F32, kind="ExternalInput")
    woT_h = ap("woT", [16, 128, D], F32, kind="ExternalInput")
    wob_h = ap("wob", [128, 16], F32, kind="ExternalInput")
    w1p_h = ap("w1p", [E, 16, 128, 2816], F32, kind="ExternalInput")
    w2t_h = ap("w2t", [E, 11, 128, 2048], F32, kind="ExternalInput")
    gateT_h = ap("gateT", [16, 128, 8], F32, kind="ExternalInput")
    outT = ap("outT", [16, 128, TPC], F32, kind="ExternalOutput")

    kvfm_d = nc.dram_tensor("kvfm_d", [128, 4 * 2 * S], BF16, kind="Internal")
    kvfm_g = nc.dram_tensor("kvfm_g", [128, 4 * S], BF16, kind="Internal")
    kpe_g = nc.dram_tensor("kpe_g", [66, S], BF16, kind="Internal")
    zrow_d = nc.dram_tensor("zrow_d", [1, S], BF16, kind="Internal")
    mr2_d = nc.dram_tensor("mr2_d", [2, S], F32, kind="Internal")
    kpe_d = nc.dram_tensor("kpe_d", [66, 2 * S], BF16, kind="Internal")
    qn_d = nc.dram_tensor("qn_d", [H, 128, TPC], BF16, kind="Internal")
    qaug_d = nc.dram_tensor("qaug_d", [H, 66, TPC], BF16, kind="Internal")
    mrow_d = nc.dram_tensor("mrow_d", [8, 128], BF16, kind="Internal")

    with ExitStack() as stack:
        p = P(nc, stack)
        p.cpool = p.sbuf("cpool_sb", [128, 16], F32)
        p.ident_f32 = p.sbuf("ident_sb", [128, 128], F32)
        p.ident_bf = p.sbuf("identb_sb", [128, 128], BF16)
        p.ones_f32 = p.sbuf("ones_f32", [128, 128], F32)
        p.ones_bf = p.sbuf("ones_bf", [128, 128], BF16)
        p.ones_col = p.sbuf("ones_col", [1, 128], F32)
        p.ones_col_bf = p.sbuf("ones_col_bf", [1, 128], BF16)
        ones_row = p.sbuf("ones_row", [1, 512], BF16)
        zeros_row = p.sbuf("zeros_row", [1, 512], BF16)
        wob_sb = p.sbuf("wob_sb", [128, 16], F32)
        p.meta_sb = p.sbuf("meta_sb", [1, 8], I32)
        win_mark0 = p.sbuf("win_mark0", [128, 16], F32)
        p.new_sem("ldx", "ldw", "std", "crdy", "kvd", "win", "qpk", "odw")
        blk = p.en(nc.Block())

        @blk.sync
        def _(sy):
            p.inc(sy.dma_start(out=p.cpool[:, :], in_=cpool_h), "ldx", 16)
            p.inc(sy.dma_start(out=p.ident_f32[:, :], in_=ident_h), "ldx", 16)
            p.inc(sy.dma_start(out=wob_sb[:, :], in_=wob_h), "ldx", 16)
            p.inc(sy.dma_start(out=p.meta_sb[:, :], in_=meta_h), "ldx", 16)

        @blk.gpsimd
        def _(g):
            p.inc(g.dma_start(out=p.ident_bf[:, :], in_=ident_h), "ldw", 16)

        @blk.vector
        def _(v):
            p.wait(v, "ldx")
            p.wait(v, "ldw")
            v.memset(p.ones_f32[:, :], 1.0)
            v.memset(p.ones_bf[:, :], 1.0)
            v.memset(p.ones_col[:, :], 1.0)
            v.memset(p.ones_col_bf[:, :], 1.0)
            v.memset(zeros_row[:, :], 0.0)
            p.inc(v.memset(ones_row[:, :], 1.0), "crdy")

        negbig_row = p.sbuf("negbig_row", [1, 512], BF16)

        @blk.vector
        def _(v):
            p.wait(v, "crdy")
            p.inc(v.memset(negbig_row[:, :], -30000.0), "crdy")

        @blk.sync
        def _(sy):
            p.wait(sy, "crdy")
            for g_ in range(16):
                p.inc(sy.dma_start(
                    out=kpe_d.ap()[64:65, g_ * 512:(g_ + 1) * 512],
                    in_=ones_row[0:1, :]), "kvd", 16)
                p.inc(sy.dma_start(
                    out=kpe_d.ap()[65:66, g_ * 512:(g_ + 1) * 512],
                    in_=negbig_row[0:1, :]), "kvd", 16)

        with ExitStack() as b2:
            en2 = b2.enter_context
            ut = [en2(nc.sbuf_tensor(f"b2ut{i}", [128, 16 * 512], F32))
                  for i in range(2)]
            utb = [en2(nc.sbuf_tensor(f"b2utb{i}", [128, 16 * 512], BF16))
                   for i in range(2)]
            sqln = en2(nc.sbuf_tensor("b2sql", [128, 16 * 512], BF16))
            cosg = [en2(nc.sbuf_tensor(f"b2cos{i}", [64, 512], F32))
                    for i in range(2)]
            sing = [en2(nc.sbuf_tensor(f"b2sin{i}", [64, 512], F32))
                    for i in range(2)]
            wkva = en2(nc.sbuf_tensor("b2wkva", [128, 16 * 576], BF16))
            wkvab = en2(nc.sbuf_tensor("b2wkvab", [128, 5], F32))
            kvnw = en2(nc.sbuf_tensor("b2kvnw", [128, 4], F32))
            kvt = en2(nc.sbuf_tensor("b2kvt", [128, 4 * 512], F32))
            sqkv = en2(nc.sbuf_tensor("b2sq", [128, 4 * 512], BF16))
            rstd2 = en2(nc.sbuf_tensor("b2rstd", [128, 512], F32))
            t1b = en2(nc.sbuf_tensor("b2t1", [128, 512], F32))
            tln = en2(nc.sbuf_tensor("b2tln", [128, 512], F32))
            mrow2 = [en2(nc.sbuf_tensor(f"b2mr{i}", [128, 512], F32))
                     for i in range(2)]
            vrow2 = [en2(nc.sbuf_tensor(f"b2vr{i}", [128, 512], F32))
                     for i in range(2)]
            wsum = en2(nc.sbuf_tensor("b2wsum", [128, 5], F32))
            kvbf = en2(nc.sbuf_tensor("b2kvbf", [128, 4 * 512], BF16))
            kperaw = en2(nc.sbuf_tensor("b2kpr", [64, 512], F32))
            kperot = en2(nc.sbuf_tensor("b2krt", [64, 512], F32))
            kpea = en2(nc.sbuf_tensor("b2kpa", [64, 512], F32))
            kpebf = en2(nc.sbuf_tensor("b2kpb", [64, 512], BF16))
            psb = [en2(nc.psum_tensor(f"b2ps{i}", [128, 512], F32))
                   for i in range(6)]
            zfill = en2(nc.sbuf_tensor("b2zf", [128, 2048], BF16))
            p.new_sem("b2mm", "b2vv", "b2ss", "b2ld")

            @blk.vector
            def _(v):
                p.wait(v, "crdy")
                p.inc(v.memset(zfill[:, :], 0.0), "crdy")

            @blk.sync
            def _(sy):
                p.wait(sy, "crdy")
                p.inc(sy.dma_start(out=wkvab[:, :], in_=wkvab_h), "ldx", 16)
                p.inc(sy.dma_start(out=kvnw[:, :], in_=kvnw_h), "ldx", 16)
                for m in range(4):
                    for cc_ in range(4):
                        p.inc(sy.dma_start(
                            out=kvfm_d.ap()[:, m * 2 * S + cc_ * 2048:
                                            m * 2 * S + (cc_ + 1) * 2048],
                            in_=zfill[:, :]), "kvd", 16)
                for cc_ in range(16):
                    p.inc(sy.dma_start(
                        out=kpe_d.ap()[0:64, cc_ * 512:(cc_ + 1) * 512],
                        in_=zfill[0:64, 0:512]), "kvd", 16)

            @blk.gpsimd
            def _(g):
                p.wait(g, "crdy")
                p.inc(g.dma_start(
                    out=wkva.ap().rearrange("pp (k m) -> pp k m", k=16),
                    in_=wkvaT_h.rearrange("k pp m -> pp k m")), "b2ld", 16)

            @blk.tensor
            def _(t):
                p.wait(t, "b2ld")
                for m in range(5):
                    w = 128 if m < 4 else 64
                    for k in range(16):
                        ins = t.matmul(
                            psb[5][0:w, m:m + 1],
                            wkva[:, k * 576 + m * 128:k * 576 + m * 128 + w],
                            p.ones_bf[:, 0:1],
                            start=(k == 0), stop=(k == 15))
                p.inc(ins, "b2mm")
            wsum_mm = p.cnt["b2mm"]

            @blk.vector
            def _(v):
                p.wait_at(v, "b2mm", wsum_mm)
                p.wait(v, "ldx")
                p.inc(v.tensor_copy(wsum[:, :], psb[5][:, 0:5]), "b2vv")
            wsum_done = p.cnt["b2vv"]

            stats_mm = {}
            sq_done = {}
            kv_last = {}
            rope_done = {}
            store_done = {}
            statv_done = {}
            bank2 = {}
            for g_ in range(16):
                gb = g_ % 2

                @blk.sync
                def _(sy, g_=g_, gb=gb):
                    if g_ - 2 in stats_mm:
                        p.wait_at(sy, "b2mm", stats_mm[g_ - 2])
                        p.wait_at(sy, "b2vv", sq_done[g_ - 2])
                        p.wait_at(sy, "b2vv", rope_done[g_ - 2])
                    p.inc(sy.dma_start(
                        out=ut[gb].ap().rearrange("pp (k m) -> pp k m", k=16),
                        in_=uf[:, :, g_ * 512:(g_ + 1) * 512].rearrange(
                            "k pp m -> pp k m")), "ldx", 16)
                    p.inc(sy.dma_start(
                        out=cosg[gb][:, :],
                        in_=cosg_h[:, g_ * 512:(g_ + 1) * 512]), "ldx", 16)
                    p.inc(sy.dma_start(
                        out=sing[gb][:, :],
                        in_=sing_h[:, g_ * 512:(g_ + 1) * 512]), "ldx", 16)
                ld_need = p.cnt["ldx"]

                @blk.gpsimd
                def _(g, g_=g_, gb=gb):
                    if g_ - 2 in kv_last:
                        p.wait_at(g, "b2mm", kv_last[g_ - 2])
                    p.inc(g.dma_start(
                        out=utb[gb].ap().rearrange("pp (k m) -> pp k m",
                                                   k=16),
                        in_=uf[:, :, g_ * 512:(g_ + 1) * 512].rearrange(
                            "k pp m -> pp k m")), "b2ld", 16)
                ldb_need = p.cnt["b2ld"]

                @blk.vector
                def _(v, g_=g_, gb=gb, ld_need=ld_need):
                    p.wait_at(v, "ldx", ld_need)
                    if g_ - 1 in stats_mm:
                        p.wait_at(v, "b2mm", stats_mm[g_ - 1])
                    p.inc(v.tensor_mul(sqln[:, :], ut[gb][:, :],
                                       ut[gb][:, :]), "b2vv")
                sq_done[g_] = p.cnt["b2vv"]

                @blk.tensor
                def _(t, g_=g_, gb=gb, ld_need=ld_need):
                    p.wait_at(t, "ldx", ld_need)
                    p.wait_at(t, "b2vv", sq_done[g_])
                    if g_ - 1 in statv_done:
                        p.wait_at(t, "b2vv", statv_done[g_ - 1])
                    for k in range(16):
                        t.matmul(psb[3][:, :], p.ones_f32[:, :],
                                 ut[gb][:, k * 512:(k + 1) * 512],
                                 start=(k == 0), stop=(k == 15))
                        ins = t.matmul(psb[4][:, :], p.ones_bf[:, :],
                                       sqln[:, k * 512:(k + 1) * 512],
                                       start=(k == 0), stop=(k == 15))
                    p.inc(ins, "b2mm")
                stats_mm[g_] = p.cnt["b2mm"]

                @blk.vector
                def _(v, g_=g_, gb=gb):
                    p.wait_at(v, "b2mm", stats_mm[g_])
                    p.wait_at(v, "b2vv", wsum_done)
                    if g_ - 2 in store_done:
                        p.wait_at(v, "kvd", store_done[g_ - 2])
                    p.edge(v, v.tensor_scalar(mrow2[gb][:, :], psb[3][:, :],
                                              p.cp("inv_d"), None, ALU.mult),
                           "b2vv")
                    p.edge(v, v.tensor_mul(tln[:, :], mrow2[gb][:, :],
                                           mrow2[gb][:, :]), "b2vv")
                    p.inc(v.scalar_tensor_tensor(
                        vrow2[gb][:, :], psb[4][:, :], p.cp("inv_d"),
                        tln[:, :], ALU.mult, ALU.subtract), "b2vv")
                statv_done[g_] = p.cnt["b2vv"]

                @blk.scalar
                def _(s, g_=g_, gb=gb):
                    p.wait_at(s, "b2vv", statv_done[g_])
                    p.edge(s, s.activation(vrow2[gb][:, :], vrow2[gb][:, :],
                                           AF.Ln, bias=p.cp("ln_eps")),
                           "b2ss")
                    p.inc(s.activation(vrow2[gb][:, :], vrow2[gb][:, :],
                                       AF.Exp, scale=p.cp("neg_half")),
                          "b2ss")
                rstd_ss = p.cnt["b2ss"]

                @blk.tensor
                def _(t, g_=g_, gb=gb, ldb_need=ldb_need, rstd_ss=rstd_ss):
                    p.wait_at(t, "b2ld", ldb_need)
                    for m in range(5):
                        w = 128 if m < 4 else 64
                        key = f"kv{m % 2}"
                        if key in bank2:
                            t.wait_ge(p.sem["b2vv"], bank2[key])
                        for k in range(16):
                            ins = t.matmul(
                                psb[m % 2][0:w, :],
                                wkva[:, k * 576 + m * 128:
                                     k * 576 + m * 128 + w],
                                utb[gb][:, k * 512:(k + 1) * 512],
                                start=(k == 0), stop=(k == 15))
                        mmv = p.cnt["b2mm"] + 1
                        p.inc(ins, "b2mm")

                        @blk.vector
                        def _(v, m=m, mmv=mmv, gb=gb, rstd_ss=rstd_ss):
                            p.wait_at(v, "b2mm", mmv)
                            p.wait_at(v, "b2ss", rstd_ss)
                            if m < 4:
                                p.edge(v, v.tensor_scalar(
                                    tln[:, :], mrow2[gb][:, :],
                                    wsum[:, m:m + 1], None, ALU.mult),
                                    "b2vv")
                                p.edge(v, v.tensor_sub(
                                    t1b[:, :], psb[m % 2][:, :], tln[:, :]),
                                    "b2vv")
                                p.edge(v, v.tensor_mul(
                                    t1b[:, :], t1b[:, :], vrow2[gb][:, :]),
                                    "b2vv")
                                p.edge(v, v.tensor_scalar(
                                    kvt[:, m * 512:(m + 1) * 512], t1b[:, :],
                                    wkvab[:, m:m + 1], None, ALU.add),
                                    "b2vv")
                                p.inc(v.tensor_mul(
                                    sqkv[:, m * 512:(m + 1) * 512],
                                    kvt[:, m * 512:(m + 1) * 512],
                                    kvt[:, m * 512:(m + 1) * 512]), "b2vv")
                            else:
                                p.edge(v, v.tensor_scalar(
                                    tln[0:64, :], mrow2[gb][0:64, :],
                                    wsum[0:64, 4:5], None, ALU.mult), "b2vv")
                                p.edge(v, v.tensor_sub(
                                    t1b[0:64, :], psb[m % 2][0:64, :],
                                    tln[0:64, :]), "b2vv")
                                p.edge(v, v.tensor_mul(
                                    t1b[0:64, :], t1b[0:64, :],
                                    vrow2[gb][0:64, :]), "b2vv")
                                p.inc(v.tensor_scalar(
                                    kperaw[:, :], t1b[0:64, :],
                                    wkvab[0:64, 4:5], None, ALU.add), "b2vv")
                        bank2[key] = p.cnt["b2vv"]
                kv_last[g_] = p.cnt["b2mm"]
                kv_done = p.cnt["b2vv"]

                @blk.tensor
                def _(t, g_=g_, kv_done=kv_done):
                    p.wait_at(t, "b2vv", kv_done)
                    if "r2" in bank2:
                        t.wait_ge(p.sem["b2vv"], bank2["r2"])
                    for m in range(4):
                        ins = t.matmul(psb[2][:, :], p.ones_bf[:, :],
                                       sqkv[:, m * 512:(m + 1) * 512],
                                       start=(m == 0), stop=(m == 3))
                    p.inc(ins, "b2mm")

                @blk.scalar
                def _(s, g_=g_):
                    p.wait(s, "b2mm")
                    p.edge(s, s.activation(rstd2[:, :], psb[2][:, :], AF.Ln,
                                           bias=p.cp("rms_eps"),
                                           scale=p.cp("inv_kvlr")), "b2ss")
                    p.inc(s.activation(rstd2[:, :], rstd2[:, :], AF.Exp,
                                       scale=p.cp("neg_half")), "b2ss")

                @blk.vector
                def _(v, g_=g_):
                    p.wait(v, "b2ss")
                    for m in range(4):
                        p.edge(v, v.tensor_mul(t1b[:, :],
                                               kvt[:, m * 512:(m + 1) * 512],
                                               rstd2[:, :]), "b2vv")
                        ins = v.tensor_scalar(kvbf[:, m * 512:(m + 1) * 512],
                                              t1b[:, :], kvnw[:, m:m + 1],
                                              None, ALU.mult)
                        if m < 3:
                            p.edge(v, ins, "b2vv")
                    p.inc(ins, "b2vv")
                bank2["r2"] = p.cnt["b2vv"]

                @blk.scalar
                def _(s, g_=g_):
                    p.wait(s, "b2vv")
                    p.edge(s, s.mul(kperot[0:32, :], kperaw[32:64, :],
                                    p.cp("neg_one")[0:32, :]), "b2ss")
                    p.inc(s.copy(kperot[32:64, :], kperaw[0:32, :]), "b2ss")

                @blk.vector
                def _(v, g_=g_, gb=gb):
                    p.wait(v, "b2ss")
                    p.edge(v, v.tensor_mul(
                        kpea[:, :], kperaw[:, :], cosg[gb][:, :]), "b2vv")
                    p.edge(v, v.tensor_mul(
                        kperot[:, :], kperot[:, :], sing[gb][:, :]), "b2vv")
                    p.inc(v.tensor_add(kpebf[:, :], kpea[:, :],
                                       kperot[:, :]), "b2vv")
                rope_done[g_] = p.cnt["b2vv"]

                @blk.sync
                def _(sy, g_=g_, gb=gb):
                    p.wait(sy, "b2vv")
                    p.wait(sy, "b2ss")
                    p.inc(sy.dma_start(
                        out=kvfm_g.ap().rearrange(
                            "pp (m t) -> pp m t", m=4)[:, :,
                                                       g_ * 512:(g_ + 1) * 512],
                        in_=kvbf.ap().rearrange("pp (m t) -> pp m t", m=4)),
                        "kvd", 16)
                    p.inc(sy.dma_start(
                        out=kpe_g.ap()[0:64, g_ * 512:(g_ + 1) * 512],
                        in_=kpebf[:, :]), "kvd", 16)
                    p.inc(sy.dma_start(
                        out=mr2_d.ap()[0:1, g_ * 512:(g_ + 1) * 512],
                        in_=mrow2[gb][0:1, :]), "kvd", 16)
                    p.inc(sy.dma_start(
                        out=mr2_d.ap()[1:2, g_ * 512:(g_ + 1) * 512],
                        in_=vrow2[gb][0:1, :]), "kvd", 16)
                    if g_ == 0:
                        for cc_ in range(16):
                            p.inc(sy.dma_start(
                                out=kpe_g.ap()[65:66,
                                               cc_ * 512:(cc_ + 1) * 512],
                                in_=zeros_row[0:1, :]), "kvd", 16)
                store_done[g_] = p.cnt["kvd"]

        with ExitStack() as b4:
            en4 = b4.enter_context
            coso = en4(nc.sbuf_tensor("b4cos", [64, TPC], F32))
            sino = en4(nc.sbuf_tensor("b4sin", [64, TPC], F32))
            wqab = en4(nc.sbuf_tensor("b4wqab", [128, 12], F32))
            qnw = en4(nc.sbuf_tensor("b4qnw", [128, 12], F32))
            rstd4 = en4(nc.sbuf_tensor("b4rstd", [128, TPC], F32))
            mroM = en4(nc.sbuf_tensor("b4mroM", [1, TPC], F32))
            mroR = en4(nc.sbuf_tensor("b4mroR", [1, TPC], F32))
            m4bc = en4(nc.sbuf_tensor("b4mbc", [128, TPC], F32))
            r4bc = en4(nc.sbuf_tensor("b4rbc", [128, TPC], F32))
            wsumq = en4(nc.sbuf_tensor("b4wsq", [128, 12], F32))
            t14 = en4(nc.sbuf_tensor("b4t1", [128, TPC], F32))
            qn = en4(nc.sbuf_tensor("b4qn", [128, 12 * TPC], BF16))
            ps4 = [en4(nc.psum_tensor(f"b4ps{i}", [128, 512], F32))
                   for i in range(6)]
            p.new_sem("b4mm", "b4vv", "b4ss", "b4w")
            b4s1 = ExitStack()
            en41 = b4s1.enter_context
            qlo = en41(nc.sbuf_tensor("b4qlo", [128, 12 * TPC], BF16))
            sq4 = en41(nc.sbuf_tensor("b4sq", [128, 12 * TPC], BF16))
            xho = en41(nc.sbuf_tensor("b4xh", [128, 16 * TPC], BF16))
            wqa = [en41(nc.sbuf_tensor(f"b4wqa{i}", [128, 16 * 512], BF16))
                   for i in range(2)]

            @blk.sync
            def _(sy):
                p.wait(sy, "kvd")
                p.inc(sy.dma_start(out=coso[:, :], in_=coso_h), "ldx", 16)
                p.inc(sy.dma_start(out=sino[:, :], in_=sino_h), "ldx", 16)
                p.inc(sy.dma_start(out=wqab[:, :], in_=wqab_h), "ldx", 16)
                p.inc(sy.dma_start(out=qnw[:, :], in_=qnw_h), "ldx", 16)
                with sy.register("ost") as r2:
                    sy.reg_load(r2, p.meta_sb[0:1, 5:6])
                    ot2 = sy.snap(r2, min_val=0, max_val=14)
                    mrv = mr2_d.ap().rearrange("r (a m) -> r a m", m=512)
                    p.inc(sy.dma_start(
                        out=mroM.ap().rearrange("r (a m) -> r a m", a=2),
                        in_=mrv[0:1, bass.ds(ot2, 2), :]), "ldx", 16)
                    p.inc(sy.dma_start(
                        out=mroR.ap().rearrange("r (a m) -> r a m", a=2),
                        in_=mrv[1:2, bass.ds(ot2, 2), :]), "ldx", 16)
            mro_need = p.cnt["ldx"]

            @blk.gpsimd
            def _(g):
                p.wait(g, "kvd")
                p.inc(g.dma_start(
                    out=xho.ap().rearrange("pp (k m) -> pp k m", k=16),
                    in_=uo.rearrange("k pp m -> pp k m")), "b4w", 16)
            xho_need = p.cnt["b4w"]

            @blk.tensor
            def _(t, mro_need=mro_need):
                p.wait_at(t, "ldx", mro_need)
                for r_ in range(2):
                    src = mroM if r_ == 0 else mroR
                    for a_ in range(2):
                        p.inc(t.matmul(
                            ps4[2 + a_][:, :], p.ones_col[0:1, :],
                            src[0:1, a_ * 512:a_ * 512 + 512],
                            start=True, stop=True), "b4mm")
                    mmb = p.cnt["b4mm"]

                    @blk.vector
                    def _(v, r_=r_, mmb=mmb):
                        p.wait_at(v, "b4mm", mmb)
                        dst = m4bc if r_ == 0 else r4bc
                        for a_ in range(2):
                            ins = v.tensor_copy(
                                dst[:, a_ * 512:a_ * 512 + 512],
                                ps4[2 + a_][:, :])
                            if a_ == 0:
                                p.edge(v, ins, "b4vv")
                        p.inc(ins, "b4vv")
                    vvb = p.cnt["b4vv"]
                    t.wait_ge(p.sem["b4vv"], vvb)
            bc_done = p.cnt["b4vv"]

            lnr3 = "ldx"
            bank_free = {}

            for sl in range(3):
                @blk.gpsimd
                def _(g, sl=sl):
                    if sl >= 2:
                        p.wait_at(g, "b4mm", p.cnt["b4mm"])
                    else:
                        p.wait(g, "kvd")
                    p.inc(g.dma_start(
                        out=wqa[sl % 2].ap().rearrange(
                            "pp (k m) -> pp k m", k=16),
                        in_=wqaT_h[:, :, sl * 512:(sl + 1) * 512].rearrange(
                            "k pp m -> pp k m")), "b4w", 16)
                w_need = p.cnt["b4w"]

                @blk.tensor
                def _(t, sl=sl, w_need=w_need):
                    p.wait_at(t, "b4w", w_need)
                    p.wait_at(t, "b4w", xho_need)
                    p.wait(t, lnr3)
                    if "r5" in bank_free:
                        t.wait_ge(p.sem["b4ss"], bank_free["r5"])
                    for ml in range(4):
                        m = sl * 4 + ml
                        for k in range(16):
                            ins = t.matmul(
                                ps4[5][:, m:m + 1],
                                wqa[sl % 2][:, k * 512 + ml * 128:
                                            k * 512 + ml * 128 + 128],
                                p.ones_bf[:, 0:1],
                                start=(k == 0), stop=(k == 15))
                    mmw = p.cnt["b4mm"] + 1
                    p.inc(ins, "b4mm")

                    @blk.vector
                    def _(v, sl=sl, mmw=mmw):
                        p.wait_at(v, "b4mm", mmw)
                        p.inc(v.tensor_copy(wsumq[:, sl * 4:sl * 4 + 4],
                                            ps4[5][:, sl * 4:sl * 4 + 4]),
                              "b4ss")
                    bank_free["r5"] = p.cnt["b4ss"]
                    ws_done = p.cnt["b4ss"]
                    for ml in range(4):
                        m = sl * 4 + ml
                        for hh in range(NH):
                            key = f"ql{ml % 2}"
                            if key in bank_free:
                                t.wait_ge(p.sem["b4ss"], bank_free[key])
                            bk = ps4[ml % 2]
                            for k in range(16):
                                ins = t.matmul(
                                    bk[:, :],
                                    wqa[sl % 2][:, k * 512 + ml * 128:
                                                k * 512 + ml * 128 + 128],
                                    xho[:, k * TPC + hh * 512:
                                        k * TPC + hh * 512 + 512],
                                    start=(k == 0), stop=(k == 15))
                            mmv = p.cnt["b4mm"] + 1
                            p.inc(ins, "b4mm")

                            @blk.vector
                            def _(v, m=m, hh=hh, mmv=mmv, bk=bk,
                                  ws_done=ws_done):
                                p.wait_at(v, "b4mm", mmv)
                                p.wait_at(v, "b4ss", ws_done)
                                p.wait_at(v, "b4vv", bc_done)
                                p.edge(v, v.tensor_scalar(
                                    t14[:, 0:512],
                                    m4bc[:, hh * 512:hh * 512 + 512],
                                    wsumq[:, m:m + 1], None, ALU.mult),
                                    "b4ss")
                                p.edge(v, v.tensor_sub(
                                    t14[:, 0:512], bk[:, :], t14[:, 0:512]),
                                    "b4ss")
                                p.edge(v, v.tensor_mul(
                                    t14[:, 0:512], t14[:, 0:512],
                                    r4bc[:, hh * 512:hh * 512 + 512]),
                                    "b4ss")
                                p.edge(v, v.tensor_scalar(
                                    qlo[:, m * TPC + hh * 512:
                                        m * TPC + hh * 512 + 512],
                                    t14[:, 0:512], wqab[:, m:m + 1],
                                    None, ALU.add), "b4ss")
                                p.inc(v.tensor_mul(
                                    sq4[:, m * TPC + hh * 512:
                                        m * TPC + hh * 512 + 512],
                                    qlo[:, m * TPC + hh * 512:
                                        m * TPC + hh * 512 + 512],
                                    qlo[:, m * TPC + hh * 512:
                                        m * TPC + hh * 512 + 512]), "b4ss")
                            bank_free[key] = p.cnt["b4ss"]

            @blk.tensor
            def _(t):
                p.wait(t, "b4ss")
                for hh in range(NH):
                    if "r5" in bank_free:
                        t.wait_ge(p.sem["b4ss"], bank_free["r5"])
                    for m in range(12):
                        ins = t.matmul(ps4[5][:, :], p.ones_bf[:, :],
                                       sq4[:, m * TPC + hh * 512:
                                           m * TPC + hh * 512 + 512],
                                       start=(m == 0), stop=(m == 11))
                    mmv = p.cnt["b4mm"] + 1
                    p.inc(ins, "b4mm")

                    @blk.scalar
                    def _(s, hh=hh, mmv=mmv):
                        p.wait_at(s, "b4mm", mmv)
                        p.edge(s, s.activation(
                            rstd4[:, hh * 512:hh * 512 + 512], ps4[5][:, :],
                            AF.Ln, bias=p.cp("rms_eps"),
                            scale=p.cp("inv_qlr")), "b4ss")
                        p.inc(s.activation(
                            rstd4[:, hh * 512:hh * 512 + 512],
                            rstd4[:, hh * 512:hh * 512 + 512],
                            AF.Exp, scale=p.cp("neg_half")), "b4ss")
                    bank_free["r5"] = p.cnt["b4ss"]

            @blk.vector
            def _(v):
                p.wait(v, "b4ss")
                for m in range(12):
                    p.edge(v, v.tensor_mul(t14[:, :],
                                           qlo[:, m * TPC:(m + 1) * TPC],
                                           rstd4[:, :]), "b4vv")
                    ins = v.tensor_scalar(
                        qn[:, m * TPC:(m + 1) * TPC], t14[:, :],
                        qnw[:, m:m + 1], None, ALU.mult)
                    if m < 11:
                        p.edge(v, ins, "b4vv")
                p.inc(ins, "b4vv")
            qn_done = p.cnt["b4vv"]

            @blk.sync
            def _(sy):
                p.wait(sy, "b4vv")
                p.wait(sy, "b4ss")
                p.inc(sy.dma_start(out=win_mark0[:, :], in_=cpool_h),
                      "qpk", 16)
            b4s1.close()
            wqb = [en4(nc.sbuf_tensor(f"b4wqb{i}", [128, 12 * 192], BF16))
                   for i in range(2)]
            wqbb = en4(nc.sbuf_tensor("b4wqbb", [128, 2], F32))
            qpe = en4(nc.sbuf_tensor("b4qpe", [64, TPC], F32))
            qper = en4(nc.sbuf_tensor("b4qper", [64, TPC], F32))
            qnope = en4(nc.sbuf_tensor("b4qnope", [128, TPC], BF16))
            qaug = en4(nc.sbuf_tensor("b4qaug", [66, TPC], BF16))

            @blk.sync
            def _(sy):
                p.wait(sy, "qpk")
                for cc_ in range(2):
                    p.inc(sy.dma_start(
                        out=qaug.ap()[65:66, cc_ * 512:(cc_ + 1) * 512],
                        in_=ones_row[0:1, :]), "qpk", 16)

            for h in range(H):
                @blk.gpsimd
                def _(g, h=h):
                    if h >= 2:
                        p.wait_at(g, "b4mm", p.cnt["b4mm"])
                    else:
                        p.wait(g, "qpk")
                    p.inc(g.dma_start(
                        out=wqb[h % 2].ap().rearrange(
                            "pp (k m) -> pp k m", k=12),
                        in_=wqbT_h[h].rearrange("k pp m -> pp k m")),
                        "b4w", 16)

                @blk.sync
                def _(sy, h=h):
                    p.wait(sy, "qpk")
                    p.inc(sy.dma_start(out=wqbb[:, :], in_=wqbb_h[h]),
                          "ldx", 16)
                wqbb_need = p.cnt["ldx"]
                w_need = p.cnt["b4w"]

                @blk.vector
                def _(v, h=h, wqbb_need=wqbb_need):
                    p.wait_at(v, "ldx", wqbb_need)
                    p.wait(v, "b4vv")
                    p.inc(v.tensor_scalar(wqbb[:, :], wqbb[:, :],
                                          p.cp("scale"), None, ALU.mult),
                          "b4vv")
                bias_done = p.cnt["b4vv"]

                @blk.tensor
                def _(t, h=h, w_need=w_need, bias_done=bias_done):
                    p.wait_at(t, "b4w", w_need)
                    p.wait_at(t, "b4vv", max(qn_done, bias_done))
                    p.wait(t, "qpk")
                    for mq in range(2):
                        w = 128 if mq == 0 else 64
                        for hh in range(NH):
                            key = f"qf{mq}"
                            if key in bank_free:
                                t.wait_ge(p.sem["b4ss"], bank_free[key])
                            for k in range(12):
                                ins = t.matmul(
                                    ps4[2 + mq][0:w, :],
                                    wqb[h % 2][:, k * 192 + mq * 128:
                                               k * 192 + mq * 128 + w],
                                    qn[:, k * TPC + hh * 512:
                                       k * TPC + hh * 512 + 512],
                                    start=(k == 0), stop=(k == 11))
                            mmv = p.cnt["b4mm"] + 1
                            p.inc(ins, "b4mm")

                            @blk.vector
                            def _(v, mq=mq, w=w, hh=hh, mmv=mmv):
                                p.wait_at(v, "b4mm", mmv)
                                if mq == 0:
                                    p.inc(v.tensor_scalar(
                                        qnope[:, hh * 512:hh * 512 + 512],
                                        ps4[2][:, :], p.cp("scale"),
                                        wqbb[:, 0:1], ALU.mult, ALU.add),
                                        "b4ss")
                                else:
                                    p.inc(v.tensor_scalar(
                                        qpe[:, hh * 512:hh * 512 + 512],
                                        ps4[3][0:64, :],
                                        p.cp("scale")[0:64, :],
                                        wqbb[0:64, 1:2], ALU.mult, ALU.add),
                                        "b4ss")
                            bank_free[key] = p.cnt["b4ss"]
                qf_done = p.cnt["b4ss"]

                @blk.scalar
                def _(s, h=h, qf_done=qf_done):
                    p.wait_at(s, "b4ss", qf_done)
                    p.wait(s, "qpk")
                    p.edge(s, s.mul(qper[0:32, :], qpe[32:64, :],
                                    p.cp("neg_one")[0:32, :]), "b4ss")
                    p.inc(s.copy(qper[32:64, :], qpe[0:32, :]), "b4ss")

                @blk.vector
                def _(v, h=h):
                    p.wait(v, "b4ss")
                    p.wait(v, "qpk")
                    p.edge(v, v.tensor_mul(qpe[:, :], qpe[:, :], coso[:, :]),
                           "b4vv")
                    p.edge(v, v.tensor_mul(qper[:, :], qper[:, :],
                                           sino[:, :]), "b4vv")
                    p.edge(v, v.tensor_add(qpe[:, :], qpe[:, :], qper[:, :]),
                           "b4vv")
                    p.inc(v.tensor_copy(qaug[0:64, :], qpe[:, :]), "b4vv")

                @blk.sync
                def _(sy, h=h):
                    p.wait(sy, "b4ss")
                    p.wait(sy, "b4vv")
                    p.inc(sy.dma_start(
                        out=qaug_d.ap()[h, 0:64, :], in_=qaug[0:64, :]),
                        "qpk", 16)
                    p.inc(sy.dma_start(
                        out=qaug_d.ap()[h, 65:66, :], in_=qaug[65:66, :]),
                        "qpk", 16)
                    p.inc(sy.dma_start(out=qn_d.ap()[h], in_=qnope[:, :]),
                          "qpk", 16)

        o_sb = p.sbuf("o_sbr", [128, 16 * TPC], BF16)
        with ExitStack() as b5:
            en5 = b5.enter_context
            kvfm = en5(nc.sbuf_tensor("kvfm", [128, 4 * S], BF16))
            kpew = en5(nc.sbuf_tensor("kpew", [66, S], BF16))
            knop = en5(nc.sbuf_tensor("b5kn", [128, S], BF16))
            vsb = en5(nc.sbuf_tensor("b5vs", [128, S], BF16))
            wkvvT = en5(nc.sbuf_tensor("wkvvT_sb", [128, H * 4 * 128], BF16))
            wknT = en5(nc.sbuf_tensor("wknT_sb", [128, H * 4 * 128], BF16))
            kmask = en5(nc.sbuf_tensor("kmask_sb", [128, 4 * 512], BF16))
            dmask5 = en5(nc.sbuf_tensor("b5dm", [128, 128], F32))
            qns4 = [en5(nc.sbuf_tensor(f"b5qn{i}", [128, TPC], BF16))
                    for i in range(2)]
            qgs4 = [en5(nc.sbuf_tensor(f"b5qg{i}", [66, TPC], BF16))
                    for i in range(2)]
            probs = [en5(nc.sbuf_tensor(f"b5pr{i}", [128, 512], BF16))
                     for i in range(3)]
            denr = en5(nc.sbuf_tensor("b5den", [128, 512], F32))
            m8 = en5(nc.sbuf_tensor("b5m8", [128, 8], F32))
            m8b = en5(nc.sbuf_tensor("b5m8b", [128, 8], BF16))
            mr = en5(nc.sbuf_tensor("b5mr", [8, 128], BF16))
            tdg = en5(nc.sbuf_tensor("b5td", [128, 128], F32))
            win_mark = en5(nc.sbuf_tensor("win_mark", [128, 16], F32))
            psS = [en5(nc.psum_tensor(f"b5s{i}", [128, 512], F32))
                   for i in range(2)]
            psO = [en5(nc.psum_tensor(f"b5o{i}", [128, 512], F32))
                   for i in range(2)]
            psD = en5(nc.psum_tensor("b5d", [128, 512], F32))
            psE = [en5(nc.psum_tensor(f"b5e{i}", [128, 512], F32))
                   for i in range(2)]
            ptm = en5(nc.psum_tensor("b5pt", [128, 128], BF16))
            p.new_sem("b5mm", "b5vv", "b5ss", "b5q", "qgr")

            @blk.gpsimd
            def _(g):
                p.wait(g, "qpk")
                p.inc(g.dma_start(
                    out=wkvvT.ap().rearrange("pp (h c m) -> pp h c m",
                                             h=H, c=4),
                    in_=wkvvT_h.rearrange("h c pp m -> pp h c m")),
                    "ldw", 16)
                p.inc(g.dma_start(
                    out=wknT.ap().rearrange("pp (h c m) -> pp h c m",
                                            h=H, c=4),
                    in_=wknT_h.rearrange("h c pp m -> pp h c m")), "ldw", 16)
                p.inc(g.dma_start(
                    out=kmask.ap().rearrange("pp (r m) -> pp r m", r=4),
                    in_=kmask_h.rearrange("r pp m -> pp r m")), "ldw", 16)

            @blk.sync
            def _(sy):
                p.wait(sy, "qpk")
                p.inc(sy.dma_start(out=dmask5[:, :], in_=dmask_h), "ldx", 16)
            dm_need = p.cnt["ldx"]

            @blk.vector
            def _(v):
                p.wait(v, "qpk")
                p.inc(v.memset(win_mark[:, :], 0.0), "b5vv")

            @blk.sync
            def _(sy):
                p.wait(sy, "b5vv")
                p.wait(sy, "kvd")
                p.wait(sy, "ldx")
                winfm = kvfm_d.ap().rearrange("pp (b t) -> pp b t", t=512)
                glofm = kvfm_g.ap().rearrange("pp (b t) -> pp b t", t=512)
                winpe = kpe_d.ap().rearrange("pp (b t) -> pp b t", t=512)
                glope = kpe_g.ap().rearrange("pp (b t) -> pp b t", t=512)
                with sy.register("pB") as r0:
                    sy.reg_load(r0, p.meta_sb[0:1, 3:4])
                    padB = sy.snap(r0, min_val=0, max_val=(S - TPC) // 512)
                    for m in range(4):
                        p.inc(sy.dma_start(
                            out=winfm[:, bass.ds(padB + m * 32, 16), :],
                            in_=glofm[:, m * 16:(m + 1) * 16, :]),
                            "win", 16)
                    p.inc(sy.dma_start(
                        out=winpe[0:64, bass.ds(padB, 16), :],
                        in_=glope[0:64, :, :]), "win", 16)
                    p.inc(sy.dma_start(
                        out=winpe[65:66, bass.ds(padB, 16), :],
                        in_=glope[65:66, :, :]), "win", 16)
                p.wait(sy, "win")
                p.inc(sy.dma_start(
                    out=kvfm.ap().rearrange("pp (m t) -> pp m t", m=4),
                    in_=kvfm_d.ap().rearrange(
                        "pp (m t) -> pp m t", m=4)[:, :, 0:S]), "odw", 16)
                p.inc(sy.dma_start(out=kpew.ap()[:, :],
                                   in_=kpe_d.ap()[:, 0:S]), "odw", 16)

            eg = [0]

            def emit_kng(g_, h, first=False):
                par = eg[0] % 2
                eg[0] += 1

                @blk.tensor
                def _(t, g_=g_, h=h, par=par, first=first):
                    if first:
                        p.wait(t, "odw")
                        p.wait(t, "ldw")
                    p.bank_wait(t, [f"b5e{par}"])
                    for c in range(4):
                        ins = t.matmul(
                            psE[par][:, :],
                            wknT[:, h * 512 + c * 128:h * 512 + c * 128 + 128],
                            kvfm[:, c * S + g_ * 512:c * S + g_ * 512 + 512],
                            start=(c == 0), stop=(c == 3))
                    p.inc(ins, "b5mm")
                mmv = p.cnt["b5mm"]

                @blk.vector
                def _(v, g_=g_, par=par, mmv=mmv):
                    p.wait_at(v, "b5mm", mmv)
                    p.inc(v.tensor_copy(
                        knop[:, g_ * 512:(g_ + 1) * 512], psE[par][:, :]),
                        "b5vv")
                p.bank_set([f"b5e{par}"], "b5vv")

            def emit_vg(g_, h):
                par = eg[0] % 2
                eg[0] += 1

                @blk.tensor
                def _(t, g_=g_, h=h, par=par):
                    p.bank_wait(t, [f"b5e{par}"])
                    for j in range(4):
                        for c in range(4):
                            ins = t.matmul(
                                psE[par][:, j * 128:j * 128 + 128],
                                kvfm[:, c * S + g_ * 512 + j * 128:
                                     c * S + g_ * 512 + j * 128 + 128],
                                wkvvT[:, h * 512 + c * 128:
                                      h * 512 + c * 128 + 128],
                                start=(c == 0), stop=(c == 3))
                    p.inc(ins, "b5mm")
                mmv = p.cnt["b5mm"]

                @blk.vector
                def _(v, g_=g_, par=par, mmv=mmv):
                    p.wait_at(v, "b5mm", mmv)
                    p.inc(v.tensor_copy(
                        vsb[:, g_ * 512:(g_ + 1) * 512], psE[par][:, :]),
                        "b5vv")
                p.bank_set([f"b5e{par}"], "b5vv")

            for h in range(H):
                par_h = h % 2

                @blk.sync
                def _(sy, h=h, par_h=par_h):
                    if h < 2:
                        p.wait(sy, "qpk")
                    p.bank_wait(sy, [f"b5qb{par_h}"])
                    p.inc(sy.dma_start(out=qns4[par_h][:, :],
                                       in_=qn_d.ap()[h]), "b5q", 16)
                    p.inc(sy.dma_start(out=qgs4[par_h][:, :],
                                       in_=qaug_d.ap()[h]), "b5q", 16)
                q_need = p.cnt["b5q"]

                emit_kng(14, h, first=(h == 0))
                emit_kng(15, h)
                kn_tail = p.cnt["b5vv"]

                for qb in range(8):
                    sbq, j = qb // 4, qb % 4
                    td_ = DIAG_T[sbq][j]
                    par = eg[0] % 2
                    eg[0] += 1

                    @blk.tensor
                    def _(t, h=h, qb=qb, sbq=sbq, j=j, td_=td_, par=par,
                          q_need=q_need, kn_tail=kn_tail, par_h=par_h):
                        if qb == 0:
                            p.wait_at(t, "b5q", q_need)
                            p.wait_at(t, "b5vv", kn_tail)
                        p.bank_wait(t, [f"b5e{par}"])
                        t.matmul(psE[par][:, 0:128],
                                 qns4[par_h][:, sbq * 512 + j * 128:
                                             sbq * 512 + j * 128 + 128],
                                 knop[:, td_ * 128:td_ * 128 + 128],
                                 start=True, stop=False)
                        ins = t.matmul(
                            psE[par][:, 0:128],
                            qgs4[par_h][0:64, sbq * 512 + j * 128:
                                        sbq * 512 + j * 128 + 128],
                            kpew[0:64, td_ * 128:td_ * 128 + 128],
                            start=False, stop=True)
                        p.inc(ins, "b5mm")
                    mmv = p.cnt["b5mm"]

                    @blk.vector
                    def _(v, qb=qb, par=par, mmv=mmv):
                        p.wait_at(v, "b5mm", mmv)
                        p.wait_at(v, "ldx", dm_need)
                        p.edge(v, v.tensor_add(tdg[:, :],
                                               psE[par][:, 0:128],
                                               dmask5[:, :]), "b5vv")
                        p.inc(v.reduce_max(m8[:, qb:qb + 1], tdg[:, :],
                                           axis=AX.X), "b5vv")
                    p.bank_set([f"b5e{par}"], "b5vv")

                @blk.vector
                def _(v, h=h):
                    p.wait(v, "b5vv")
                    p.inc(v.tensor_scalar(m8b[:, :], m8[:, :],
                                          p.cp("neg_one"), None, ALU.mult),
                          "b5vv")
                neg_done = p.cnt["b5vv"]

                @blk.tensor
                def _(t, h=h, neg_done=neg_done):
                    p.wait_at(t, "b5vv", neg_done)
                    p.bank_wait(t, ["b5pt"])
                    p.inc(t.transpose(ptm[0:8, :], m8b[:, :],
                                      p.ident_bf[:, :]), "b5mm")
                mmv2 = p.cnt["b5mm"]

                @blk.vector
                def _(v, h=h, mmv2=mmv2):
                    p.wait_at(v, "b5mm", mmv2)
                    p.bank_wait(v, ["b5mr"])
                    p.inc(v.tensor_copy(mr[:, :], ptm[0:8, :]), "b5vv")
                p.bank_set(["b5pt"], "b5vv")
                mr_done = p.cnt["b5vv"]

                @blk.sync
                def _(sy, h=h, mr_done=mr_done):
                    p.wait_at(sy, "b5vv", mr_done)
                    p.wait(sy, "qgr")
                    p.inc(sy.dma_start(out=mrow_d.ap(), in_=mr[:, :]),
                          "b5q", 16)
                p.bank_set(["b5mr"], "b5q")
                mrd = p.cnt["b5q"]

                @blk.sync
                def _(sy, h=h, mrd=mrd, par_h=par_h):
                    p.wait_at(sy, "b5q", mrd)
                    p.inc(sy.dma_start(
                        out=qgs4[par_h].ap()[64:65, :],
                        in_=mrow_d.ap().rearrange("(a b) q -> a (b q)", a=1)),
                        "qgr", 16)
                qgr_need = p.cnt["qgr"]

                for g_ in range(14):
                    emit_kng(g_, h)
                kn_all = p.cnt["b5vv"]
                for g_ in range(16):
                    emit_vg(g_, h)
                v_all = p.cnt["b5vv"]

                for sb in range(2):
                    ih2 = h * 2 + sb
                    opar = ih2 % 2
                    LT = LAST_T[sb]
                    ss0 = p.cnt["b5ss"]
                    mm_sc = {}
                    pv_mm = {}
                    pv_ready = {}

                    def emit_scores_t(t_):
                        sbk = t_ % 2

                        @blk.tensor
                        def _(t, sb=sb, t_=t_, sbk=sbk, opar=opar,
                              par_h=par_h, ss0=ss0):
                            if t_ == 0:
                                p.wait_at(t, "qgr", qgr_need)
                                p.wait_at(t, "b5vv", kn_all)
                                p.bank_wait(t, [f"b5o{opar}", "b5d"])
                            if t_ >= 2:
                                p.wait_at(t, "b5ss", ss0 + t_ - 1)
                            else:
                                p.bank_wait(t, [f"b5sc{sbk}"])
                            t.matmul(psS[sbk][:, :],
                                     knop[:, t_ * 128:t_ * 128 + 128],
                                     qns4[par_h][:, sb * 512:sb * 512 + 512],
                                     start=True, stop=False)
                            ins = t.matmul(
                                psS[sbk][:, :],
                                kpew[:, t_ * 128:t_ * 128 + 128],
                                qgs4[par_h][:, sb * 512:sb * 512 + 512],
                                start=False, stop=True)
                            p.inc(ins, "b5mm")
                        mm_sc[t_] = p.cnt["b5mm"]
                        pbuf = probs[t_ % 3]

                        if t_ in DIAG_T[sb]:
                            ri = t_ - DIAG_T[sb][0]

                            @blk.vector
                            def _(v, ri=ri, pbuf=pbuf, t_=t_, ss0=ss0):
                                p.wait_at(v, "b5ss", ss0 + t_ + 1)
                                p.wait(v, "ldw")
                                p.inc(v.tensor_mul(
                                    pbuf[:, :], pbuf[:, :],
                                    kmask[:, ri * 512:ri * 512 + 512]),
                                    "b5vv")
                            pv_ready[t_] = ("b5vv", p.cnt["b5vv"])
                        else:
                            pv_ready[t_] = ("b5ss", ss0 + t_ + 1)

                    def emit_pv_t(t_):
                        pbuf = probs[t_ % 3]
                        pv_sem, pv_val = pv_ready[t_]

                        @blk.tensor
                        def _(t, t_=t_, pbuf=pbuf, pv_sem=pv_sem,
                              pv_val=pv_val, opar=opar):
                            p.wait_at(t, pv_sem, pv_val)
                            if t_ == 0:
                                p.wait_at(t, "b5vv", v_all)
                            last = (t_ == LT - 1)
                            t.matmul(psO[opar][:, :],
                                     vsb[:, t_ * 128:t_ * 128 + 128],
                                     pbuf[:, :],
                                     start=(t_ == 0), stop=last)
                            ins = t.matmul(psD[:, :], p.ones_bf[:, :],
                                           pbuf[:, :],
                                           start=(t_ == 0), stop=last)
                            p.inc(ins, "b5mm")
                        pv_mm[t_] = p.cnt["b5mm"]

                    emit_scores_t(0)
                    for t_ in range(1, LT):
                        emit_scores_t(t_)
                        emit_pv_t(t_ - 1)
                    emit_pv_t(LT - 1)
                    mm_pv = p.cnt["b5mm"]

                    @blk.scalar
                    def _(s, sb=sb, LT=LT, mm_sc=dict(mm_sc)):
                        for t_ in range(LT):
                            sbk = t_ % 2
                            p.wait_at(s, "b5mm", mm_sc[t_])
                            p.inc(s.activation(probs[t_ % 3][:, :],
                                               psS[sbk][:, :], AF.Exp),
                                  "b5ss")
                    assert p.cnt["b5ss"] == ss0 + LT
                    p.bank["b5sc0"] = ("b5ss", ss0 + LT - 1)
                    p.bank["b5sc1"] = ("b5ss", ss0 + LT)
                    for idx in range(3):
                        lastt = max(t for t in range(LT) if t % 3 == idx)
                        p.bank[f"b5pb{idx}"] = ("b5mm", pv_mm[lastt])

                    @blk.vector
                    def _(v, sb=sb, h=h, mm_pv=mm_pv, opar=opar):
                        p.wait_at(v, "b5mm", mm_pv)
                        p.edge(v, v.reciprocal(denr[:, :], psD[:, :]),
                               "b5vv")
                        p.inc(v.tensor_mul(
                            o_sb[:, h * TPC + sb * 512:
                                 h * TPC + sb * 512 + 512],
                            psO[opar][:, :], denr[:, :]), "b5vv")
                    p.bank_set(["b5d", f"b5o{opar}"], "b5vv")
                p.bank_set([f"b5qb{par_h}"], "b5mm")

            @blk.sync
            def _(sy):
                p.wait(sy, "odw")
                p.wait(sy, "b5vv")
                p.wait(sy, "b5ss")
                p.wait(sy, "b5mm")
                p.inc(sy.dma_start(out=win_mark0[:, :], in_=cpool_h),
                      "odw", 16)

        v_sb_l = p.sbuf("b6v", [128, 16 * TPC], F32)
        with ExitStack() as b6:
            en6 = b6.enter_context
            wo_b_ = [en6(nc.sbuf_tensor(f"b6wo{i}", [128, 16 * 512], BF16))
                     for i in range(2)]
            t1_6 = en6(nc.sbuf_tensor("b6t1", [128, 512], F32))
            ps6_ = [en6(nc.psum_tensor(f"b6ps{i}", [128, 512], F32))
                    for i in range(4)]
            p.new_sem("b6mm", "b6vv", "b6ss", "b6w")

            @blk.sync
            def _(sy):
                p.wait(sy, "odw")
                p.inc(sy.dma_start(
                    out=v_sb_l.ap().rearrange("pp (k m) -> pp k m", k=16),
                    in_=uo.rearrange("k pp m -> pp k m")), "ldx", 16)

            bank6 = {}
            for mg in range(4):
                @blk.gpsimd
                def _(g, mg=mg):
                    if mg >= 2:
                        p.wait_at(g, "b6mm", p.cnt["b6mm"])
                    else:
                        p.wait(g, "odw")
                    p.inc(g.dma_start(
                        out=wo_b_[mg % 2].ap().rearrange(
                            "pp (k m) -> pp k m", k=16),
                        in_=woT_h[:, :, mg * 512:(mg + 1) * 512].rearrange(
                            "k pp m -> pp k m")), "b6w", 16)
                w_need = p.cnt["b6w"]

                @blk.tensor
                def _(t, mg=mg, w_need=w_need):
                    p.wait_at(t, "b6w", w_need)
                    p.wait(t, "ldx")
                    if mg == 0:
                        p.wait(t, "b5vv")
                    for ml in range(4):
                        m = mg * 4 + ml
                        for hh in range(NH):
                            key = f"wo{ml % 2}"
                            if key in bank6:
                                t.wait_ge(p.sem["b6vv"], bank6[key])
                            for k in range(16):
                                ins = t.matmul(
                                    ps6_[ml % 2][:, :],
                                    wo_b_[mg % 2][:, k * 512 + ml * 128:
                                                  k * 512 + ml * 128 + 128],
                                    o_sb[:, k * TPC + hh * 512:
                                         k * TPC + hh * 512 + 512],
                                    start=(k == 0), stop=(k == 15))
                            mmv = p.cnt["b6mm"] + 1
                            p.inc(ins, "b6mm")

                            @blk.vector
                            def _(v, m=m, ml=ml, hh=hh, mmv=mmv):
                                p.wait_at(v, "b6mm", mmv)
                                p.edge(v, v.tensor_scalar(
                                    t1_6[:, :], ps6_[ml % 2][:, :],
                                    wob_sb[:, m:m + 1], None, ALU.add),
                                    "b6vv")
                                p.inc(v.tensor_add(
                                    v_sb_l[:, m * TPC + hh * 512:
                                           m * TPC + hh * 512 + 512],
                                    t1_6[:, :],
                                    v_sb_l[:, m * TPC + hh * 512:
                                           m * TPC + hh * 512 + 512]),
                                    "b6vv")
                            bank6[f"wo{ml % 2}"] = p.cnt["b6vv"]

            @blk.sync
            def _(sy):
                p.wait(sy, "b6vv")
                p.wait(sy, "b6ss")
                p.wait(sy, "b6mm")
                p.wait(sy, "b6w")
                p.inc(sy.dma_start(out=win_mark0[:, :], in_=cpool_h),
                      "b6w", 16)

        xh3 = p.sbuf("b7xh", [128, 16 * TPC], BF16)
        gate_sb = p.sbuf("b7gate", [128, 16 * 8], BF16)

        @blk.gpsimd
        def _(g):
            p.wait(g, "b6vv")
            p.wait(g, "b6w")
            p.inc(g.dma_start(
                out=gate_sb.ap().rearrange("pp (k m) -> pp k m", k=16),
                in_=gateT_h.rearrange("k pp m -> pp k m")), "ldw", 16)

        ln3, _c3 = emit_layernorm(p, blk, "b7ln", v_sb_l, xh3, TPC,
                                  ["b6vv", "b6w", "ldw"])
        f2, _c4 = emit_ffn(p, blk, "f2", xh3, v_sb_l, w1p_h, w2t_h,
                           gate_sb, TPC, [ln3])

        @blk.sync
        def _(sy):
            p.wait(sy, f2)
            p.inc(sy.dma_start(
                out=outT.rearrange("k pp m -> pp k m"),
                in_=v_sb_l.ap().rearrange("pp (k m) -> pp k m", k=16)),
                "std", 16)
            p.wait(sy, "std")
    return nc


def prep_attn_weights(ins):
    f32 = np.float32
    g = lambda k: np.asarray(ins[k], f32)
    w = {}
    w["wqaT"] = np.ascontiguousarray(g("L0_attn_wq_a_w").T.reshape(16, 128, QLR))
    w["wqab"] = _bias_tiles(g("L0_attn_wq_a_b"), 12)
    w["qnw"] = _bias_tiles(g("L0_attn_q_norm_w"), 12)
    wqb = g("L0_attn_wq_b_w")
    wqbT = np.stack([
        np.ascontiguousarray(wqb[192 * h:192 * h + 192, :].T.reshape(
            12, 128, 192)) for h in range(H)])
    w["wqbT"] = wqbT
    bb = g("L0_attn_wq_b_b")
    t = np.zeros((H, 128, 2), f32)
    for h in range(H):
        t[h, :, 0] = bb[192 * h:192 * h + 128]
        t[h, 0:64, 1] = bb[192 * h + 128:192 * h + 192]
    w["wqbb"] = t
    w["wkvaT"] = np.ascontiguousarray(
        g("L0_attn_wkv_a_w").T.reshape(16, 128, 576))
    w["wkvab"] = _bias_tiles(g("L0_attn_wkv_a_b"), 5)
    w["kvnw"] = _bias_tiles(g("L0_attn_kv_norm_w"), 4)
    wkvb = g("L0_attn_wkv_b_w").reshape(H, 256, KVLR)
    w["wknT"] = np.ascontiguousarray(
        wkvb[:, 0:128, :].transpose(0, 2, 1).reshape(H, 4, 128, 128))
    w["wkvvT"] = np.ascontiguousarray(
        wkvb[:, 128:256, :].transpose(0, 2, 1).reshape(H, 4, 128, 128))
    w["woT"] = np.ascontiguousarray(g("L0_attn_wo_w").T.reshape(16, 128, D))
    w["wob"] = _bias_tiles(g("L0_attn_wo_b"), 16)
    return w


def _rope_tables_T():
    inv = 1.0 / 10000.0 ** (np.arange(0, ROPE, 2, dtype=np.float32) / ROPE)
    t = np.arange(S, dtype=np.float32)[:, None]
    fr = t * inv[None, :]
    fr = np.concatenate([fr, fr], axis=-1)
    return (np.ascontiguousarray(np.cos(fr).T),
            np.ascontiguousarray(np.sin(fr).T))


def _masks():
    q = np.arange(128)[:, None]
    k = np.arange(128)[None, :]
    dmask = np.where(k <= q, 0.0, -1e30).astype(np.float32)
    kk = np.arange(128)[:, None]
    qq = np.arange(512)[None, :]
    km = np.stack([(kk <= qq - 128 * ri).astype(np.float32)
                   for ri in range(4)])
    return dmask, np.ascontiguousarray(km)


def run_launch_B(inputs, u):
    aw = prep_attn_weights(inputs)
    w1p, w2t, gateT = prep_ffn_weights(inputs)
    cpool = _cpool_np()
    ident = np.eye(128, dtype=np.float32)
    cosT, sinT = _rope_tables_T()
    dmask, kmask = _masks()
    uf = np.ascontiguousarray(u.reshape(16, 128, S))
    nc = build_B()
    ims = []
    for c in range(NCORE):
        pad = 7168 - 1024 * c
        meta = np.array([[pad, 4 * pad, 1024 * c, pad // 512,
                          pad // 128, 2 * c, 0, 0]], np.int32)
        ims.append({
            "uf": uf,
            "uo": np.ascontiguousarray(
                u[:, c * TPC:(c + 1) * TPC].reshape(16, 128, TPC)),
            "cpool": cpool, "ident": ident,
            "cosg": cosT, "sing": sinT,
            "coso": np.ascontiguousarray(cosT[:, c * TPC:(c + 1) * TPC]),
            "sino": np.ascontiguousarray(sinT[:, c * TPC:(c + 1) * TPC]),
            "dmask": dmask, "kmask": kmask, "meta": meta,
            "wqaT": aw["wqaT"], "wqab": aw["wqab"], "qnw": aw["qnw"],
            "wqbT": aw["wqbT"], "wqbb": aw["wqbb"],
            "wkvaT": aw["wkvaT"], "wkvab": aw["wkvab"], "kvnw": aw["kvnw"],
            "wknT": aw["wknT"], "wkvvT": aw["wkvvT"],
            "woT": aw["woT"], "wob": aw["wob"],
            "w1p": w1p, "w2t": w2t, "gateT": gateT,
        })
    res = run_bass_kernel_spmd(nc, ims, core_ids=list(range(NCORE)))
    out = np.empty((D, S), np.float32)
    for c in range(NCORE):
        out[:, c * TPC:(c + 1) * TPC] = \
            np.asarray(res.results[c]["outT"]).reshape(D, TPC)
    return out


def kernel(**inputs):
    u = run_launch_A(inputs)
    out = run_launch_B(inputs, u)
    return np.ascontiguousarray(out.T).reshape(1, S, D).astype(np.float32)



# version 11
# speedup vs baseline: 2.1071x; 1.0071x over previous
import math
from contextlib import ExitStack

import numpy as np
import ml_dtypes
import concourse.bass as bass
import concourse.mybir as mybir
from concourse.bass_utils import run_bass_kernel_spmd

F32 = mybir.dt.float32
BF16 = mybir.dt.bfloat16
I32 = mybir.dt.int32
AF = mybir.ActivationFunctionType
ALU = mybir.AluOpType
AX = mybir.AxisListType
BF16NP = ml_dtypes.bfloat16

S, D, H = 8192, 2048, 16
QLR, KVLR = 1536, 512
NOPE, ROPE, VD = 128, 64, 128
QKD = NOPE + ROPE
E, I_FF = 8, 1408
NCORE = 8
TPC = S // NCORE
NH = TPC // 512
SCALE = 1.0 / math.sqrt(QKD)

CP = {"ln_eps": 0, "rms_eps": 1, "neg_half": 2, "inv_d": 3, "inv_qlr": 4,
      "inv_kvlr": 5, "half": 6, "one": 7, "neg_big": 8, "zero": 9,
      "neg_one": 10, "scale": 11}


def _cpool_np():
    v = np.zeros((128, 16), np.float32)
    for k, val in [("ln_eps", 1e-5), ("rms_eps", 1e-6), ("neg_half", -0.5),
                   ("inv_d", 1.0 / D), ("inv_qlr", 1.0 / QLR),
                   ("inv_kvlr", 1.0 / KVLR), ("half", 0.5), ("one", 1.0),
                   ("neg_big", -1e30), ("zero", 0.0), ("neg_one", -1.0),
                   ("scale", SCALE)]:
        v[:, CP[k]] = val
    return v


class P:

    def __init__(self, nc, stack):
        self.nc = nc
        self.stack = stack
        self.en = stack.enter_context
        self.sem = {}
        self.cnt = {}
        self.bank = {}

    def sbuf(self, name, shape, dt):
        return self.en(self.nc.sbuf_tensor(name, shape, dt))

    def new_sem(self, *names):
        for n in names:
            if n not in self.sem:
                self.sem[n] = self.en(self.nc.semaphore(n))
                self.cnt[n] = 0

    def inc(self, instr, name, n=1):
        self.cnt[name] += n
        instr.then_inc(self.sem[name], n)
        return instr

    def wait(self, eng, name):
        if self.cnt[name] > 0:
            eng.wait_ge(self.sem[name], self.cnt[name])

    def wait_at(self, eng, name, v):
        if v > 0:
            eng.wait_ge(self.sem[name], v)

    def bank_wait(self, eng, keys):
        for k in keys:
            if k in self.bank:
                s, v = self.bank[k]
                self.wait_at(eng, s, v)

    def bank_set(self, keys, sem_name):
        for k in keys:
            self.bank[k] = (sem_name, self.cnt[sem_name])

    def edge(self, eng, instr, name):
        self.inc(instr, name)
        eng.wait_ge(self.sem[name], self.cnt[name])

    def cp(self, name):
        return self.cpool[:, CP[name]:CP[name] + 1]


def emit_layernorm(p, blk, tag, in_sb, out_sb, T, ready_sems, sem_tag=None,
                   pss_ext=None):
    nc = p.nc
    nh = T // 512
    st = sem_tag or tag
    with ExitStack() as ph:
        en = ph.enter_context
        sq = en(nc.sbuf_tensor(f"{tag}sq", [128, 16 * T], BF16))
        m_sb = en(nc.sbuf_tensor(f"{tag}m", [128, T], F32))
        v_sb = en(nc.sbuf_tensor(f"{tag}v", [128, T], F32))
        rstd = en(nc.sbuf_tensor(f"{tag}r", [128, T], F32))
        tsc = en(nc.sbuf_tensor(f"{tag}t", [128, T], F32))
        pss = pss_ext if pss_ext is not None else [
            en(nc.psum_tensor(f"{tag}ps{i}", [128, 512], F32))
            for i in range(2 * nh)]
        p.new_sem(f"{st}mm", f"{st}vv", f"{st}ss")
        mm, vv, ss = f"{st}mm", f"{st}vv", f"{st}ss"
        ent = {n: p.cnt[n] for n in (mm, vv, ss)}

        @blk.vector
        def _(v):
            for r in ready_sems:
                p.wait(v, r)
            for n, v_ in ent.items():
                p.wait_at(v, n, v_)
            p.inc(v.tensor_mul(sq[:, :], in_sb[:, :], in_sb[:, :]), ss)

        @blk.tensor
        def _(t):
            for r in ready_sems:
                p.wait(t, r)
            for n, v_ in ent.items():
                p.wait_at(t, n, v_)
            p.wait(t, ss)
            for k in range(16):
                for h in range(nh):
                    t.matmul(pss[h][:, :], p.ones_f32[:, :],
                             in_sb[:, k * T + h * 512: k * T + h * 512 + 512],
                             start=(k == 0), stop=(k == 15))
                    ins = t.matmul(pss[nh + h][:, :], p.ones_bf[:, :],
                                   sq[:, k * T + h * 512: k * T + h * 512 + 512],
                                   start=(k == 0), stop=(k == 15))
            p.inc(ins, mm)

        @blk.vector
        def _(v):
            p.wait(v, mm)
            for h in range(nh):
                ins = v.tensor_scalar(m_sb[:, h * 512:h * 512 + 512],
                                      pss[h][:, :], p.cp("inv_d"), None,
                                      ALU.mult)
                if h < nh - 1:
                    p.edge(v, ins, ss)
            p.inc(ins, ss)

        @blk.vector
        def _(v):
            p.wait(v, ss)
            for n, v_ in ent.items():
                p.wait_at(v, n, v_)
            p.edge(v, v.tensor_mul(tsc[:, :], m_sb[:, :], m_sb[:, :]), vv)
            for h in range(nh):
                ins = v.scalar_tensor_tensor(
                    v_sb[:, h * 512:h * 512 + 512], pss[nh + h][:, :],
                    p.cp("inv_d"), tsc[:, h * 512:h * 512 + 512],
                    ALU.mult, ALU.subtract)
            p.inc(ins, vv)

        @blk.scalar
        def _(s):
            p.wait(s, vv)
            p.edge(s, s.activation(rstd[:, :], v_sb[:, :], AF.Ln,
                                   bias=p.cp("ln_eps")), ss)
            p.inc(s.activation(rstd[:, :], rstd[:, :], AF.Exp,
                               scale=p.cp("neg_half")), ss)

        @blk.vector
        def _(v):
            p.wait(v, ss)
            for k in range(16):
                p.edge(v, v.tensor_sub(tsc[:, :], in_sb[:, k * T:(k + 1) * T],
                                       m_sb[:, :]), vv)
                ins = v.tensor_mul(out_sb[:, k * T:(k + 1) * T], tsc[:, :],
                                   rstd[:, :])
                if k == 15:
                    p.inc(ins, vv)
                else:
                    p.edge(v, ins, vv)
        done = p.cnt[vv]
    return vv, done


def emit_ffn(p, blk, tag, xh_sb, resid_sb, w1p_ap, w2t_ap, gate_sb, T,
             ready_sems, dbg=None):
    nc = p.nc
    nh = T // 512
    ntc = T // 128
    with ExitStack() as ph:
        en = ph.enter_context
        act = en(nc.sbuf_tensor(f"{tag}act", [128, 11 * T], BF16))
        sg2 = [en(nc.sbuf_tensor(f"{tag}sg0", [128, T], F32))] * 2
        ucs = en(nc.sbuf_tensor(f"{tag}uc", [128, T], F32))
        lg = en(nc.sbuf_tensor(f"{tag}lg", [8, T], F32))
        lt = en(nc.sbuf_tensor(f"{tag}lt", [128, 8], F32))
        ltd = en(nc.sbuf_tensor(f"{tag}ltd", [128, 8], F32))
        lsel = en(nc.sbuf_tensor(f"{tag}lsel", [128, 8], F32))
        lmsk = en(nc.sbuf_tensor(f"{tag}lmsk", [128, 8], F32))
        nm1 = en(nc.sbuf_tensor(f"{tag}nm1", [128, 1], F32))
        m2s = en(nc.sbuf_tensor(f"{tag}m2", [128, 1], F32))
        lz = en(nc.sbuf_tensor(f"{tag}lz", [128, 1], F32))
        cmbT = en(nc.sbuf_tensor(f"{tag}cmbT", [8, 128], F32))
        cTb = [en(nc.sbuf_tensor(f"{tag}cT{i}", [1, T], BF16))
               for i in range(2)]
        c_bc = en(nc.sbuf_tensor(f"{tag}cbc", [128, T], F32))
        w1b = [en(nc.sbuf_tensor(f"{tag}w1{i}", [128, 16 * 256], BF16))
               for i in range(2)]
        w2b = [en(nc.sbuf_tensor(f"{tag}w2{i}", [128, 11 * 256], BF16))
               for i in range(2)]
        ps = [en(nc.psum_tensor(f"{tag}ps{i}", [128, 512], F32))
              for i in range(8)]
        cT_d = nc.dram_tensor(f"{tag}cTd", [E, T], F32, kind="Internal")

        p.new_sem(f"{tag}mm", f"{tag}vv", f"{tag}ss", f"{tag}w1", f"{tag}w2",
                  f"{tag}af", f"{tag}cd", f"{tag}ct")
        mm, vv, ss = f"{tag}mm", f"{tag}vv", f"{tag}ss"
        w1s, w2s, afs, cds = f"{tag}w1", f"{tag}w2", f"{tag}af", f"{tag}cd"
        cts = f"{tag}ct"

        @blk.tensor
        def _(t):
            for r in ready_sems:
                p.wait(t, r)
            p.bank_wait(t, ["ps6", "ps7"])
            for h in range(nh):
                for k in range(16):
                    ins = t.matmul(
                        ps[6 + h][0:8, :], gate_sb[:, k * 8:k * 8 + 8],
                        xh_sb[:, k * T + h * 512: k * T + h * 512 + 512],
                        start=(k == 0), stop=(k == 15))
            p.inc(ins, mm)

        @blk.vector
        def _(v):
            p.wait(v, mm)
            for h in range(nh):
                ins = v.tensor_copy(lg[0:8, h * 512:h * 512 + 512],
                                    ps[6 + h][0:8, :])
                if h < nh - 1:
                    p.edge(v, ins, ss)
            p.inc(ins, ss)
            p.bank_set(["ps6", "ps7"], ss)

        if dbg is not None:
            @blk.sync
            def _(sy):
                p.wait(sy, ss)
                p.inc(sy.dma_start(out=dbg["xh"], in_=xh_sb[:, :]), "std", 16)
                p.inc(sy.dma_start(out=dbg["lg"], in_=lg[0:8, :]), "std", 16)

        for tc in range(ntc):
            @blk.tensor
            def _(t):
                p.wait(t, ss)
                p.bank_wait(t, ["ps6"])
                p.inc(t.transpose(ps[6][:, 0:8],
                                  lg[0:8, tc * 128:(tc + 1) * 128],
                                  p.ident_f32[0:8, 0:8]), mm)

            @blk.vector
            def _(v):
                p.wait(v, mm)
                p.wait(v, ss)
                p.edge(v, v.tensor_copy(lt[:, :], ps[6][:, 0:8]), vv)
                p.edge(v, v.reduce_max(nm1[:, :], lt[:, :], axis=AX.X,
                                       negate=True), vv)
                p.bank_set(["ps6"], vv)
                p.edge(v, v.tensor_scalar(ltd[:, :], lt[:, :], nm1[:, :],
                                          None, ALU.add), vv)
                p.edge(v, v.tensor_scalar(lmsk[:, :], ltd[:, :], p.cp("zero"),
                                          None, ALU.is_equal), vv)
                p.edge(v, v.scalar_tensor_tensor(lmsk[:, :], lmsk[:, :],
                                                 p.cp("neg_big"), lt[:, :],
                                                 ALU.mult, ALU.add), vv)
                p.edge(v, v.reduce_max(m2s[:, :], lmsk[:, :], axis=AX.X), vv)
                p.inc(v.tensor_scalar(lsel[:, :], lt[:, :], m2s[:, :], None,
                                      ALU.is_ge), vv)

            @blk.scalar
            def _(s):
                p.wait(s, vv)
                p.inc(s.activation(ltd[:, :], lt[:, :], AF.Exp,
                                   bias=nm1[:, :]), ss)

            @blk.vector
            def _(v):
                p.wait(v, ss)
                p.edge(v, v.tensor_mul(ltd[:, :], ltd[:, :], lsel[:, :]), vv)
                p.edge(v, v.reduce_sum(lz[:, :], ltd[:, :], axis=AX.X), vv)
                p.edge(v, v.reciprocal(lz[:, :], lz[:, :]), vv)
                p.inc(v.tensor_scalar(ltd[:, :], ltd[:, :], lz[:, :],
                                      p.cp("half"), ALU.mult, ALU.mult), vv)

            @blk.tensor
            def _(t):
                p.wait(t, vv)
                p.bank_wait(t, ["ps6"])
                p.inc(t.transpose(ps[6][0:8, 0:128], ltd[:, :],
                                  p.ident_f32[:, :]), mm)

            @blk.vector
            def _(v):
                p.wait(v, mm)
                p.bank_wait(v, ["cmbT"])
                p.inc(v.tensor_copy(cmbT[:, :], ps[6][0:8, 0:128]), ss)
                p.bank_set(["ps6"], ss)

            @blk.sync
            def _(sy):
                p.wait(sy, ss)
                p.inc(sy.dma_start(out=cT_d.ap()[:, tc * 128:(tc + 1) * 128],
                                   in_=cmbT[:, :]), cds, 16)
                p.bank_set(["cmbT"], cds)


        if dbg is not None:
            dmisc = en(nc.sbuf_tensor(f"{tag}dmisc", [128, 256], F32))

            @blk.vector
            def _(v):
                p.wait(v, cts)
                v.memset(dmisc[:, :], 0.0)
                v.tensor_copy(dmisc[0:8, 0:128], cmbT[:, :])
                v.tensor_copy(dmisc[:, 128:136], ltd[:, :])
                v.tensor_copy(dmisc[:, 136:137], nm1[:, :])
                v.tensor_copy(dmisc[:, 137:138], lz[:, :])
                v.tensor_copy(dmisc[:, 138:146], lt[:, :])
                v.tensor_copy(dmisc[:, 146:154], lsel[:, :])
                p.inc(v.tensor_copy(dmisc[0:1, 160:160 + 64],
                                    cTb[1][0:1, 0:64]), vv)

            @blk.sync
            def _(sy):
                p.wait(sy, vv)
                p.inc(sy.dma_start(out=dbg["misc"], in_=dmisc[:, :]),
                      "std", 16)

        NP_SL = [1] * 11
        w1_free = []
        w2_free = []

        bcast_mm = {}
        for e in range(E):
            @blk.gpsimd
            def _(g, e=e):
                p.wait(g, cds)
                if e - 2 in bcast_mm:
                    p.wait_at(g, mm, bcast_mm[e - 2])
                p.wait(g, cts)
                p.inc(g.dma_start(out=cTb[e % 2][0:1, :],
                                  in_=cT_d.ap()[e:e + 1, :]), cts, 16)
            ct_need = p.cnt[cts]

            @blk.tensor
            def _(t, e=e, ct_need=ct_need):
                p.wait_at(t, cts, ct_need)
                p.bank_wait(t, ["ps6", "ps7"])
                for h in range(nh):
                    p.inc(t.matmul(
                        ps[6 + h][:, :], p.ones_col_bf[0:1, :],
                        cTb[e % 2][0:1, h * 512:h * 512 + 512],
                        start=True, stop=True), mm)
            bcast_mm[e] = p.cnt[mm]

            @blk.vector
            def _(v):
                p.wait(v, mm)
                p.bank_wait(v, ["cbc"])
                for h in range(nh):
                    ins = v.tensor_copy(c_bc[:, h * 512:h * 512 + 512],
                                        ps[6 + h][:, :])
                    if h < nh - 1:
                        p.edge(v, ins, ss)
                p.inc(ins, ss)
                p.bank_set(["ps6", "ps7"], ss)

            cbc_ready = p.cnt[ss]
            silu_jobs = []
            ep_afs = {}
            ss_sil0 = p.cnt[ss]

            pair0 = 0
            for sli, npair in enumerate(NP_SL):
                gslice = e * len(NP_SL) + sli
                buf = gslice % 2

                @blk.gpsimd
                def _(g):
                    if gslice >= 2:
                        sname, v = w1_free[gslice - 2]
                        p.wait_at(g, sname, v)
                    else:
                        for r in ready_sems:
                            p.wait(g, r)
                    p.wait(g, w1s)
                    cols = npair * 256
                    p.inc(g.dma_start(
                        out=w1b[buf].ap().rearrange(
                            "pp (k m) -> pp k m", k=16)[:, :, 0:cols],
                        in_=w1p_ap[e, :, :,
                                   pair0 * 256:pair0 * 256 + cols].rearrange(
                            "k pp m -> pp k m")), w1s, 16)
                w1_need = p.cnt[w1s]

                for pl in range(npair):
                    gp = pair0 + pl
                    grp = gp % 2
                    bks = [f"ps{4 * grp + j}" for j in range(4)]
                    bg = [ps[4 * grp + j] for j in range(4)]

                    @blk.tensor
                    def _(t):
                        p.wait_at(t, w1s, w1_need)
                        p.wait_at(t, ss, cbc_ready)
                        p.bank_wait(t, bks)
                        W = 256
                        for k in range(16):
                            for hh in range(nh):
                                t.matmul(bg[hh][:, :],
                                         w1b[buf][:, k * W + pl * 256:
                                                  k * W + pl * 256 + 128],
                                         xh_sb[:, k * T + hh * 512:
                                               k * T + hh * 512 + 512],
                                         start=(k == 0), stop=(k == 15))
                                ins = t.matmul(
                                    bg[2 + hh][:, :],
                                    w1b[buf][:, k * W + pl * 256 + 128:
                                             k * W + pl * 256 + 256],
                                    xh_sb[:, k * T + hh * 512:
                                          k * T + hh * 512 + 512],
                                    start=(k == 0), stop=(k == 15))
                        p.inc(ins, mm)
                    mm_grp = p.cnt[mm]

                    ss_grp = ss_sil0 + gp + 1
                    silu_jobs.append((mm_grp, gp, bg, bks))

                    @blk.vector
                    def _(v, mm_grp=mm_grp, ss_grp=ss_grp, gp=gp, bg=bg):
                        p.wait_at(v, mm, mm_grp)
                        p.wait_at(v, ss, ss_grp)
                        sgb = sg2[gp % 2]
                        for hh in range(nh):
                            p.edge(v, v.tensor_mul(
                                ucs[:, hh * 512:hh * 512 + 512],
                                bg[2 + hh][:, :],
                                c_bc[:, hh * 512:hh * 512 + 512]), vv)
                            ins = v.tensor_mul(
                                act[:, gp * T + hh * 512:
                                    gp * T + hh * 512 + 512],
                                sgb[:, hh * 512:hh * 512 + 512],
                                ucs[:, hh * 512:hh * 512 + 512])
                            if hh < nh - 1:
                                p.edge(v, ins, vv)
                        p.inc(ins, afs)
                    p.bank_set(bks, afs)
                    ep_afs[gp] = p.cnt[afs]
                pair0 += npair
                w1_free.append((afs, p.cnt[afs]))

            @blk.scalar
            def _(s, silu_jobs=list(silu_jobs), e=e):
                for mm_grp, gp, bg, bks in silu_jobs:
                    p.wait_at(s, mm, mm_grp)
                    if gp - 1 in ep_afs:
                        p.wait_at(s, afs, ep_afs[gp - 1])
                    sgb = sg2[gp % 2]
                    for hh in range(nh):
                        ins = s.activation(sgb[:, hh * 512:hh * 512 + 512],
                                           bg[hh][:, :], AF.Silu)
                    p.inc(ins, ss)
            assert p.cnt[ss] == ss_sil0 + 11

            act_all = p.cnt[afs]

            for q in range(8):
                gq = e * 8 + q
                buf2 = gq % 2

                @blk.gpsimd
                def _(g):
                    if gq >= 2:
                        sname, v = w2_free[gq - 2]
                        p.wait_at(g, sname, v)
                    else:
                        for r in ready_sems:
                            p.wait(g, r)
                    p.wait(g, w2s)
                    p.inc(g.dma_start(
                        out=w2b[buf2].ap().rearrange(
                            "pp (k m) -> pp k m", k=11),
                        in_=w2t_ap[e, :, :, q * 256:q * 256 + 256].rearrange(
                            "k pp m -> pp k m")), w2s, 16)
                w2_need = p.cnt[w2s]

                for m2l in range(2):
                    m2 = q * 2 + m2l
                    slot = m2l % 2
                    bks = [f"ps{slot * 2}", f"ps{slot * 2 + 1}"]
                    bk = [ps[slot * 2], ps[slot * 2 + 1]]

                    @blk.tensor
                    def _(t):
                        p.wait_at(t, w2s, w2_need)
                        p.wait_at(t, afs, act_all)
                        p.bank_wait(t, bks)
                        for k2 in range(11):
                            for hh in range(nh):
                                ins = t.matmul(
                                    bk[hh][:, :],
                                    w2b[buf2][:, k2 * 256 + m2l * 128:
                                              k2 * 256 + m2l * 128 + 128],
                                    act[:, k2 * T + hh * 512:
                                        k2 * T + hh * 512 + 512],
                                    start=(k2 == 0), stop=(k2 == 10))
                        p.inc(ins, mm)
                    mm_w2 = p.cnt[mm]

                    @blk.vector
                    def _(v):
                        p.wait_at(v, mm, mm_w2)
                        p.bank_wait(v, [f"res{m2}"])
                        for hh in range(nh):
                            ins = v.tensor_add(
                                resid_sb[:, m2 * T + hh * 512:
                                         m2 * T + hh * 512 + 512],
                                bk[hh][:, :],
                                resid_sb[:, m2 * T + hh * 512:
                                         m2 * T + hh * 512 + 512])
                            if hh < nh - 1:
                                p.edge(v, ins, vv)
                        p.inc(ins, vv)
                    p.bank_set(bks, vv)
                    p.bank_set([f"res{m2}"], vv)
                w2_free.append((mm, p.cnt[mm]))

            p.bank_set(["cbc"], afs)
        if dbg is not None:
            @blk.sync
            def _(sy):
                p.wait(sy, afs)
                p.inc(sy.dma_start(out=dbg["cbc"], in_=c_bc[:, :]), "std", 16)
                p.inc(sy.dma_start(out=dbg["sg"], in_=sg[:, :]), "std", 16)
                p.inc(sy.dma_start(out=dbg["ucs"], in_=ucs[:, :]), "std", 16)

            @blk.vector
            def _(v):
                p.wait(v, afs)
                p.inc(v.tensor_copy(sg[:, :], act[:, 10 * T:11 * T]), vv)

            @blk.sync
            def _(sy):
                p.wait(sy, vv)
                p.wait(sy, "std")
                p.inc(sy.dma_start(out=dbg["act"], in_=sg[:, :]),
                      "std", 16)
        done = p.cnt[vv]
    return vv, done


def prep_ffn_weights(ins):
    f32 = np.float32
    w1 = np.asarray(ins["L0_ffn_w1"], f32)
    w1t = w1.transpose(0, 2, 1)
    idx = np.empty(2816, np.int64)
    for i in range(11):
        idx[i * 256:i * 256 + 128] = np.arange(i * 128, (i + 1) * 128)
        idx[i * 256 + 128:(i + 1) * 256] = np.arange(1408 + i * 128,
                                                     1408 + (i + 1) * 128)
    w1p = np.ascontiguousarray(w1t[:, :, idx].reshape(E, 16, 128, 2816))
    w2 = np.asarray(ins["L0_ffn_w2"], f32)
    w2t = np.ascontiguousarray(
        w2.transpose(0, 2, 1).reshape(E, 11, 128, 2048))
    gw = np.asarray(ins["L0_ffn_gate_w"], f32)
    gateT = np.ascontiguousarray(gw.T.reshape(16, 128, 8))
    return w1p, w2t, gateT


def _bias_tiles(b, nchunk):
    b = np.asarray(b, np.float32)
    padded = np.zeros(nchunk * 128, np.float32)
    padded[:b.shape[0]] = b
    return np.ascontiguousarray(padded.reshape(nchunk, 128).T)


def build_A(debug=False):
    nc = bass.Bass("TRN2", target_bir_lowering=False)
    xT = nc.dram_tensor("xT", [16, 128, TPC], F32, kind="ExternalInput").ap()
    cpool_h = nc.dram_tensor("cpool", [128, 16], F32,
                             kind="ExternalInput").ap()
    ident_h = nc.dram_tensor("ident", [128, 128], F32,
                             kind="ExternalInput").ap()
    w1p = nc.dram_tensor("w1p", [E, 16, 128, 2816], F32,
                         kind="ExternalInput").ap()
    w2t = nc.dram_tensor("w2t", [E, 11, 128, 2048], F32,
                         kind="ExternalInput").ap()
    gateT = nc.dram_tensor("gateT", [16, 128, 8], F32,
                           kind="ExternalInput").ap()
    uT = nc.dram_tensor("uT", [16, 128, TPC], F32, kind="ExternalOutput").ap()
    if debug:
        dbg_xh = nc.dram_tensor("dbg_xh", [128, 16 * TPC], BF16,
                            kind="ExternalOutput").ap()
        dbg_lg = nc.dram_tensor("dbg_lg", [8, TPC], F32,
                                kind="ExternalOutput").ap()
        dbg_cT = nc.dram_tensor("dbg_cT", [1, E * TPC], F32,
                                kind="ExternalOutput").ap()
        dbg_act = nc.dram_tensor("dbg_act", [128, TPC], F32,
                                 kind="ExternalOutput").ap()
        dbg_cbc = nc.dram_tensor("dbg_cbc", [128, TPC], F32,
                                 kind="ExternalOutput").ap()
        dbg_sg = nc.dram_tensor("dbg_sg", [128, TPC], F32,
                                kind="ExternalOutput").ap()
        dbg_ucs = nc.dram_tensor("dbg_ucs", [128, TPC], F32,
                                 kind="ExternalOutput").ap()
        dbg_misc = nc.dram_tensor("dbg_misc", [128, 256], F32,
                                  kind="ExternalOutput").ap()

    with ExitStack() as stack:
        p = P(nc, stack)
        p.cpool = p.sbuf("cpool_sb", [128, 16], F32)
        p.ident_f32 = p.sbuf("ident_sb", [128, 128], F32)
        p.ones_f32 = p.sbuf("ones_f32", [128, 128], F32)
        p.ones_bf = p.sbuf("ones_bf", [128, 128], BF16)
        p.ones_col = p.sbuf("ones_col", [1, 128], F32)
        p.ones_col_bf = p.sbuf("ones_col_bf", [1, 128], BF16)
        x_sb = p.sbuf("x_sb", [128, 16 * TPC], F32)
        xh_sb = p.sbuf("xh_sb", [128, 16 * TPC], BF16)
        gate_sb = p.sbuf("gate_sb", [128, 16 * 8], BF16)
        p.new_sem("ldx", "ldw", "std", "crdy")
        blk = p.en(nc.Block())

        @blk.sync
        def _(sy):
            p.inc(sy.dma_start(out=p.cpool[:, :], in_=cpool_h), "ldx", 16)
            p.inc(sy.dma_start(out=p.ident_f32[:, :], in_=ident_h), "ldx", 16)
            p.inc(sy.dma_start(
                out=x_sb.ap().rearrange("pp (k m) -> pp k m", k=16),
                in_=xT.rearrange("k pp m -> pp k m")), "ldx", 16)

        @blk.gpsimd
        def _(g):
            p.inc(g.dma_start(
                out=gate_sb.ap().rearrange("pp (k m) -> pp k m", k=16),
                in_=gateT.rearrange("k pp m -> pp k m")), "ldw", 16)

        @blk.vector
        def _(v):
            p.wait(v, "ldx")
            p.wait(v, "ldw")
            v.memset(p.ones_f32[:, :], 1.0)
            v.memset(p.ones_bf[:, :], 1.0)
            v.memset(p.ones_col[:, :], 1.0)
            p.inc(v.memset(p.ones_col_bf[:, :], 1.0), "crdy")

        lnsem, _c = emit_layernorm(p, blk, "ln1", x_sb, xh_sb, TPC, ["crdy"])
        dbg = None
        if debug:
            dbg = {"xh": dbg_xh, "lg": dbg_lg, "cT": dbg_cT, "act": dbg_act,
                   "cbc": dbg_cbc, "sg": dbg_sg, "ucs": dbg_ucs,
                   "misc": dbg_misc}
        fsem, _c2 = emit_ffn(p, blk, "f1", xh_sb, x_sb, w1p, w2t, gate_sb,
                             TPC, [lnsem], dbg=dbg)

        @blk.sync
        def _(sy):
            p.wait(sy, fsem)
            p.inc(sy.dma_start(
                out=uT.rearrange("k pp m -> pp k m"),
                in_=x_sb.ap().rearrange("pp (k m) -> pp k m", k=16)),
                "std", 16)
            p.wait(sy, "std")
    return nc


def host_prep_common(inputs):
    x = np.asarray(inputs["L0_attn_x"], np.float32).reshape(S, D)
    xT = np.ascontiguousarray(x.T)
    w1p, w2t, gateT = prep_ffn_weights(inputs)
    cpool = _cpool_np()
    ident = np.eye(128, dtype=np.float32)
    return x, xT, w1p, w2t, gateT, cpool, ident


def run_launch_A(inputs):
    x, xT, w1p, w2t, gateT, cpool, ident = host_prep_common(inputs)
    nc = build_A()
    ims = []
    for c in range(NCORE):
        sl = xT[:, c * TPC:(c + 1) * TPC]
        ims.append({
            "xT": np.ascontiguousarray(sl.reshape(16, 128, TPC)),
            "cpool": cpool, "ident": ident,
            "w1p": w1p, "w2t": w2t, "gateT": gateT,
        })
    res = run_bass_kernel_spmd(nc, ims, core_ids=list(range(NCORE)))
    u = np.empty((D, S), np.float32)
    for c in range(NCORE):
        u[:, c * TPC:(c + 1) * TPC] = \
            np.asarray(res.results[c]["uT"]).reshape(D, TPC)
    return u




NKT = 64
DIAG_T = {0: list(range(56, 60)), 1: list(range(60, 64))}
LAST_T = {0: 60, 1: 64}


def build_B():
    nc = bass.Bass("TRN2", target_bir_lowering=False)
    ap = lambda *a, **k: nc.dram_tensor(*a, **k).ap()
    uf = ap("uf", [16, 128, S], F32, kind="ExternalInput")
    uo = ap("uo", [16, 128, TPC], F32, kind="ExternalInput")
    cpool_h = ap("cpool", [128, 16], F32, kind="ExternalInput")
    ident_h = ap("ident", [128, 128], F32, kind="ExternalInput")
    cosg_h = ap("cosg", [64, S], F32, kind="ExternalInput")
    sing_h = ap("sing", [64, S], F32, kind="ExternalInput")
    coso_h = ap("coso", [64, TPC], F32, kind="ExternalInput")
    sino_h = ap("sino", [64, TPC], F32, kind="ExternalInput")
    dmask_h = ap("dmask", [128, 128], F32, kind="ExternalInput")
    kmask_h = ap("kmask", [4, 128, 512], F32, kind="ExternalInput")
    meta_h = ap("meta", [1, 8], I32, kind="ExternalInput")
    wqaT_h = ap("wqaT", [16, 128, QLR], F32, kind="ExternalInput")
    wqab_h = ap("wqab", [128, 12], F32, kind="ExternalInput")
    qnw_h = ap("qnw", [128, 12], F32, kind="ExternalInput")
    wqbT_h = ap("wqbT", [H, 12, 128, 192], F32, kind="ExternalInput")
    wqbb_h = ap("wqbb", [H, 128, 2], F32, kind="ExternalInput")
    wkvaT_h = ap("wkvaT", [16, 128, 576], F32, kind="ExternalInput")
    wkvab_h = ap("wkvab", [128, 5], F32, kind="ExternalInput")
    kvnw_h = ap("kvnw", [128, 4], F32, kind="ExternalInput")
    wknT_h = ap("wknT", [H, 4, 128, 128], F32, kind="ExternalInput")
    wkvvT_h = ap("wkvvT", [H, 4, 128, 128], F32, kind="ExternalInput")
    woT_h = ap("woT", [16, 128, D], F32, kind="ExternalInput")
    wob_h = ap("wob", [128, 16], F32, kind="ExternalInput")
    w1p_h = ap("w1p", [E, 16, 128, 2816], F32, kind="ExternalInput")
    w2t_h = ap("w2t", [E, 11, 128, 2048], F32, kind="ExternalInput")
    gateT_h = ap("gateT", [16, 128, 8], F32, kind="ExternalInput")
    outT = ap("outT", [16, 128, TPC], F32, kind="ExternalOutput")

    kvfm_d = nc.dram_tensor("kvfm_d", [128, 4 * 2 * S], BF16, kind="Internal")
    kvfm_g = nc.dram_tensor("kvfm_g", [128, 4 * S], BF16, kind="Internal")
    kpe_g = nc.dram_tensor("kpe_g", [66, S], BF16, kind="Internal")
    zrow_d = nc.dram_tensor("zrow_d", [1, S], BF16, kind="Internal")
    mr2_d = nc.dram_tensor("mr2_d", [2, S], F32, kind="Internal")
    kpe_d = nc.dram_tensor("kpe_d", [66, 2 * S], BF16, kind="Internal")
    qn_d = nc.dram_tensor("qn_d", [H, 128, TPC], BF16, kind="Internal")
    qaug_d = nc.dram_tensor("qaug_d", [H, 66, TPC], BF16, kind="Internal")
    mrow_d = nc.dram_tensor("mrow_d", [8, 128], BF16, kind="Internal")

    with ExitStack() as stack:
        p = P(nc, stack)
        p.cpool = p.sbuf("cpool_sb", [128, 16], F32)
        p.ident_f32 = p.sbuf("ident_sb", [128, 128], F32)
        p.ident_bf = p.sbuf("identb_sb", [128, 128], BF16)
        p.ones_f32 = p.sbuf("ones_f32", [128, 128], F32)
        p.ones_bf = p.sbuf("ones_bf", [128, 128], BF16)
        p.ones_col = p.sbuf("ones_col", [1, 128], F32)
        p.ones_col_bf = p.sbuf("ones_col_bf", [1, 128], BF16)
        ones_row = p.sbuf("ones_row", [1, 512], BF16)
        zeros_row = p.sbuf("zeros_row", [1, 512], BF16)
        wob_sb = p.sbuf("wob_sb", [128, 16], F32)
        p.meta_sb = p.sbuf("meta_sb", [1, 8], I32)
        win_mark0 = p.sbuf("win_mark0", [128, 16], F32)
        p.new_sem("ldx", "ldw", "std", "crdy", "kvd", "win", "qpk", "odw")
        blk = p.en(nc.Block())

        @blk.sync
        def _(sy):
            p.inc(sy.dma_start(out=p.cpool[:, :], in_=cpool_h), "ldx", 16)
            p.inc(sy.dma_start(out=p.ident_f32[:, :], in_=ident_h), "ldx", 16)
            p.inc(sy.dma_start(out=wob_sb[:, :], in_=wob_h), "ldx", 16)
            p.inc(sy.dma_start(out=p.meta_sb[:, :], in_=meta_h), "ldx", 16)

        @blk.gpsimd
        def _(g):
            p.inc(g.dma_start(out=p.ident_bf[:, :], in_=ident_h), "ldw", 16)

        @blk.vector
        def _(v):
            p.wait(v, "ldx")
            p.wait(v, "ldw")
            v.memset(p.ones_f32[:, :], 1.0)
            v.memset(p.ones_bf[:, :], 1.0)
            v.memset(p.ones_col[:, :], 1.0)
            v.memset(p.ones_col_bf[:, :], 1.0)
            v.memset(zeros_row[:, :], 0.0)
            p.inc(v.memset(ones_row[:, :], 1.0), "crdy")

        negbig_row = p.sbuf("negbig_row", [1, 512], BF16)

        @blk.vector
        def _(v):
            p.wait(v, "crdy")
            p.inc(v.memset(negbig_row[:, :], -30000.0), "crdy")

        @blk.sync
        def _(sy):
            p.wait(sy, "crdy")
            for g_ in range(16):
                p.inc(sy.dma_start(
                    out=kpe_d.ap()[64:65, g_ * 512:(g_ + 1) * 512],
                    in_=ones_row[0:1, :]), "kvd", 16)
                p.inc(sy.dma_start(
                    out=kpe_d.ap()[65:66, g_ * 512:(g_ + 1) * 512],
                    in_=negbig_row[0:1, :]), "kvd", 16)

        with ExitStack() as b2:
            en2 = b2.enter_context
            ut = [en2(nc.sbuf_tensor(f"b2ut{i}", [128, 16 * 512], F32))
                  for i in range(2)]
            utb = [en2(nc.sbuf_tensor(f"b2utb{i}", [128, 16 * 512], BF16))
                   for i in range(2)]
            sqln = en2(nc.sbuf_tensor("b2sql", [128, 16 * 512], BF16))
            cosg = [en2(nc.sbuf_tensor(f"b2cos{i}", [64, 512], F32))
                    for i in range(2)]
            sing = [en2(nc.sbuf_tensor(f"b2sin{i}", [64, 512], F32))
                    for i in range(2)]
            wkva = en2(nc.sbuf_tensor("b2wkva", [128, 16 * 576], BF16))
            wkvab = en2(nc.sbuf_tensor("b2wkvab", [128, 5], F32))
            kvnw = en2(nc.sbuf_tensor("b2kvnw", [128, 4], F32))
            kvt = en2(nc.sbuf_tensor("b2kvt", [128, 4 * 512], F32))
            sqkv = en2(nc.sbuf_tensor("b2sq", [128, 4 * 512], BF16))
            rstd2 = en2(nc.sbuf_tensor("b2rstd", [128, 512], F32))
            t1b = en2(nc.sbuf_tensor("b2t1", [128, 512], F32))
            tln = en2(nc.sbuf_tensor("b2tln", [128, 512], F32))
            mrow2 = [en2(nc.sbuf_tensor(f"b2mr{i}", [128, 512], F32))
                     for i in range(2)]
            vrow2 = [en2(nc.sbuf_tensor(f"b2vr{i}", [128, 512], F32))
                     for i in range(2)]
            wsum = en2(nc.sbuf_tensor("b2wsum", [128, 5], F32))
            kvbf = en2(nc.sbuf_tensor("b2kvbf", [128, 4 * 512], BF16))
            kperaw = en2(nc.sbuf_tensor("b2kpr", [64, 512], F32))
            kperot = en2(nc.sbuf_tensor("b2krt", [64, 512], F32))
            kpea = en2(nc.sbuf_tensor("b2kpa", [64, 512], F32))
            kpebf = en2(nc.sbuf_tensor("b2kpb", [64, 512], BF16))
            psb = [en2(nc.psum_tensor(f"b2ps{i}", [128, 512], F32))
                   for i in range(6)]
            zfill = en2(nc.sbuf_tensor("b2zf", [128, 2048], BF16))
            p.new_sem("b2mm", "b2vv", "b2ss", "b2ld")

            @blk.vector
            def _(v):
                p.wait(v, "crdy")
                p.inc(v.memset(zfill[:, :], 0.0), "crdy")

            @blk.sync
            def _(sy):
                p.wait(sy, "crdy")
                p.inc(sy.dma_start(out=wkvab[:, :], in_=wkvab_h), "ldx", 16)
                p.inc(sy.dma_start(out=kvnw[:, :], in_=kvnw_h), "ldx", 16)
                for m in range(4):
                    for cc_ in range(4):
                        p.inc(sy.dma_start(
                            out=kvfm_d.ap()[:, m * 2 * S + cc_ * 2048:
                                            m * 2 * S + (cc_ + 1) * 2048],
                            in_=zfill[:, :]), "kvd", 16)
                for cc_ in range(16):
                    p.inc(sy.dma_start(
                        out=kpe_d.ap()[0:64, cc_ * 512:(cc_ + 1) * 512],
                        in_=zfill[0:64, 0:512]), "kvd", 16)

            @blk.gpsimd
            def _(g):
                p.wait(g, "crdy")
                p.inc(g.dma_start(
                    out=wkva.ap().rearrange("pp (k m) -> pp k m", k=16),
                    in_=wkvaT_h.rearrange("k pp m -> pp k m")), "b2ld", 16)

            @blk.tensor
            def _(t):
                p.wait(t, "b2ld")
                for m in range(5):
                    w = 128 if m < 4 else 64
                    for k in range(16):
                        ins = t.matmul(
                            psb[5][0:w, m:m + 1],
                            wkva[:, k * 576 + m * 128:k * 576 + m * 128 + w],
                            p.ones_bf[:, 0:1],
                            start=(k == 0), stop=(k == 15))
                p.inc(ins, "b2mm")
            wsum_mm = p.cnt["b2mm"]

            @blk.vector
            def _(v):
                p.wait_at(v, "b2mm", wsum_mm)
                p.wait(v, "ldx")
                p.inc(v.tensor_copy(wsum[:, :], psb[5][:, 0:5]), "b2vv")
            wsum_done = p.cnt["b2vv"]

            stats_mm = {}
            sq_done = {}
            kv_last = {}
            rope_done = {}
            store_done = {}
            statv_done = {}
            bank2 = {}
            for g_ in range(16):
                gb = g_ % 2

                @blk.sync
                def _(sy, g_=g_, gb=gb):
                    if g_ - 2 in stats_mm:
                        p.wait_at(sy, "b2mm", stats_mm[g_ - 2])
                        p.wait_at(sy, "b2vv", sq_done[g_ - 2])
                        p.wait_at(sy, "b2vv", rope_done[g_ - 2])
                    p.wait(sy, "ldx")
                    p.inc(sy.dma_start(
                        out=ut[gb].ap().rearrange("pp (k m) -> pp k m", k=16),
                        in_=uf[:, :, g_ * 512:(g_ + 1) * 512].rearrange(
                            "k pp m -> pp k m")), "ldx", 16)
                    p.inc(sy.dma_start(
                        out=cosg[gb][:, :],
                        in_=cosg_h[:, g_ * 512:(g_ + 1) * 512]), "ldx", 16)
                    p.inc(sy.dma_start(
                        out=sing[gb][:, :],
                        in_=sing_h[:, g_ * 512:(g_ + 1) * 512]), "ldx", 16)
                ld_need = p.cnt["ldx"]

                @blk.gpsimd
                def _(g, g_=g_, gb=gb):
                    if g_ - 2 in kv_last:
                        p.wait_at(g, "b2mm", kv_last[g_ - 2])
                    p.wait(g, "b2ld")
                    p.inc(g.dma_start(
                        out=utb[gb].ap().rearrange("pp (k m) -> pp k m",
                                                   k=16),
                        in_=uf[:, :, g_ * 512:(g_ + 1) * 512].rearrange(
                            "k pp m -> pp k m")), "b2ld", 16)
                ldb_need = p.cnt["b2ld"]

                @blk.vector
                def _(v, g_=g_, gb=gb, ld_need=ld_need):
                    p.wait_at(v, "ldx", ld_need)
                    if g_ - 1 in stats_mm:
                        p.wait_at(v, "b2mm", stats_mm[g_ - 1])
                    p.inc(v.tensor_mul(sqln[:, :], ut[gb][:, :],
                                       ut[gb][:, :]), "b2vv")
                sq_done[g_] = p.cnt["b2vv"]

                @blk.tensor
                def _(t, g_=g_, gb=gb, ld_need=ld_need):
                    p.wait_at(t, "ldx", ld_need)
                    p.wait_at(t, "b2vv", sq_done[g_])
                    if g_ - 1 in statv_done:
                        p.wait_at(t, "b2vv", statv_done[g_ - 1])
                    for k in range(16):
                        t.matmul(psb[3][:, :], p.ones_f32[:, :],
                                 ut[gb][:, k * 512:(k + 1) * 512],
                                 start=(k == 0), stop=(k == 15))
                        ins = t.matmul(psb[4][:, :], p.ones_bf[:, :],
                                       sqln[:, k * 512:(k + 1) * 512],
                                       start=(k == 0), stop=(k == 15))
                    p.inc(ins, "b2mm")
                stats_mm[g_] = p.cnt["b2mm"]

                @blk.vector
                def _(v, g_=g_, gb=gb):
                    p.wait_at(v, "b2mm", stats_mm[g_])
                    p.wait_at(v, "b2vv", wsum_done)
                    if g_ - 2 in store_done:
                        p.wait_at(v, "kvd", store_done[g_ - 2])
                    p.edge(v, v.tensor_scalar(mrow2[gb][:, :], psb[3][:, :],
                                              p.cp("inv_d"), None, ALU.mult),
                           "b2vv")
                    p.edge(v, v.tensor_mul(tln[:, :], mrow2[gb][:, :],
                                           mrow2[gb][:, :]), "b2vv")
                    p.inc(v.scalar_tensor_tensor(
                        vrow2[gb][:, :], psb[4][:, :], p.cp("inv_d"),
                        tln[:, :], ALU.mult, ALU.subtract), "b2vv")
                statv_done[g_] = p.cnt["b2vv"]

                @blk.scalar
                def _(s, g_=g_, gb=gb):
                    p.wait_at(s, "b2vv", statv_done[g_])
                    p.edge(s, s.activation(vrow2[gb][:, :], vrow2[gb][:, :],
                                           AF.Ln, bias=p.cp("ln_eps")),
                           "b2ss")
                    p.inc(s.activation(vrow2[gb][:, :], vrow2[gb][:, :],
                                       AF.Exp, scale=p.cp("neg_half")),
                          "b2ss")
                rstd_ss = p.cnt["b2ss"]

                @blk.tensor
                def _(t, g_=g_, gb=gb, ldb_need=ldb_need, rstd_ss=rstd_ss):
                    p.wait_at(t, "b2ld", ldb_need)
                    for m in range(5):
                        w = 128 if m < 4 else 64
                        key = f"kv{m % 2}"
                        if key in bank2:
                            t.wait_ge(p.sem["b2vv"], bank2[key])
                        for k in range(16):
                            ins = t.matmul(
                                psb[m % 2][0:w, :],
                                wkva[:, k * 576 + m * 128:
                                     k * 576 + m * 128 + w],
                                utb[gb][:, k * 512:(k + 1) * 512],
                                start=(k == 0), stop=(k == 15))
                        mmv = p.cnt["b2mm"] + 1
                        p.inc(ins, "b2mm")

                        @blk.vector
                        def _(v, m=m, mmv=mmv, gb=gb, rstd_ss=rstd_ss):
                            p.wait_at(v, "b2mm", mmv)
                            p.wait_at(v, "b2ss", rstd_ss)
                            if m < 4:
                                p.edge(v, v.tensor_scalar(
                                    tln[:, :], mrow2[gb][:, :],
                                    wsum[:, m:m + 1], None, ALU.mult),
                                    "b2vv")
                                p.edge(v, v.tensor_sub(
                                    t1b[:, :], psb[m % 2][:, :], tln[:, :]),
                                    "b2vv")
                                p.edge(v, v.tensor_mul(
                                    t1b[:, :], t1b[:, :], vrow2[gb][:, :]),
                                    "b2vv")
                                p.edge(v, v.tensor_scalar(
                                    kvt[:, m * 512:(m + 1) * 512], t1b[:, :],
                                    wkvab[:, m:m + 1], None, ALU.add),
                                    "b2vv")
                                p.inc(v.tensor_mul(
                                    sqkv[:, m * 512:(m + 1) * 512],
                                    kvt[:, m * 512:(m + 1) * 512],
                                    kvt[:, m * 512:(m + 1) * 512]), "b2vv")
                            else:
                                p.edge(v, v.tensor_scalar(
                                    tln[0:64, :], mrow2[gb][0:64, :],
                                    wsum[0:64, 4:5], None, ALU.mult), "b2vv")
                                p.edge(v, v.tensor_sub(
                                    t1b[0:64, :], psb[m % 2][0:64, :],
                                    tln[0:64, :]), "b2vv")
                                p.edge(v, v.tensor_mul(
                                    t1b[0:64, :], t1b[0:64, :],
                                    vrow2[gb][0:64, :]), "b2vv")
                                p.inc(v.tensor_scalar(
                                    kperaw[:, :], t1b[0:64, :],
                                    wkvab[0:64, 4:5], None, ALU.add), "b2vv")
                        bank2[key] = p.cnt["b2vv"]
                kv_last[g_] = p.cnt["b2mm"]
                kv_done = p.cnt["b2vv"]

                @blk.tensor
                def _(t, g_=g_, kv_done=kv_done):
                    p.wait_at(t, "b2vv", kv_done)
                    if "r2" in bank2:
                        t.wait_ge(p.sem["b2vv"], bank2["r2"])
                    for m in range(4):
                        ins = t.matmul(psb[2][:, :], p.ones_bf[:, :],
                                       sqkv[:, m * 512:(m + 1) * 512],
                                       start=(m == 0), stop=(m == 3))
                    p.inc(ins, "b2mm")

                @blk.scalar
                def _(s, g_=g_):
                    p.wait(s, "b2mm")
                    p.edge(s, s.activation(rstd2[:, :], psb[2][:, :], AF.Ln,
                                           bias=p.cp("rms_eps"),
                                           scale=p.cp("inv_kvlr")), "b2ss")
                    p.inc(s.activation(rstd2[:, :], rstd2[:, :], AF.Exp,
                                       scale=p.cp("neg_half")), "b2ss")

                @blk.vector
                def _(v, g_=g_):
                    p.wait(v, "b2ss")
                    for m in range(4):
                        p.edge(v, v.tensor_mul(t1b[:, :],
                                               kvt[:, m * 512:(m + 1) * 512],
                                               rstd2[:, :]), "b2vv")
                        ins = v.tensor_scalar(kvbf[:, m * 512:(m + 1) * 512],
                                              t1b[:, :], kvnw[:, m:m + 1],
                                              None, ALU.mult)
                        if m < 3:
                            p.edge(v, ins, "b2vv")
                    p.inc(ins, "b2vv")
                bank2["r2"] = p.cnt["b2vv"]

                @blk.scalar
                def _(s, g_=g_):
                    p.wait(s, "b2vv")
                    p.edge(s, s.mul(kperot[0:32, :], kperaw[32:64, :],
                                    p.cp("neg_one")[0:32, :]), "b2ss")
                    p.inc(s.copy(kperot[32:64, :], kperaw[0:32, :]), "b2ss")

                @blk.vector
                def _(v, g_=g_, gb=gb):
                    p.wait(v, "b2ss")
                    p.edge(v, v.tensor_mul(
                        kpea[:, :], kperaw[:, :], cosg[gb][:, :]), "b2vv")
                    p.edge(v, v.tensor_mul(
                        kperot[:, :], kperot[:, :], sing[gb][:, :]), "b2vv")
                    p.inc(v.tensor_add(kpebf[:, :], kpea[:, :],
                                       kperot[:, :]), "b2vv")
                rope_done[g_] = p.cnt["b2vv"]

                @blk.sync
                def _(sy, g_=g_, gb=gb):
                    p.wait(sy, "b2vv")
                    p.wait(sy, "b2ss")
                    p.inc(sy.dma_start(
                        out=kvfm_g.ap().rearrange(
                            "pp (m t) -> pp m t", m=4)[:, :,
                                                       g_ * 512:(g_ + 1) * 512],
                        in_=kvbf.ap().rearrange("pp (m t) -> pp m t", m=4)),
                        "kvd", 16)
                    p.inc(sy.dma_start(
                        out=kpe_g.ap()[0:64, g_ * 512:(g_ + 1) * 512],
                        in_=kpebf[:, :]), "kvd", 16)
                    p.inc(sy.dma_start(
                        out=mr2_d.ap()[0:1, g_ * 512:(g_ + 1) * 512],
                        in_=mrow2[gb][0:1, :]), "kvd", 16)
                    p.inc(sy.dma_start(
                        out=mr2_d.ap()[1:2, g_ * 512:(g_ + 1) * 512],
                        in_=vrow2[gb][0:1, :]), "kvd", 16)
                    if g_ == 0:
                        for cc_ in range(16):
                            p.inc(sy.dma_start(
                                out=kpe_g.ap()[65:66,
                                               cc_ * 512:(cc_ + 1) * 512],
                                in_=zeros_row[0:1, :]), "kvd", 16)
                store_done[g_] = p.cnt["kvd"]

        with ExitStack() as b4:
            en4 = b4.enter_context
            coso = en4(nc.sbuf_tensor("b4cos", [64, TPC], F32))
            sino = en4(nc.sbuf_tensor("b4sin", [64, TPC], F32))
            wqab = en4(nc.sbuf_tensor("b4wqab", [128, 12], F32))
            qnw = en4(nc.sbuf_tensor("b4qnw", [128, 12], F32))
            rstd4 = en4(nc.sbuf_tensor("b4rstd", [128, TPC], F32))
            mroM = en4(nc.sbuf_tensor("b4mroM", [1, TPC], F32))
            mroR = en4(nc.sbuf_tensor("b4mroR", [1, TPC], F32))
            m4bc = en4(nc.sbuf_tensor("b4mbc", [128, TPC], F32))
            r4bc = en4(nc.sbuf_tensor("b4rbc", [128, TPC], F32))
            wsumq = en4(nc.sbuf_tensor("b4wsq", [128, 12], F32))
            t14 = en4(nc.sbuf_tensor("b4t1", [128, TPC], F32))
            qn = en4(nc.sbuf_tensor("b4qn", [128, 12 * TPC], BF16))
            ps4 = [en4(nc.psum_tensor(f"b4ps{i}", [128, 512], F32))
                   for i in range(6)]
            p.new_sem("b4mm", "b4vv", "b4ss", "b4w")
            b4s1 = ExitStack()
            en41 = b4s1.enter_context
            qlo = en41(nc.sbuf_tensor("b4qlo", [128, 12 * TPC], BF16))
            sq4 = en41(nc.sbuf_tensor("b4sq", [128, 12 * TPC], BF16))
            xho = en41(nc.sbuf_tensor("b4xh", [128, 16 * TPC], BF16))
            wqa = [en41(nc.sbuf_tensor(f"b4wqa{i}", [128, 16 * 512], BF16))
                   for i in range(2)]

            @blk.sync
            def _(sy):
                p.wait(sy, "kvd")
                p.inc(sy.dma_start(out=coso[:, :], in_=coso_h), "ldx", 16)
                p.inc(sy.dma_start(out=sino[:, :], in_=sino_h), "ldx", 16)
                p.inc(sy.dma_start(out=wqab[:, :], in_=wqab_h), "ldx", 16)
                p.inc(sy.dma_start(out=qnw[:, :], in_=qnw_h), "ldx", 16)
                with sy.register("ost") as r2:
                    sy.reg_load(r2, p.meta_sb[0:1, 5:6])
                    ot2 = sy.snap(r2, min_val=0, max_val=14)
                    mrv = mr2_d.ap().rearrange("r (a m) -> r a m", m=512)
                    p.inc(sy.dma_start(
                        out=mroM.ap().rearrange("r (a m) -> r a m", a=2),
                        in_=mrv[0:1, bass.ds(ot2, 2), :]), "ldx", 16)
                    p.inc(sy.dma_start(
                        out=mroR.ap().rearrange("r (a m) -> r a m", a=2),
                        in_=mrv[1:2, bass.ds(ot2, 2), :]), "ldx", 16)
            mro_need = p.cnt["ldx"]

            @blk.gpsimd
            def _(g):
                p.wait(g, "kvd")
                p.inc(g.dma_start(
                    out=xho.ap().rearrange("pp (k m) -> pp k m", k=16),
                    in_=uo.rearrange("k pp m -> pp k m")), "b4w", 16)
            xho_need = p.cnt["b4w"]

            @blk.tensor
            def _(t, mro_need=mro_need):
                p.wait_at(t, "ldx", mro_need)
                for r_ in range(2):
                    src = mroM if r_ == 0 else mroR
                    for a_ in range(2):
                        p.inc(t.matmul(
                            ps4[2 + a_][:, :], p.ones_col[0:1, :],
                            src[0:1, a_ * 512:a_ * 512 + 512],
                            start=True, stop=True), "b4mm")
                    mmb = p.cnt["b4mm"]

                    @blk.vector
                    def _(v, r_=r_, mmb=mmb):
                        p.wait_at(v, "b4mm", mmb)
                        dst = m4bc if r_ == 0 else r4bc
                        for a_ in range(2):
                            ins = v.tensor_copy(
                                dst[:, a_ * 512:a_ * 512 + 512],
                                ps4[2 + a_][:, :])
                            if a_ == 0:
                                p.edge(v, ins, "b4vv")
                        p.inc(ins, "b4vv")
                    vvb = p.cnt["b4vv"]
                    t.wait_ge(p.sem["b4vv"], vvb)
            bc_done = p.cnt["b4vv"]

            lnr3 = "ldx"
            bank_free = {}

            for sl in range(3):
                @blk.gpsimd
                def _(g, sl=sl):
                    if sl >= 2:
                        p.wait_at(g, "b4mm", p.cnt["b4mm"])
                    else:
                        p.wait(g, "kvd")
                    p.wait(g, "b4w")
                    p.inc(g.dma_start(
                        out=wqa[sl % 2].ap().rearrange(
                            "pp (k m) -> pp k m", k=16),
                        in_=wqaT_h[:, :, sl * 512:(sl + 1) * 512].rearrange(
                            "k pp m -> pp k m")), "b4w", 16)
                w_need = p.cnt["b4w"]

                @blk.tensor
                def _(t, sl=sl, w_need=w_need):
                    p.wait_at(t, "b4w", w_need)
                    p.wait_at(t, "b4w", xho_need)
                    p.wait(t, lnr3)
                    if "r5" in bank_free:
                        t.wait_ge(p.sem["b4ss"], bank_free["r5"])
                    for ml in range(4):
                        m = sl * 4 + ml
                        for k in range(16):
                            ins = t.matmul(
                                ps4[5][:, m:m + 1],
                                wqa[sl % 2][:, k * 512 + ml * 128:
                                            k * 512 + ml * 128 + 128],
                                p.ones_bf[:, 0:1],
                                start=(k == 0), stop=(k == 15))
                    mmw = p.cnt["b4mm"] + 1
                    p.inc(ins, "b4mm")

                    @blk.vector
                    def _(v, sl=sl, mmw=mmw):
                        p.wait_at(v, "b4mm", mmw)
                        p.inc(v.tensor_copy(wsumq[:, sl * 4:sl * 4 + 4],
                                            ps4[5][:, sl * 4:sl * 4 + 4]),
                              "b4ss")
                    bank_free["r5"] = p.cnt["b4ss"]
                    ws_done = p.cnt["b4ss"]
                    for ml in range(4):
                        m = sl * 4 + ml
                        for hh in range(NH):
                            key = f"ql{ml % 2}"
                            if key in bank_free:
                                t.wait_ge(p.sem["b4ss"], bank_free[key])
                            bk = ps4[ml % 2]
                            for k in range(16):
                                ins = t.matmul(
                                    bk[:, :],
                                    wqa[sl % 2][:, k * 512 + ml * 128:
                                                k * 512 + ml * 128 + 128],
                                    xho[:, k * TPC + hh * 512:
                                        k * TPC + hh * 512 + 512],
                                    start=(k == 0), stop=(k == 15))
                            mmv = p.cnt["b4mm"] + 1
                            p.inc(ins, "b4mm")

                            @blk.vector
                            def _(v, m=m, hh=hh, mmv=mmv, bk=bk,
                                  ws_done=ws_done):
                                p.wait_at(v, "b4mm", mmv)
                                p.wait_at(v, "b4ss", ws_done)
                                p.wait_at(v, "b4vv", bc_done)
                                p.edge(v, v.tensor_scalar(
                                    t14[:, 0:512],
                                    m4bc[:, hh * 512:hh * 512 + 512],
                                    wsumq[:, m:m + 1], None, ALU.mult),
                                    "b4ss")
                                p.edge(v, v.tensor_sub(
                                    t14[:, 0:512], bk[:, :], t14[:, 0:512]),
                                    "b4ss")
                                p.edge(v, v.tensor_mul(
                                    t14[:, 0:512], t14[:, 0:512],
                                    r4bc[:, hh * 512:hh * 512 + 512]),
                                    "b4ss")
                                p.edge(v, v.tensor_scalar(
                                    qlo[:, m * TPC + hh * 512:
                                        m * TPC + hh * 512 + 512],
                                    t14[:, 0:512], wqab[:, m:m + 1],
                                    None, ALU.add), "b4ss")
                                p.inc(v.tensor_mul(
                                    sq4[:, m * TPC + hh * 512:
                                        m * TPC + hh * 512 + 512],
                                    qlo[:, m * TPC + hh * 512:
                                        m * TPC + hh * 512 + 512],
                                    qlo[:, m * TPC + hh * 512:
                                        m * TPC + hh * 512 + 512]), "b4ss")
                            bank_free[key] = p.cnt["b4ss"]

            @blk.tensor
            def _(t):
                p.wait(t, "b4ss")
                for hh in range(NH):
                    if "r5" in bank_free:
                        t.wait_ge(p.sem["b4ss"], bank_free["r5"])
                    for m in range(12):
                        ins = t.matmul(ps4[5][:, :], p.ones_bf[:, :],
                                       sq4[:, m * TPC + hh * 512:
                                           m * TPC + hh * 512 + 512],
                                       start=(m == 0), stop=(m == 11))
                    mmv = p.cnt["b4mm"] + 1
                    p.inc(ins, "b4mm")

                    @blk.scalar
                    def _(s, hh=hh, mmv=mmv):
                        p.wait_at(s, "b4mm", mmv)
                        p.edge(s, s.activation(
                            rstd4[:, hh * 512:hh * 512 + 512], ps4[5][:, :],
                            AF.Ln, bias=p.cp("rms_eps"),
                            scale=p.cp("inv_qlr")), "b4ss")
                        p.inc(s.activation(
                            rstd4[:, hh * 512:hh * 512 + 512],
                            rstd4[:, hh * 512:hh * 512 + 512],
                            AF.Exp, scale=p.cp("neg_half")), "b4ss")
                    bank_free["r5"] = p.cnt["b4ss"]

            @blk.vector
            def _(v):
                p.wait(v, "b4ss")
                for m in range(12):
                    p.edge(v, v.tensor_mul(t14[:, :],
                                           qlo[:, m * TPC:(m + 1) * TPC],
                                           rstd4[:, :]), "b4vv")
                    ins = v.tensor_scalar(
                        qn[:, m * TPC:(m + 1) * TPC], t14[:, :],
                        qnw[:, m:m + 1], None, ALU.mult)
                    if m < 11:
                        p.edge(v, ins, "b4vv")
                p.inc(ins, "b4vv")
            qn_done = p.cnt["b4vv"]

            @blk.sync
            def _(sy):
                p.wait(sy, "b4vv")
                p.wait(sy, "b4ss")
                p.inc(sy.dma_start(out=win_mark0[:, :], in_=cpool_h),
                      "qpk", 16)
            b4s1.close()
            wqb = [en4(nc.sbuf_tensor(f"b4wqb{i}", [128, 12 * 192], BF16))
                   for i in range(2)]
            wqbb = en4(nc.sbuf_tensor("b4wqbb", [128, 2], F32))
            qpe = en4(nc.sbuf_tensor("b4qpe", [64, TPC], F32))
            qper = en4(nc.sbuf_tensor("b4qper", [64, TPC], F32))
            qnope = en4(nc.sbuf_tensor("b4qnope", [128, TPC], BF16))
            qaug = en4(nc.sbuf_tensor("b4qaug", [66, TPC], BF16))

            @blk.sync
            def _(sy):
                p.wait(sy, "qpk")
                for cc_ in range(2):
                    p.inc(sy.dma_start(
                        out=qaug.ap()[65:66, cc_ * 512:(cc_ + 1) * 512],
                        in_=ones_row[0:1, :]), "qpk", 16)

            for h in range(H):
                @blk.gpsimd
                def _(g, h=h):
                    if h >= 2:
                        p.wait_at(g, "b4mm", p.cnt["b4mm"])
                    else:
                        p.wait(g, "qpk")
                    p.wait(g, "b4w")
                    p.inc(g.dma_start(
                        out=wqb[h % 2].ap().rearrange(
                            "pp (k m) -> pp k m", k=12),
                        in_=wqbT_h[h].rearrange("k pp m -> pp k m")),
                        "b4w", 16)

                @blk.sync
                def _(sy, h=h):
                    p.wait(sy, "qpk")
                    p.wait(sy, "ldx")
                    p.inc(sy.dma_start(out=wqbb[:, :], in_=wqbb_h[h]),
                          "ldx", 16)
                wqbb_need = p.cnt["ldx"]
                w_need = p.cnt["b4w"]

                @blk.vector
                def _(v, h=h, wqbb_need=wqbb_need):
                    p.wait_at(v, "ldx", wqbb_need)
                    p.wait(v, "b4vv")
                    p.inc(v.tensor_scalar(wqbb[:, :], wqbb[:, :],
                                          p.cp("scale"), None, ALU.mult),
                          "b4vv")
                bias_done = p.cnt["b4vv"]

                @blk.tensor
                def _(t, h=h, w_need=w_need, bias_done=bias_done):
                    p.wait_at(t, "b4w", w_need)
                    p.wait_at(t, "b4vv", max(qn_done, bias_done))
                    p.wait(t, "qpk")
                    for mq in range(2):
                        w = 128 if mq == 0 else 64
                        for hh in range(NH):
                            key = f"qf{mq}"
                            if key in bank_free:
                                t.wait_ge(p.sem["b4ss"], bank_free[key])
                            for k in range(12):
                                ins = t.matmul(
                                    ps4[2 + mq][0:w, :],
                                    wqb[h % 2][:, k * 192 + mq * 128:
                                               k * 192 + mq * 128 + w],
                                    qn[:, k * TPC + hh * 512:
                                       k * TPC + hh * 512 + 512],
                                    start=(k == 0), stop=(k == 11))
                            mmv = p.cnt["b4mm"] + 1
                            p.inc(ins, "b4mm")

                            @blk.vector
                            def _(v, mq=mq, w=w, hh=hh, mmv=mmv):
                                p.wait_at(v, "b4mm", mmv)
                                if mq == 0:
                                    p.inc(v.tensor_scalar(
                                        qnope[:, hh * 512:hh * 512 + 512],
                                        ps4[2][:, :], p.cp("scale"),
                                        wqbb[:, 0:1], ALU.mult, ALU.add),
                                        "b4ss")
                                else:
                                    p.inc(v.tensor_scalar(
                                        qpe[:, hh * 512:hh * 512 + 512],
                                        ps4[3][0:64, :],
                                        p.cp("scale")[0:64, :],
                                        wqbb[0:64, 1:2], ALU.mult, ALU.add),
                                        "b4ss")
                            bank_free[key] = p.cnt["b4ss"]
                qf_done = p.cnt["b4ss"]

                @blk.scalar
                def _(s, h=h, qf_done=qf_done):
                    p.wait_at(s, "b4ss", qf_done)
                    p.wait(s, "qpk")
                    p.edge(s, s.mul(qper[0:32, :], qpe[32:64, :],
                                    p.cp("neg_one")[0:32, :]), "b4ss")
                    p.inc(s.copy(qper[32:64, :], qpe[0:32, :]), "b4ss")

                @blk.vector
                def _(v, h=h):
                    p.wait(v, "b4ss")
                    p.wait(v, "qpk")
                    p.edge(v, v.tensor_mul(qpe[:, :], qpe[:, :], coso[:, :]),
                           "b4vv")
                    p.edge(v, v.tensor_mul(qper[:, :], qper[:, :],
                                           sino[:, :]), "b4vv")
                    p.edge(v, v.tensor_add(qpe[:, :], qpe[:, :], qper[:, :]),
                           "b4vv")
                    p.inc(v.tensor_copy(qaug[0:64, :], qpe[:, :]), "b4vv")

                @blk.sync
                def _(sy, h=h):
                    p.wait(sy, "b4ss")
                    p.wait(sy, "b4vv")
                    p.inc(sy.dma_start(
                        out=qaug_d.ap()[h, 0:64, :], in_=qaug[0:64, :]),
                        "qpk", 16)
                    p.inc(sy.dma_start(
                        out=qaug_d.ap()[h, 65:66, :], in_=qaug[65:66, :]),
                        "qpk", 16)
                    p.inc(sy.dma_start(out=qn_d.ap()[h], in_=qnope[:, :]),
                          "qpk", 16)

        o_sb = p.sbuf("o_sbr", [128, 16 * TPC], BF16)
        with ExitStack() as b5:
            en5 = b5.enter_context
            kvfm = en5(nc.sbuf_tensor("kvfm", [128, 4 * S], BF16))
            kpew = en5(nc.sbuf_tensor("kpew", [66, S], BF16))
            knop = en5(nc.sbuf_tensor("b5kn", [128, S], BF16))
            vsb = en5(nc.sbuf_tensor("b5vs", [128, S], BF16))
            wkvvT = en5(nc.sbuf_tensor("wkvvT_sb", [128, H * 4 * 128], BF16))
            wknT = en5(nc.sbuf_tensor("wknT_sb", [128, H * 4 * 128], BF16))
            kmask = en5(nc.sbuf_tensor("kmask_sb", [128, 4 * 512], BF16))
            dmask5 = en5(nc.sbuf_tensor("b5dm", [128, 128], F32))
            qns4 = [en5(nc.sbuf_tensor(f"b5qn{i}", [128, TPC], BF16))
                    for i in range(2)]
            qgs4 = [en5(nc.sbuf_tensor(f"b5qg{i}", [66, TPC], BF16))
                    for i in range(2)]
            probs = [en5(nc.sbuf_tensor(f"b5pr{i}", [128, 512], BF16))
                     for i in range(3)]
            denr = en5(nc.sbuf_tensor("b5den", [128, 512], F32))
            m8 = en5(nc.sbuf_tensor("b5m8", [128, 8], F32))
            m8b = en5(nc.sbuf_tensor("b5m8b", [128, 8], BF16))
            mr = en5(nc.sbuf_tensor("b5mr", [8, 128], BF16))
            tdg = en5(nc.sbuf_tensor("b5td", [128, 128], F32))
            win_mark = en5(nc.sbuf_tensor("win_mark", [128, 16], F32))
            psS = [en5(nc.psum_tensor(f"b5s{i}", [128, 512], F32))
                   for i in range(2)]
            psO = [en5(nc.psum_tensor(f"b5o{i}", [128, 512], F32))
                   for i in range(2)]
            psD = en5(nc.psum_tensor("b5d", [128, 512], F32))
            psE = [en5(nc.psum_tensor(f"b5e{i}", [128, 512], F32))
                   for i in range(2)]
            ptm = en5(nc.psum_tensor("b5pt", [128, 128], BF16))
            p.new_sem("b5mm", "b5vv", "b5ss", "b5q", "qgr")

            @blk.gpsimd
            def _(g):
                p.wait(g, "qpk")
                p.inc(g.dma_start(
                    out=wkvvT.ap().rearrange("pp (h c m) -> pp h c m",
                                             h=H, c=4),
                    in_=wkvvT_h.rearrange("h c pp m -> pp h c m")),
                    "ldw", 16)
                p.inc(g.dma_start(
                    out=wknT.ap().rearrange("pp (h c m) -> pp h c m",
                                            h=H, c=4),
                    in_=wknT_h.rearrange("h c pp m -> pp h c m")), "ldw", 16)
                p.inc(g.dma_start(
                    out=kmask.ap().rearrange("pp (r m) -> pp r m", r=4),
                    in_=kmask_h.rearrange("r pp m -> pp r m")), "ldw", 16)

            @blk.sync
            def _(sy):
                p.wait(sy, "qpk")
                p.inc(sy.dma_start(out=dmask5[:, :], in_=dmask_h), "ldx", 16)
            dm_need = p.cnt["ldx"]

            @blk.vector
            def _(v):
                p.wait(v, "qpk")
                p.inc(v.memset(win_mark[:, :], 0.0), "b5vv")

            @blk.sync
            def _(sy):
                p.wait(sy, "b5vv")
                p.wait(sy, "kvd")
                p.wait(sy, "ldx")
                winfm = kvfm_d.ap().rearrange("pp (b t) -> pp b t", t=512)
                glofm = kvfm_g.ap().rearrange("pp (b t) -> pp b t", t=512)
                winpe = kpe_d.ap().rearrange("pp (b t) -> pp b t", t=512)
                glope = kpe_g.ap().rearrange("pp (b t) -> pp b t", t=512)
                with sy.register("pB") as r0:
                    sy.reg_load(r0, p.meta_sb[0:1, 3:4])
                    padB = sy.snap(r0, min_val=0, max_val=(S - TPC) // 512)
                    for m in range(4):
                        p.inc(sy.dma_start(
                            out=winfm[:, bass.ds(padB + m * 32, 16), :],
                            in_=glofm[:, m * 16:(m + 1) * 16, :]),
                            "win", 16)
                    p.inc(sy.dma_start(
                        out=winpe[0:64, bass.ds(padB, 16), :],
                        in_=glope[0:64, :, :]), "win", 16)
                    p.inc(sy.dma_start(
                        out=winpe[65:66, bass.ds(padB, 16), :],
                        in_=glope[65:66, :, :]), "win", 16)
                p.wait(sy, "win")
                p.inc(sy.dma_start(
                    out=kvfm.ap().rearrange("pp (m t) -> pp m t", m=4),
                    in_=kvfm_d.ap().rearrange(
                        "pp (m t) -> pp m t", m=4)[:, :, 0:S]), "odw", 16)
                p.inc(sy.dma_start(out=kpew.ap()[:, :],
                                   in_=kpe_d.ap()[:, 0:S]), "odw", 16)

            eg = [0]

            def emit_kng(g_, h, first=False):
                par = eg[0] % 2
                eg[0] += 1

                @blk.tensor
                def _(t, g_=g_, h=h, par=par, first=first):
                    if first:
                        p.wait(t, "odw")
                        p.wait(t, "ldw")
                    p.bank_wait(t, [f"b5e{par}"])
                    for c in range(4):
                        ins = t.matmul(
                            psE[par][:, :],
                            wknT[:, h * 512 + c * 128:h * 512 + c * 128 + 128],
                            kvfm[:, c * S + g_ * 512:c * S + g_ * 512 + 512],
                            start=(c == 0), stop=(c == 3))
                    p.inc(ins, "b5mm")
                mmv = p.cnt["b5mm"]

                @blk.vector
                def _(v, g_=g_, par=par, mmv=mmv):
                    p.wait_at(v, "b5mm", mmv)
                    p.inc(v.tensor_copy(
                        knop[:, g_ * 512:(g_ + 1) * 512], psE[par][:, :]),
                        "b5vv")
                p.bank_set([f"b5e{par}"], "b5vv")

            def emit_vg(g_, h):
                par = eg[0] % 2
                eg[0] += 1

                @blk.tensor
                def _(t, g_=g_, h=h, par=par):
                    p.bank_wait(t, [f"b5e{par}"])
                    for j in range(4):
                        for c in range(4):
                            ins = t.matmul(
                                psE[par][:, j * 128:j * 128 + 128],
                                kvfm[:, c * S + g_ * 512 + j * 128:
                                     c * S + g_ * 512 + j * 128 + 128],
                                wkvvT[:, h * 512 + c * 128:
                                      h * 512 + c * 128 + 128],
                                start=(c == 0), stop=(c == 3))
                    p.inc(ins, "b5mm")
                mmv = p.cnt["b5mm"]

                @blk.vector
                def _(v, g_=g_, par=par, mmv=mmv):
                    p.wait_at(v, "b5mm", mmv)
                    p.inc(v.tensor_copy(
                        vsb[:, g_ * 512:(g_ + 1) * 512], psE[par][:, :]),
                        "b5vv")
                p.bank_set([f"b5e{par}"], "b5vv")

            for h in range(H):
                par_h = h % 2

                @blk.sync
                def _(sy, h=h, par_h=par_h):
                    if h < 2:
                        p.wait(sy, "qpk")
                    p.bank_wait(sy, [f"b5qb{par_h}"])
                    p.wait(sy, "b5q")
                    p.inc(sy.dma_start(out=qns4[par_h][:, :],
                                       in_=qn_d.ap()[h]), "b5q", 16)
                    p.inc(sy.dma_start(out=qgs4[par_h][:, :],
                                       in_=qaug_d.ap()[h]), "b5q", 16)
                q_need = p.cnt["b5q"]

                emit_kng(14, h, first=(h == 0))
                emit_kng(15, h)
                kn_tail = p.cnt["b5vv"]

                for qb in range(8):
                    sbq, j = qb // 4, qb % 4
                    td_ = DIAG_T[sbq][j]
                    par = eg[0] % 2
                    eg[0] += 1

                    @blk.tensor
                    def _(t, h=h, qb=qb, sbq=sbq, j=j, td_=td_, par=par,
                          q_need=q_need, kn_tail=kn_tail, par_h=par_h):
                        if qb == 0:
                            p.wait_at(t, "b5q", q_need)
                            p.wait_at(t, "b5vv", kn_tail)
                        p.bank_wait(t, [f"b5e{par}"])
                        t.matmul(psE[par][:, 0:128],
                                 qns4[par_h][:, sbq * 512 + j * 128:
                                             sbq * 512 + j * 128 + 128],
                                 knop[:, td_ * 128:td_ * 128 + 128],
                                 start=True, stop=False)
                        ins = t.matmul(
                            psE[par][:, 0:128],
                            qgs4[par_h][0:64, sbq * 512 + j * 128:
                                        sbq * 512 + j * 128 + 128],
                            kpew[0:64, td_ * 128:td_ * 128 + 128],
                            start=False, stop=True)
                        p.inc(ins, "b5mm")
                    mmv = p.cnt["b5mm"]

                    @blk.vector
                    def _(v, qb=qb, par=par, mmv=mmv):
                        p.wait_at(v, "b5mm", mmv)
                        p.wait_at(v, "ldx", dm_need)
                        p.edge(v, v.tensor_add(tdg[:, :],
                                               psE[par][:, 0:128],
                                               dmask5[:, :]), "b5vv")
                        p.inc(v.reduce_max(m8[:, qb:qb + 1], tdg[:, :],
                                           axis=AX.X), "b5vv")
                    p.bank_set([f"b5e{par}"], "b5vv")

                @blk.vector
                def _(v, h=h):
                    p.wait(v, "b5vv")
                    p.inc(v.tensor_scalar(m8b[:, :], m8[:, :],
                                          p.cp("neg_one"), None, ALU.mult),
                          "b5vv")
                neg_done = p.cnt["b5vv"]

                @blk.tensor
                def _(t, h=h, neg_done=neg_done):
                    p.wait_at(t, "b5vv", neg_done)
                    p.bank_wait(t, ["b5pt"])
                    p.inc(t.transpose(ptm[0:8, :], m8b[:, :],
                                      p.ident_bf[:, :]), "b5mm")
                mmv2 = p.cnt["b5mm"]

                @blk.vector
                def _(v, h=h, mmv2=mmv2):
                    p.wait_at(v, "b5mm", mmv2)
                    p.bank_wait(v, ["b5mr"])
                    p.inc(v.tensor_copy(mr[:, :], ptm[0:8, :]), "b5vv")
                p.bank_set(["b5pt"], "b5vv")
                mr_done = p.cnt["b5vv"]

                @blk.sync
                def _(sy, h=h, mr_done=mr_done):
                    p.wait_at(sy, "b5vv", mr_done)
                    p.wait(sy, "qgr")
                    p.wait(sy, "b5q")
                    p.inc(sy.dma_start(out=mrow_d.ap(), in_=mr[:, :]),
                          "b5q", 16)
                p.bank_set(["b5mr"], "b5q")
                mrd = p.cnt["b5q"]

                @blk.sync
                def _(sy, h=h, mrd=mrd, par_h=par_h):
                    p.wait_at(sy, "b5q", mrd)
                    p.inc(sy.dma_start(
                        out=qgs4[par_h].ap()[64:65, :],
                        in_=mrow_d.ap().rearrange("(a b) q -> a (b q)", a=1)),
                        "qgr", 16)
                qgr_need = p.cnt["qgr"]

                for g_ in range(14):
                    emit_kng(g_, h)
                kn_all = p.cnt["b5vv"]
                for g_ in range(16):
                    emit_vg(g_, h)
                v_all = p.cnt["b5vv"]

                for sb in range(2):
                    ih2 = h * 2 + sb
                    opar = ih2 % 2
                    LT = LAST_T[sb]
                    ss0 = p.cnt["b5ss"]
                    mm_sc = {}
                    pv_mm = {}
                    pv_ready = {}

                    def emit_scores_t(t_):
                        sbk = t_ % 2

                        @blk.tensor
                        def _(t, sb=sb, t_=t_, sbk=sbk, opar=opar,
                              par_h=par_h, ss0=ss0):
                            if t_ == 0:
                                p.wait_at(t, "qgr", qgr_need)
                                p.wait_at(t, "b5vv", kn_all)
                                p.bank_wait(t, [f"b5o{opar}", "b5d"])
                            if t_ >= 2:
                                p.wait_at(t, "b5ss", ss0 + t_ - 1)
                            else:
                                p.bank_wait(t, [f"b5sc{sbk}"])
                            t.matmul(psS[sbk][:, :],
                                     knop[:, t_ * 128:t_ * 128 + 128],
                                     qns4[par_h][:, sb * 512:sb * 512 + 512],
                                     start=True, stop=False)
                            ins = t.matmul(
                                psS[sbk][:, :],
                                kpew[:, t_ * 128:t_ * 128 + 128],
                                qgs4[par_h][:, sb * 512:sb * 512 + 512],
                                start=False, stop=True)
                            p.inc(ins, "b5mm")
                        mm_sc[t_] = p.cnt["b5mm"]
                        pbuf = probs[t_ % 3]

                        if t_ in DIAG_T[sb]:
                            ri = t_ - DIAG_T[sb][0]

                            @blk.vector
                            def _(v, ri=ri, pbuf=pbuf, t_=t_, ss0=ss0):
                                p.wait_at(v, "b5ss", ss0 + t_ + 1)
                                p.wait(v, "ldw")
                                p.inc(v.tensor_mul(
                                    pbuf[:, :], pbuf[:, :],
                                    kmask[:, ri * 512:ri * 512 + 512]),
                                    "b5vv")
                            pv_ready[t_] = ("b5vv", p.cnt["b5vv"])
                        else:
                            pv_ready[t_] = ("b5ss", ss0 + t_ + 1)

                    def emit_pv_t(t_):
                        pbuf = probs[t_ % 3]
                        pv_sem, pv_val = pv_ready[t_]

                        @blk.tensor
                        def _(t, t_=t_, pbuf=pbuf, pv_sem=pv_sem,
                              pv_val=pv_val, opar=opar):
                            p.wait_at(t, pv_sem, pv_val)
                            if t_ == 0:
                                p.wait_at(t, "b5vv", v_all)
                            last = (t_ == LT - 1)
                            t.matmul(psO[opar][:, :],
                                     vsb[:, t_ * 128:t_ * 128 + 128],
                                     pbuf[:, :],
                                     start=(t_ == 0), stop=last)
                            ins = t.matmul(psD[:, :], p.ones_bf[:, :],
                                           pbuf[:, :],
                                           start=(t_ == 0), stop=last)
                            p.inc(ins, "b5mm")
                        pv_mm[t_] = p.cnt["b5mm"]

                    emit_scores_t(0)
                    for t_ in range(1, LT):
                        emit_scores_t(t_)
                        emit_pv_t(t_ - 1)
                    emit_pv_t(LT - 1)
                    mm_pv = p.cnt["b5mm"]

                    @blk.scalar
                    def _(s, sb=sb, LT=LT, mm_sc=dict(mm_sc)):
                        for t_ in range(LT):
                            sbk = t_ % 2
                            p.wait_at(s, "b5mm", mm_sc[t_])
                            p.inc(s.activation(probs[t_ % 3][:, :],
                                               psS[sbk][:, :], AF.Exp),
                                  "b5ss")
                    assert p.cnt["b5ss"] == ss0 + LT
                    p.bank["b5sc0"] = ("b5ss", ss0 + LT - 1)
                    p.bank["b5sc1"] = ("b5ss", ss0 + LT)
                    for idx in range(3):
                        lastt = max(t for t in range(LT) if t % 3 == idx)
                        p.bank[f"b5pb{idx}"] = ("b5mm", pv_mm[lastt])

                    @blk.vector
                    def _(v, sb=sb, h=h, mm_pv=mm_pv, opar=opar):
                        p.wait_at(v, "b5mm", mm_pv)
                        p.edge(v, v.reciprocal(denr[:, :], psD[:, :]),
                               "b5vv")
                        p.inc(v.tensor_mul(
                            o_sb[:, h * TPC + sb * 512:
                                 h * TPC + sb * 512 + 512],
                            psO[opar][:, :], denr[:, :]), "b5vv")
                    p.bank_set(["b5d", f"b5o{opar}"], "b5vv")
                p.bank_set([f"b5qb{par_h}"], "b5mm")

            @blk.sync
            def _(sy):
                p.wait(sy, "odw")
                p.wait(sy, "b5vv")
                p.wait(sy, "b5ss")
                p.wait(sy, "b5mm")
                p.inc(sy.dma_start(out=win_mark0[:, :], in_=cpool_h),
                      "odw", 16)

        v_sb_l = p.sbuf("b6v", [128, 16 * TPC], F32)
        with ExitStack() as b6:
            en6 = b6.enter_context
            wo_b_ = [en6(nc.sbuf_tensor(f"b6wo{i}", [128, 16 * 512], BF16))
                     for i in range(2)]
            t1_6 = en6(nc.sbuf_tensor("b6t1", [128, 512], F32))
            ps6_ = [en6(nc.psum_tensor(f"b6ps{i}", [128, 512], F32))
                    for i in range(4)]
            p.new_sem("b6mm", "b6vv", "b6ss", "b6w")

            @blk.sync
            def _(sy):
                p.wait(sy, "odw")
                p.inc(sy.dma_start(
                    out=v_sb_l.ap().rearrange("pp (k m) -> pp k m", k=16),
                    in_=uo.rearrange("k pp m -> pp k m")), "ldx", 16)

            bank6 = {}
            for mg in range(4):
                @blk.gpsimd
                def _(g, mg=mg):
                    if mg >= 2:
                        p.wait_at(g, "b6mm", p.cnt["b6mm"])
                    else:
                        p.wait(g, "odw")
                    p.wait(g, "b6w")
                    p.inc(g.dma_start(
                        out=wo_b_[mg % 2].ap().rearrange(
                            "pp (k m) -> pp k m", k=16),
                        in_=woT_h[:, :, mg * 512:(mg + 1) * 512].rearrange(
                            "k pp m -> pp k m")), "b6w", 16)
                w_need = p.cnt["b6w"]

                @blk.tensor
                def _(t, mg=mg, w_need=w_need):
                    p.wait_at(t, "b6w", w_need)
                    p.wait(t, "ldx")
                    if mg == 0:
                        p.wait(t, "b5vv")
                    for ml in range(4):
                        m = mg * 4 + ml
                        for hh in range(NH):
                            key = f"wo{ml % 2}"
                            if key in bank6:
                                t.wait_ge(p.sem["b6vv"], bank6[key])
                            for k in range(16):
                                ins = t.matmul(
                                    ps6_[ml % 2][:, :],
                                    wo_b_[mg % 2][:, k * 512 + ml * 128:
                                                  k * 512 + ml * 128 + 128],
                                    o_sb[:, k * TPC + hh * 512:
                                         k * TPC + hh * 512 + 512],
                                    start=(k == 0), stop=(k == 15))
                            mmv = p.cnt["b6mm"] + 1
                            p.inc(ins, "b6mm")

                            @blk.vector
                            def _(v, m=m, ml=ml, hh=hh, mmv=mmv):
                                p.wait_at(v, "b6mm", mmv)
                                p.edge(v, v.tensor_scalar(
                                    t1_6[:, :], ps6_[ml % 2][:, :],
                                    wob_sb[:, m:m + 1], None, ALU.add),
                                    "b6vv")
                                p.inc(v.tensor_add(
                                    v_sb_l[:, m * TPC + hh * 512:
                                           m * TPC + hh * 512 + 512],
                                    t1_6[:, :],
                                    v_sb_l[:, m * TPC + hh * 512:
                                           m * TPC + hh * 512 + 512]),
                                    "b6vv")
                            bank6[f"wo{ml % 2}"] = p.cnt["b6vv"]

            @blk.sync
            def _(sy):
                p.wait(sy, "b6vv")
                p.wait(sy, "b6ss")
                p.wait(sy, "b6mm")
                p.wait(sy, "b6w")
                p.inc(sy.dma_start(out=win_mark0[:, :], in_=cpool_h),
                      "b6w", 16)

        xh3 = p.sbuf("b7xh", [128, 16 * TPC], BF16)
        gate_sb = p.sbuf("b7gate", [128, 16 * 8], BF16)

        @blk.gpsimd
        def _(g):
            p.wait(g, "b6vv")
            p.wait(g, "b6w")
            p.inc(g.dma_start(
                out=gate_sb.ap().rearrange("pp (k m) -> pp k m", k=16),
                in_=gateT_h.rearrange("k pp m -> pp k m")), "ldw", 16)

        ln3, _c3 = emit_layernorm(p, blk, "b7ln", v_sb_l, xh3, TPC,
                                  ["b6vv", "b6w", "ldw"])
        f2, _c4 = emit_ffn(p, blk, "f2", xh3, v_sb_l, w1p_h, w2t_h,
                           gate_sb, TPC, [ln3])

        @blk.sync
        def _(sy):
            p.wait(sy, f2)
            p.inc(sy.dma_start(
                out=outT.rearrange("k pp m -> pp k m"),
                in_=v_sb_l.ap().rearrange("pp (k m) -> pp k m", k=16)),
                "std", 16)
            p.wait(sy, "std")
    return nc


def prep_attn_weights(ins):
    f32 = np.float32
    g = lambda k: np.asarray(ins[k], f32)
    w = {}
    w["wqaT"] = np.ascontiguousarray(g("L0_attn_wq_a_w").T.reshape(16, 128, QLR))
    w["wqab"] = _bias_tiles(g("L0_attn_wq_a_b"), 12)
    w["qnw"] = _bias_tiles(g("L0_attn_q_norm_w"), 12)
    wqb = g("L0_attn_wq_b_w")
    wqbT = np.stack([
        np.ascontiguousarray(wqb[192 * h:192 * h + 192, :].T.reshape(
            12, 128, 192)) for h in range(H)])
    w["wqbT"] = wqbT
    bb = g("L0_attn_wq_b_b")
    t = np.zeros((H, 128, 2), f32)
    for h in range(H):
        t[h, :, 0] = bb[192 * h:192 * h + 128]
        t[h, 0:64, 1] = bb[192 * h + 128:192 * h + 192]
    w["wqbb"] = t
    w["wkvaT"] = np.ascontiguousarray(
        g("L0_attn_wkv_a_w").T.reshape(16, 128, 576))
    w["wkvab"] = _bias_tiles(g("L0_attn_wkv_a_b"), 5)
    w["kvnw"] = _bias_tiles(g("L0_attn_kv_norm_w"), 4)
    wkvb = g("L0_attn_wkv_b_w").reshape(H, 256, KVLR)
    w["wknT"] = np.ascontiguousarray(
        wkvb[:, 0:128, :].transpose(0, 2, 1).reshape(H, 4, 128, 128))
    w["wkvvT"] = np.ascontiguousarray(
        wkvb[:, 128:256, :].transpose(0, 2, 1).reshape(H, 4, 128, 128))
    w["woT"] = np.ascontiguousarray(g("L0_attn_wo_w").T.reshape(16, 128, D))
    w["wob"] = _bias_tiles(g("L0_attn_wo_b"), 16)
    return w


def _rope_tables_T():
    inv = 1.0 / 10000.0 ** (np.arange(0, ROPE, 2, dtype=np.float32) / ROPE)
    t = np.arange(S, dtype=np.float32)[:, None]
    fr = t * inv[None, :]
    fr = np.concatenate([fr, fr], axis=-1)
    return (np.ascontiguousarray(np.cos(fr).T),
            np.ascontiguousarray(np.sin(fr).T))


def _masks():
    q = np.arange(128)[:, None]
    k = np.arange(128)[None, :]
    dmask = np.where(k <= q, 0.0, -1e30).astype(np.float32)
    kk = np.arange(128)[:, None]
    qq = np.arange(512)[None, :]
    km = np.stack([(kk <= qq - 128 * ri).astype(np.float32)
                   for ri in range(4)])
    return dmask, np.ascontiguousarray(km)


def run_launch_B(inputs, u):
    aw = prep_attn_weights(inputs)
    w1p, w2t, gateT = prep_ffn_weights(inputs)
    cpool = _cpool_np()
    ident = np.eye(128, dtype=np.float32)
    cosT, sinT = _rope_tables_T()
    dmask, kmask = _masks()
    uf = np.ascontiguousarray(u.reshape(16, 128, S))
    nc = build_B()
    ims = []
    for c in range(NCORE):
        pad = 7168 - 1024 * c
        meta = np.array([[pad, 4 * pad, 1024 * c, pad // 512,
                          pad // 128, 2 * c, 0, 0]], np.int32)
        ims.append({
            "uf": uf,
            "uo": np.ascontiguousarray(
                u[:, c * TPC:(c + 1) * TPC].reshape(16, 128, TPC)),
            "cpool": cpool, "ident": ident,
            "cosg": cosT, "sing": sinT,
            "coso": np.ascontiguousarray(cosT[:, c * TPC:(c + 1) * TPC]),
            "sino": np.ascontiguousarray(sinT[:, c * TPC:(c + 1) * TPC]),
            "dmask": dmask, "kmask": kmask, "meta": meta,
            "wqaT": aw["wqaT"], "wqab": aw["wqab"], "qnw": aw["qnw"],
            "wqbT": aw["wqbT"], "wqbb": aw["wqbb"],
            "wkvaT": aw["wkvaT"], "wkvab": aw["wkvab"], "kvnw": aw["kvnw"],
            "wknT": aw["wknT"], "wkvvT": aw["wkvvT"],
            "woT": aw["woT"], "wob": aw["wob"],
            "w1p": w1p, "w2t": w2t, "gateT": gateT,
        })
    res = run_bass_kernel_spmd(nc, ims, core_ids=list(range(NCORE)))
    out = np.empty((D, S), np.float32)
    for c in range(NCORE):
        out[:, c * TPC:(c + 1) * TPC] = \
            np.asarray(res.results[c]["outT"]).reshape(D, TPC)
    return out


def kernel(**inputs):
    u = run_launch_A(inputs)
    out = run_launch_B(inputs, u)
    return np.ascontiguousarray(out.T).reshape(1, S, D).astype(np.float32)

